# version 02
# baseline (speedup 1.0000x reference)
import numpy as np
import ml_dtypes
from contextlib import ExitStack
import concourse.bass as bass
import concourse.mybir as mybir
from concourse.bass_utils import run_bass_kernel_spmd

F32 = mybir.dt.float32
BF16 = mybir.dt.bfloat16
U32 = mybir.dt.uint32
I32 = mybir.dt.int32
AF = mybir.ActivationFunctionType
ALU = mybir.AluOpType
AX = mybir.AxisListType
bfnp = ml_dtypes.bfloat16

T = 4096
D = 1024
DI = 2048
GI = 16
GD = 8
NS = 32
RM = 256
NCHM = T // RM
NQ = 4
QN = 8
EPS = 1e-5


class Sem:
    def __init__(self, es, nc, name):
        self.h = es.enter_context(nc.semaphore(name))
        self.n = 0

    def inc(self, instr, k=1):
        instr.then_inc(self.h, k)
        self.n += k
        return instr


class K:
    def __init__(self, nc, es):
        self.nc = nc
        self.es = es
        self.scopes = []
        self.t = {}

    def push_scope(self):
        self.scopes.append(ExitStack())

    def pop_scope(self):
        self.scopes.pop().close()

    def sb(self, name, shape, dt=F32):
        es = self.scopes[-1] if self.scopes else self.es
        h = es.enter_context(self.nc.sbuf_tensor(name, list(shape), dt))
        self.t[name] = h
        return h

    def ps(self, name, shape, dt=F32):
        h = self.es.enter_context(self.nc.psum_tensor(name, list(shape), dt))
        self.t[name] = h
        return h

    def dram(self, name, shape, dt=F32, kind="Internal"):
        h = self.nc.dram_tensor(name, list(shape), dt, kind=kind)
        self.t[name] = h
        return h

    def sem(self, name):
        s = Sem(self.es, self.nc, name)
        self.t[name] = s
        return s


def ln_chunk(k, blkmk, x_tm, out_bf, ntt, width, s_v, s_s, tmp, gain=None, bias=None):
    nc = k.nc
    inv = 1.0 / width
    with nc.Block() as blk:
        @blk.vector
        def _(vector):
            vector.tensor_mul(tmp["sq"][:, 0:ntt * width].rearrange("p (a b) -> p a b", a=ntt),
                              x_tm, x_tm)
            vector.reduce_sum(tmp["s1"][:, 0:ntt], x_tm, axis=AX.X)
            vector.drain()
            vector.reduce_sum(tmp["s2"][:, 0:ntt],
                              tmp["sq"][:, 0:ntt * width].rearrange("p (a b) -> p a b", a=ntt),
                              axis=AX.X)
            vector.tensor_scalar(out=tmp["m"][:, 0:ntt], in0=tmp["s1"][:, 0:ntt],
                                 scalar1=inv, scalar2=None, op0=ALU.mult)
            vector.drain()
            vector.tensor_mul(tmp["var"][:, 0:ntt], tmp["m"][:, 0:ntt], tmp["m"][:, 0:ntt])
            vector.drain()
            vector.scalar_tensor_tensor(out=tmp["var"][:, 0:ntt], in0=tmp["s2"][:, 0:ntt],
                                        scalar=inv, in1=tmp["var"][:, 0:ntt],
                                        op0=ALU.mult, op1=ALU.subtract)
            s_v.inc(vector.drain(), 1)

        @blk.scalar
        def _(scalar):
            scalar.wait_ge(s_v.h, s_v.n)
            scalar.activation(tmp["rstd"][:, 0:ntt], tmp["var"][:, 0:ntt], AF.Ln, bias=k.t["epsT"][:, 0:1])
            scalar.drain()
            scalar.activation(tmp["rstd"][:, 0:ntt], tmp["rstd"][:, 0:ntt], AF.Exp, scale=-0.5)
            s_s.inc(scalar.drain(), 1)

    with nc.Block() as blk2:
        @blk2.vector
        def _(vector):
            vector.wait_ge(s_s.h, s_s.n)
            for tt in range(ntt):
                dst = out_bf[:, tt, :] if gain is None else tmp["xn"][:, tt, :]
                vector.tensor_scalar(out=dst, in0=x_tm[:, tt, :],
                                     scalar1=tmp["m"][:, tt:tt + 1],
                                     scalar2=tmp["rstd"][:, tt:tt + 1],
                                     op0=ALU.subtract, op1=ALU.mult)
            if gain is not None:
                vector.drain()
                for tt in range(ntt):
                    vector.tensor_mul(tmp["xn"][:, tt, :], tmp["xn"][:, tt, :],
                                      gain)
                vector.drain()
                for tt in range(ntt):
                    vector.tensor_add(out_bf[:, tt, :], tmp["xn"][:, tt, :], bias)
            s_v.inc(vector.drain(), 1)


def transpose_tm_to_fm(k, in_bf, out_bf, ntt, nd, psT, ident, s_pe, s_s):
    nc = k.nc
    with nc.Block() as blk:
        @blk.tensor
        def _(tensor):
            tensor.wait_ge(s_s.h, s_s.n)
            for tt in range(ntt):
                for dj in range(nd):
                    ins = tensor.transpose(psT[:, dj * 128:(dj + 1) * 128],
                                           in_bf[:, tt, dj * 128:(dj + 1) * 128],
                                           ident[:])
                s_pe.inc(ins, 1)

        @blk.scalar
        def _(scalar):
            base = s_pe.n - ntt
            for tt in range(ntt):
                scalar.wait_ge(s_pe.h, base + tt + 1)
                scalar.copy(out_bf[:, :, tt * 128:(tt + 1) * 128],
                            psT[:, 0:nd * 128].rearrange("p (a b) -> p a b", a=nd))
                s_s.inc(scalar.drain(), 1)


def build_ln_prepass(k, x_in, xlnT_dram, ident, gain=None, bias=None, nch=NCHM):
    nc = k.nc
    s_ld = k.t["s_ld"]; s_v = k.t["s_v"]; s_s = k.t["s_s"]; s_pe = k.t["s_pe"]; s_st = k.t["s_st"]
    xc = k.sb("lp_xc", [128, 2, D], F32)
    xnbf = k.sb("lp_xnbf", [128, 2, D], BF16)
    xT = k.sb("lp_xT", [128, GD, 256], BF16)
    tmp = {"m": k.sb("lp_m", [128, 2]), "s1": k.sb("lp_s1", [128, 2]),
           "s2": k.sb("lp_s2", [128, 2]), "var": k.sb("lp_var", [128, 2]),
           "rstd": k.sb("lp_rstd", [128, 2]), "sq": k.sb("lp_sq", [128, 2 * D]),
           "xn": None}
    psT = k.t["psT"]
    for c in range(nch):
        with nc.Block() as blk:
            @blk.sync
            def _(sync, c=c):
                sync.wait_ge(s_st.h, s_st.n)
                s_ld.inc(sync.dma_start(
                    out=xc[:], in_=x_in[c * RM:(c + 1) * RM, :]
                    .rearrange("(tt p) d -> p tt d", p=128)), 16)

            @blk.vector
            def _(vector, c=c):
                vector.wait_ge(s_ld.h, s_ld.n)
        ln_chunk(k, None, xc[:], xnbf[:], 2, D, s_v, s_s, tmp)
        with nc.Block() as blk2:
            @blk2.tensor
            def _(tensor, c=c):
                tensor.wait_ge(s_v.h, s_v.n)
                tensor.wait_ge(s_s.h, s_s.n)
                for tt in range(2):
                    for dj in range(GD):
                        ins = tensor.transpose(
                            psT[:, (tt % 2) * 1024 + dj * 128:(tt % 2) * 1024 + (dj + 1) * 128],
                            xnbf[:, tt, dj * 128:(dj + 1) * 128], ident[:])
                    s_pe.inc(ins, 1)

            @blk2.scalar
            def _(scalar, c=c):
                base = s_pe.n - 2
                for tt in range(2):
                    scalar.wait_ge(s_pe.h, base + tt + 1)
                    scalar.copy(xT[:, :, tt * 128:(tt + 1) * 128],
                                psT[:, (tt % 2) * 1024:(tt % 2) * 1024 + GD * 128]
                                .rearrange("p (a b) -> p a b", a=GD))
                    s_s.inc(scalar.drain(), 1)

            @blk2.gpsimd
            def _(gpsimd, c=c):
                gpsimd.wait_ge(s_s.h, s_s.n)
                s_st.inc(gpsimd.dma_start(
                    out=xlnT_dram.ap()[:, :, c * RM:(c + 1) * RM]
                    .rearrange("a p t -> p a t"), in_=xT[:]), 16)


def carve(buf, lo, n, shape, dt=F32):
    ap = buf[:, lo:lo + n]
    if dt == BF16:
        ap = ap.bitcast(BF16)
    if len(shape) == 1:
        return ap
    return ap.rearrange("p (a b) -> p a b", a=shape[0]) if len(shape) == 2 else ap.rearrange(
        "p (a b c) -> p a b c", a=shape[0], b=shape[1])


def build_mamba_dir(k, d, weights, xlnT_dram, x0_in, x1_dram, dbg=None, nch=NCHM):
    nc = k.nc
    fwd = (d == 'f')
    s_ld = k.t["s_ld"]; s_v = k.t["s_v"]; s_s = k.t["s_s"]; s_pe = k.t["s_pe"]
    s_st = k.t["s_st"]; s_g = k.t["s_g"]; s_gd = k.t["s_gd"]; s_bc = k.t["s_bc"]
    s_xln = k.t["s_xln"]; s_pn = [k.t["s_pn0"], k.t["s_pn1"]]
    s_rep = [k.t["s_rep0"], k.t["s_rep1"]]; s_wp = k.t["s_wp"]; s_x0 = k.t["s_x0"]
    psT = k.t["psT"]
    pb = [k.t[f"pb{i}"] for i in range(4)]
    ident = k.t["ident_sb"]

    xlnTc = k.t["mb_xlnTc"]; winp = [k.t["mb_winp0"], k.t["mb_winp1"]]
    upT = k.t["mb_upT"]; tail3 = k.t["mb_tail3"]; zT = k.t["mb_zT"]
    ucsT = k.t["mb_ucsT"]; dtT = k.t["mb_dtT"]
    xdblo = k.t["mb_xdblo"]
    dtpre = k.t["mb_dtpre"]
    wdtT = k.t["mb_wdtT"]
    sp_t1 = k.t["mb_spt1"]; sp_t2 = k.t["mb_spt2"]
    hcar = k.t["mb_hcar"]; ysT = k.t["mb_ysT"]; ypart = k.t["mb_ypart"]
    scr = k.t["mb_scr"]
    convw = k.t["mb_convw"]; convb = k.t["mb_convb"]; Dp = k.t["mb_Dp"]
    bdt = k.t["mb_bdt"]; A_sb = k.t["mb_A"]
    gR = k.t["mb_gR"]; bR = k.t["mb_bR"]; bfR = k.t["mb_bfR"]
    cc1 = k.t["mb_cc1"]
    yfbf = k.t["mb_yfbf"]
    lntmp = {"m": k.t["mb_lm"], "s1": k.t["mb_ls1"], "s2": k.t["mb_ls2"],
             "var": k.t["mb_lvar"], "rstd": k.t["mb_lrstd"],
             "sq": carve(k.t["mb_scr"], 12288, 2048, (2048,)),
             "xn": carve(k.t["mb_scr"], 14336, 2048, (2, D))}
    bc_scr = k.t["bc_scr"]

    dA = [carve(scr, 0, 2048, (QN, RM)), carve(scr, 10240, 2048, (QN, RM))]
    wB = carve(scr, 2048, 2048, (QN, RM))
    hh = carve(scr, 4096, 2048, (QN, RM))
    Brep = [carve(scr, 6144, 2048, (QN, RM)), carve(scr, 12288, 2048, (QN, RM))]
    Crep = [carve(scr, 8192, 2048, (QN, RM)), carve(scr, 14336, 2048, (QN, RM))]
    mo = carve(scr, 0, 2048, (2, D))
    x0c = carve(scr, 2048, 2048, (2, D))
    woutp = carve(scr, 4096, 4096, (GI, 512), BF16)
    wfusep = carve(scr, 8192, 2048, (GD, 512), BF16)
    molnbf = carve(scr, 10240, 1024, (2, D), BF16)
    folnT = carve(scr, 11264, 1024, (GD, RM), BF16)
    ucbf = carve(scr, 12288, 2048, (GI, RM), BF16)
    convtmp = carve(scr, 0, 4096, (GI, RM))
    gtmp = carve(scr, 4096, 4096, (GI, RM))

    WinT = weights[f"WinT_{d}"]
    WoutT = weights[f"WoutT_{d}"]; WfuseT = weights["WfuseT"]
    WxT = k.t["mb_wxT"]

    cord = list(range(nch)) if fwd else list(range(NCHM - 1, NCHM - 1 - nch, -1))
    pnl_done = []
    rep_done = [0, 0, 0, 0]
    wp_done = [0, 0]
    pad0 = 3 if fwd else 0

    for ci, c in enumerate(cord):
        t0 = c * RM
        with nc.Block() as blk:
            @blk.sync
            def _(sync, c=c, ci=ci):
                sync.wait_ge(s_s.h, s_s.n)
                sync.wait_ge(s_gd.h, s_gd.n)
                sync.wait_ge(s_st.h, s_st.n)
                s_xln.inc(sync.dma_start(
                    out=xlnTc[:], in_=xlnT_dram.ap()[:, :, t0:t0 + RM]
                    .rearrange("a p t -> p a t")), 16)
                pnl_done.clear()
                pe_base = s_pe.n
                for et in range(32):
                    if et >= 2:
                        sync.wait_ge(s_pe.h, pe_base + et - 1)
                    s_pn[et % 2].inc(sync.dma_start(
                        out=winp[et % 2][:], in_=WinT.ap()[:, :, et * 128:(et + 1) * 128]
                        .rearrange("a p t -> p a t")), 16)
                    pnl_done.append(s_pn[et % 2].n)

            @blk.tensor
            def _(tensor, c=c):
                s_base = s_s.n
                tensor.wait_ge(s_xln.h, s_xln.n)
                for et in range(32):
                    tensor.wait_ge(s_pn[et % 2].h, pnl_done[et])
                    if et >= 2:
                        tensor.wait_ge(s_s.h, s_base + et - 1)
                    for kd in range(GD):
                        ins = tensor.matmul(pb[et % 2][0:128, 0:RM],
                                            winp[et % 2][:, kd, :],
                                            xlnTc[:, kd, :],
                                            start=(kd == 0), stop=(kd == GD - 1))
                    s_pe.inc(ins, 1)

            @blk.scalar
            def _(scalar, c=c, ci=ci):
                pe_base = s_pe.n - 32
                scalar.copy(upT[:, :, (0 if fwd else RM):(3 if fwd else RM + 3)][:, 0:GI, :],
                            tail3[:])
                for et in range(32):
                    scalar.wait_ge(s_pe.h, pe_base + et + 1)
                    if et < GI:
                        scalar.copy(upT[:, et, pad0:pad0 + RM], pb[et % 2][0:128, 0:RM])
                    else:
                        scalar.copy(zT[:, et - GI, :], pb[et % 2][0:128, 0:RM])
                    s_s.inc(scalar.drain(), 1)
                if fwd:
                    scalar.copy(tail3[:], upT[:, :, RM:RM + 3])
                else:
                    scalar.copy(tail3[:], upT[:, :, 0:3])
                s_s.inc(scalar.drain(), 1)

        if dbg is not None and "dbg_u_" + d in dbg:
            with nc.Block() as blkd:
                @blkd.gpsimd
                def _(gpsimd, c=c, t0=t0):
                    gpsimd.wait_ge(s_s.h, s_s.n)
                    s_gd.inc(gpsimd.dma_start(
                        out=dbg[f"dbg_u_{d}"].ap()[:, :, t0:t0 + RM].rearrange("a p t -> p a t"),
                        in_=upT[:, :, pad0:pad0 + RM]), 16)
                    s_gd.inc(gpsimd.dma_start(
                        out=dbg[f"dbg_z_{d}"].ap()[:, :, t0:t0 + RM].rearrange("a p t -> p a t"),
                        in_=zT[:]), 16)

        with nc.Block() as blk:
            @blk.gpsimd
            def _(gpsimd, c=c):
                gpsimd.wait_ge(s_s.h, s_s.n)
                ks = [2, 3]
                sel = (lambda kk: kk) if fwd else (lambda kk: 3 - kk)
                gpsimd.tensor_tensor(
                    ucsT[:], upT[:, :, 2:2 + RM],
                    convw[:, :, sel(2):sel(2) + 1].broadcast_to([128, GI, RM]), ALU.mult)
                gpsimd.drain()
                gpsimd.tensor_tensor(
                    gtmp, upT[:, :, 3:3 + RM],
                    convw[:, :, sel(3):sel(3) + 1].broadcast_to([128, GI, RM]), ALU.mult)
                gpsimd.drain()
                gpsimd.tensor_add(ucsT[:], ucsT[:], gtmp)
                s_g.inc(gpsimd.drain(), 1)

            @blk.vector
            def _(vector, c=c):
                vector.wait_ge(s_s.h, s_s.n)
                sel = (lambda kk: kk) if fwd else (lambda kk: 3 - kk)
                vector.tensor_tensor(
                    cc1[:], upT[:, :, 0:RM],
                    convw[:, :, sel(0):sel(0) + 1].broadcast_to([128, GI, RM]), ALU.mult)
                vector.drain()
                vector.tensor_tensor(
                    convtmp, upT[:, :, 1:1 + RM],
                    convw[:, :, sel(1):sel(1) + 1].broadcast_to([128, GI, RM]), ALU.mult)
                vector.drain()
                vector.tensor_add(cc1[:], cc1[:], convtmp)
                vector.wait_ge(s_g.h, s_g.n)
                vector.drain()
                vector.tensor_add(cc1[:], cc1[:], ucsT[:])
                vector.drain()
                vector.tensor_tensor(cc1[:], cc1[:],
                                     convb[:].unsqueeze(2).broadcast_to([128, GI, RM]),
                                     ALU.add)
                s_v.inc(vector.drain(), 1)

            @blk.scalar
            def _(scalar, c=c):
                scalar.wait_ge(s_v.h, s_v.n)
                scalar.activation(ucsT[:], cc1[:], AF.Silu)
                s_s.inc(scalar.drain(), 1)

        with nc.Block() as blk:
            @blk.vector
            def _(vector, c=c):
                vector.wait_ge(s_s.h, s_s.n)
                vector.tensor_copy(ucbf[:], ucsT[:])
                s_v.inc(vector.drain(), 1)

            @blk.tensor
            def _(tensor, c=c):
                tensor.wait_ge(s_v.h, s_v.n)
                for kg in range(GI):
                    ins = tensor.matmul(pb[2][0:128, 0:RM], WxT[:, kg, :], ucbf[:, kg, :],
                                        start=(kg == 0), stop=(kg == GI - 1))
                s_pe.inc(ins, 1)

            @blk.scalar
            def _(scalar, c=c):
                scalar.wait_ge(s_pe.h, s_pe.n)
                scalar.copy(dtpre[0:64, :], pb[2][0:64, 0:RM])
                scalar.copy(xdblo[64:128, :], pb[2][64:128, 0:RM])
                s_s.inc(scalar.drain(), 1)

            @blk.gpsimd
            def _(gpsimd, c=c):
                gpsimd.wait_ge(s_s.h, s_s.n)
                s_gd.inc(gpsimd.dma_start(out=bc_scr.ap()[:, :], in_=xdblo[64:128, :]), 16)

        with nc.Block() as blk:
            @blk.tensor
            def _(tensor, c=c):
                s_base = s_s.n
                for g in range(GI):
                    if g >= 2:
                        tensor.wait_ge(s_s.h, s_base + g - 1)
                    ins = tensor.matmul(pb[2 + g % 2][0:128, 0:RM],
                                        wdtT[0:64, g * 128:(g + 1) * 128],
                                        dtpre[0:64, :], start=True, stop=True)
                    s_pe.inc(ins, 1)

            @blk.scalar
            def _(scalar, c=c):
                pe_base = s_pe.n - GI
                v_base3 = s_v.n
                for g in range(GI):
                    scalar.wait_ge(s_pe.h, pe_base + g + 1)
                    if g >= 1:
                        scalar.wait_ge(s_v.h, v_base3 + g)
                    ps = pb[2 + g % 2][0:128, 0:RM]
                    scalar.activation(sp_t1[:], ps, AF.Abs, bias=bdt[:, g:g + 1])
                    scalar.drain()
                    scalar.activation(sp_t1[:], sp_t1[:], AF.Exp, scale=-1.0)
                    scalar.drain()
                    scalar.activation(sp_t1[:], sp_t1[:], AF.Ln, bias=1.0)
                    scalar.drain()
                    scalar.activation(sp_t2[:], ps, AF.Relu, bias=bdt[:, g:g + 1])
                    s_s.inc(scalar.drain(), 1)

            @blk.vector
            def _(vector, c=c):
                s_base = s_s.n - GI
                for g in range(GI):
                    vector.wait_ge(s_s.h, s_base + g + 1)
                    vector.tensor_add(dtT[:, g, :], sp_t1[:], sp_t2[:])
                    if g < GI - 1:
                        s_v.inc(vector.drain(), 1)
                vector.wait_ge(s_gd.h, s_gd.n)
                vector.drain()
                vector.tensor_mul(upT[:, :, 0:RM], dtT[:], ucsT[:])
                s_v.inc(vector.drain(), 1)

        wb_wT = upT[:, :, 0:RM]

        g_base = s_g.n
        s_base = s_s.n
        v_base = s_v.n
        ld_base = s_ld.n
        gd_base = s_gd.n
        with nc.Block() as blk:
            @blk.sync
            def _(sync, c=c):
                sync.wait_ge(s_gd.h, gd_base)
                for q in range(NQ):
                    if q >= 2:
                        sync.wait_ge(s_v.h, v_base + (q - 1) * GI)
                    s_rep[q % 2].inc(sync.dma_start(
                        out=Brep[q % 2][:],
                        in_=bc_scr.ap()[q * QN:(q + 1) * QN, :]
                        .unsqueeze(0).broadcast_to([128, QN, RM])), 16)
                    s_rep[q % 2].inc(sync.dma_start(
                        out=Crep[q % 2][:],
                        in_=bc_scr.ap()[NS + q * QN:NS + (q + 1) * QN, :]
                        .unsqueeze(0).broadcast_to([128, QN, RM])), 16)
                    rep_done[q] = s_rep[q % 2].n

            @blk.gpsimd
            def _(gpsimd, c=c):
                for q in range(NQ):
                    for g in range(GI):
                        u = q * GI + g
                        if u >= 2:
                            gpsimd.wait_ge(s_v.h, v_base + u - 1)
                        gpsimd.tensor_tensor(
                            dA[u % 2][:],
                            A_sb[:, g, q * QN:(q + 1) * QN].unsqueeze(2).broadcast_to([128, QN, RM]),
                            dtT[:, g, :].unsqueeze(1).broadcast_to([128, QN, RM]),
                            ALU.mult)
                        s_g.inc(gpsimd.drain(), 1)

            @blk.scalar
            def _(scalar, c=c):
                for q in range(NQ):
                    for g in range(GI):
                        u = q * GI + g
                        scalar.wait_ge(s_g.h, g_base + u + 1)
                        scalar.activation(dA[u % 2][:], dA[u % 2][:], AF.Exp)
                        s_s.inc(scalar.drain(), 1)

            @blk.vector
            def _(vector, c=c):
                for q in range(NQ):
                    vector.wait_ge(s_rep[q % 2].h, rep_done[q])
                    for g in range(GI):
                        u = q * GI + g
                        vector.wait_ge(s_s.h, s_base + u + 1)
                        vector.tensor_tensor(
                            wB[:], wb_wT[:, g, :].unsqueeze(1).broadcast_to([128, QN, RM]),
                            Brep[q % 2][:], ALU.mult)
                        vector.drain()
                        for j in range(QN):
                            n_ = q * QN + j
                            if fwd:
                                vector.tensor_tensor_scan(
                                    out=hh[:, j, :], data0=dA[u % 2][:, j, :],
                                    data1=wB[:, j, :], initial=hcar[:, g, n_:n_ + 1],
                                    op0=ALU.mult, op1=ALU.add)
                            else:
                                vector.tensor_tensor_scan(
                                    out=hh[:, j, ::-1], data0=dA[u % 2][:, j, ::-1],
                                    data1=wB[:, j, ::-1], initial=hcar[:, g, n_:n_ + 1],
                                    op0=ALU.mult, op1=ALU.add)
                        vector.drain()
                        edge = RM - 1 if fwd else 0
                        vector.tensor_copy(hcar[:, g, q * QN:(q + 1) * QN],
                                           hh[:, :, edge:edge + 1].rearrange("p a b -> p (a b)"))
                        vector.tensor_tensor(wB[:], hh[:], Crep[q % 2][:], ALU.mult)
                        vector.drain()
                        if q == 0:
                            vector.reduce_sum(ysT[:, g, :], wB[:].transpose([0, 2, 1]), axis=AX.X)
                        else:
                            vector.reduce_sum(ypart[:], wB[:].transpose([0, 2, 1]), axis=AX.X)
                            vector.drain()
                            vector.tensor_add(ysT[:, g, :], ysT[:, g, :], ypart[:])
                        s_v.inc(vector.drain(), 1)

        with nc.Block() as blk:
            @blk.scalar
            def _(scalar, c=c):
                scalar.wait_ge(s_v.h, s_v.n)
                scalar.activation(zT[:], zT[:], AF.Silu)
                s_s.inc(scalar.drain(), 1)

            @blk.vector
            def _(vector, c=c):
                vector.wait_ge(s_s.h, s_s.n)
                vector.tensor_tensor(cc1[:], ucsT[:],
                                     Dp[:].unsqueeze(2).broadcast_to([128, GI, RM]), ALU.mult)
                vector.drain()
                vector.tensor_add(cc1[:], cc1[:], ysT[:])
                vector.drain()
                vector.tensor_tensor(yfbf[:], cc1[:], zT[:], ALU.mult)
                s_v.inc(vector.drain(), 1)

        with nc.Block() as blk:
            @blk.sync
            def _(sync, c=c):
                sync.wait_ge(s_v.h, s_v.n)
                s_x0.inc(sync.dma_start(out=x0c, in_=x0_in[t0:t0 + RM, :]
                                        .rearrange("(tt p) d -> p tt d", p=128)), 16)

            @blk.tensor
            def _(tensor, c=c):
                pass

        with nc.Block() as blk:
            @blk.sync
            def _(sync, c=c):
                for ech in range(2):
                    if ech == 1:
                        sync.wait_ge(s_pe.h, s_pe.n + 2)
                    s_wp.inc(sync.dma_start(
                        out=woutp, in_=WoutT.ap()[:, :, ech * 512:(ech + 1) * 512]
                        .rearrange("a p t -> p a t")), 16)
                    wp_done[ech] = s_wp.n

            @blk.tensor
            def _(tensor, c=c):
                for ech in range(2):
                    tensor.wait_ge(s_wp.h, wp_done[ech])
                    for tsub in range(2):
                        for kg in range(GI):
                            ins = tensor.matmul(pb[ech * 2 + tsub][0:128, 0:512],
                                                yfbf[:, kg, tsub * 128:(tsub + 1) * 128],
                                                woutp[:, kg, :],
                                                start=(kg == 0), stop=(kg == GI - 1))
                        s_pe.inc(ins, 1)

            @blk.scalar
            def _(scalar, c=c):
                pe_base = s_pe.n - 4
                i = 0
                for ech in range(2):
                    for tsub in range(2):
                        i += 1
                        scalar.wait_ge(s_pe.h, pe_base + i)
                        scalar.copy(mo[:, tsub, ech * 512:(ech + 1) * 512],
                                    pb[ech * 2 + tsub][0:128, 0:512])
                        s_s.inc(scalar.drain(), 1)

        ln_chunk(k, None, mo, molnbf, 2, D, s_v, s_s, lntmp, gain=gR[:], bias=bR[:])
        with nc.Block() as blk:
            @blk.tensor
            def _(tensor, c=c):
                tensor.wait_ge(s_v.h, s_v.n)
                for tt in range(2):
                    for dj in range(GD):
                        ins = tensor.transpose(
                            psT[:, (tt % 2) * 1024 + dj * 128:(tt % 2) * 1024 + (dj + 1) * 128],
                            molnbf[:, tt, dj * 128:(dj + 1) * 128], ident[:])
                    s_pe.inc(ins, 1)

            @blk.scalar
            def _(scalar, c=c):
                base = s_pe.n - 2
                for tt in range(2):
                    scalar.wait_ge(s_pe.h, base + tt + 1)
                    scalar.copy(folnT[:, :, tt * 128:(tt + 1) * 128],
                                psT[:, (tt % 2) * 1024:(tt % 2) * 1024 + GD * 128]
                                .rearrange("p (a b) -> p a b", a=GD))
                    s_s.inc(scalar.drain(), 1)

        with nc.Block() as blk:
            @blk.sync
            def _(sync, c=c):
                for dch in range(2):
                    if dch == 1:
                        sync.wait_ge(s_pe.h, s_pe.n + 2)
                    s_wp.inc(sync.dma_start(
                        out=wfusep, in_=WfuseT.ap()[(0 if fwd else GD):(GD if fwd else 2 * GD),
                                                    :, dch * 512:(dch + 1) * 512]
                        .rearrange("a p t -> p a t")), 16)
                    wp_done[dch] = s_wp.n

            @blk.tensor
            def _(tensor, c=c):
                for dch in range(2):
                    tensor.wait_ge(s_wp.h, wp_done[dch])
                    for tsub in range(2):
                        for ke in range(GD):
                            ins = tensor.matmul(pb[dch * 2 + tsub][0:128, 0:512],
                                                folnT[:, ke, tsub * 128:(tsub + 1) * 128],
                                                wfusep[:, ke, :],
                                                start=(ke == 0), stop=(ke == GD - 1))
                        s_pe.inc(ins, 1)

            @blk.vector
            def _(vector, c=c):
                pe_base = s_pe.n - 4
                vector.wait_ge(s_x0.h, s_x0.n)
                i = 0
                for dch in range(2):
                    for tsub in range(2):
                        i += 1
                        vector.wait_ge(s_pe.h, pe_base + i)
                        vector.tensor_add(x0c[:, tsub, dch * 512:(dch + 1) * 512],
                                          x0c[:, tsub, dch * 512:(dch + 1) * 512],
                                          pb[dch * 2 + tsub][0:128, 0:512])
                if not fwd:
                    vector.drain()
                    for tsub in range(2):
                        vector.tensor_add(x0c[:, tsub, :], x0c[:, tsub, :], bfR[:])
                s_v.inc(vector.drain(), 1)

            @blk.gpsimd
            def _(gpsimd, c=c):
                gpsimd.wait_ge(s_v.h, s_v.n)
                s_gd.inc(gpsimd.dma_start(
                    out=x1_dram.ap()[t0:t0 + RM, :].rearrange("(tt p) d -> p tt d", p=128),
                    in_=x0c), 16)
                if dbg is not None:
                    s_gd.inc(gpsimd.dma_start(
                        out=dbg[f"dbg_uc_{d}"].ap()[:, :, t0:t0 + RM].rearrange("a p t -> p a t"),
                        in_=ucsT[:]), 16)
                    s_gd.inc(gpsimd.dma_start(
                        out=dbg[f"dbg_dt_{d}"].ap()[:, :, t0:t0 + RM].rearrange("a p t -> p a t"),
                        in_=dtT[:]), 16)
                    s_gd.inc(gpsimd.dma_start(
                        out=dbg[f"dbg_ys_{d}"].ap()[:, :, t0:t0 + RM].rearrange("a p t -> p a t"),
                        in_=ysT[:]), 16)

    with nc.Block() as blk:
        @blk.sync
        def _(sync):
            sync.wait_ge(s_gd.h, s_gd.n)


def build_weight_prep(k, casts, trans, ident):
    nc = k.nc
    s_ld = k.t["s_ld"]; s_v = k.t["s_v"]; s_s = k.t["s_s"]; s_pe = k.t["s_pe"]; s_st = k.t["s_st"]
    psT = k.t["psT"]
    tf = k.t["wp_f32"]; tb = k.t["wp_bf"]; to = k.t["wp_out"]
    ident = ident

    for (srch, dsth, rows, cols) in casts:
        for i in range(rows // 128):
            with nc.Block() as blk:
                @blk.sync
                def _(sync, i=i, srch=srch, cols=cols):
                    sync.wait_ge(s_v.h, s_v.n)
                    s_ld.inc(sync.dma_start(out=tf[:, 0:cols],
                                            in_=srch.ap()[i * 128:(i + 1) * 128, :]), 16)

                @blk.vector
                def _(vector, cols=cols):
                    vector.wait_ge(s_ld.h, s_ld.n)
                    vector.wait_ge(s_st.h, s_st.n)
                    vector.tensor_copy(tb[:, 0:cols], tf[:, 0:cols])
                    s_v.inc(vector.drain(), 1)

                @blk.gpsimd
                def _(gpsimd, i=i, dsth=dsth, cols=cols):
                    gpsimd.wait_ge(s_v.h, s_v.n)
                    s_st.inc(gpsimd.dma_start(out=dsth.ap()[i, :, 0:cols],
                                              in_=tb[:, 0:cols]), 16)

    for (srch, dsth, rows, cols) in trans:
        ncol = cols // 128
        for i in range(rows // 128):
            with nc.Block() as blk:
                @blk.sync
                def _(sync, i=i, srch=srch, cols=cols):
                    sync.wait_ge(s_v.h, s_v.n)
                    s_ld.inc(sync.dma_start(out=tf[:, 0:cols],
                                            in_=srch.ap()[i * 128:(i + 1) * 128, :]), 16)

                @blk.vector
                def _(vector, cols=cols):
                    vector.wait_ge(s_ld.h, s_ld.n)
                    vector.wait_ge(s_pe.h, s_pe.n)
                    vector.wait_ge(s_st.h, s_st.n)
                    vector.tensor_copy(tb[:, 0:cols], tf[:, 0:cols])
                    s_v.inc(vector.drain(), 1)

                @blk.tensor
                def _(tensor, ncol=ncol):
                    tensor.wait_ge(s_v.h, s_v.n)
                    s_base_t = s_s.n
                    for j in range(ncol):
                        if j >= 2:
                            tensor.wait_ge(s_s.h, s_base_t + j - 1)
                        ins = tensor.transpose(
                            psT[:, (j % 2) * 1024:(j % 2) * 1024 + 128],
                            tb[:, j * 128:(j + 1) * 128], ident[:])
                        s_pe.inc(ins, 1)

                @blk.scalar
                def _(scalar, ncol=ncol):
                    pe_base = s_pe.n - ncol
                    scalar.wait_ge(s_st.h, s_st.n)
                    for j in range(ncol):
                        scalar.wait_ge(s_pe.h, pe_base + j + 1)
                        scalar.copy(to[:, j * 128:(j + 1) * 128],
                                    psT[:, (j % 2) * 1024:(j % 2) * 1024 + 128])
                        s_s.inc(scalar.drain(), 1)

                @blk.gpsimd
                def _(gpsimd, i=i, dsth=dsth, ncol=ncol):
                    gpsimd.wait_ge(s_s.h, s_s.n)
                    s_st.inc(gpsimd.dma_start(
                        out=dsth.ap()[:, :, i * 128:(i + 1) * 128]
                        .rearrange("a p t -> p a t"),
                        in_=to[:, 0:ncol * 128].rearrange("p (a b) -> p a b", a=ncol)), 16)


def build_prep_wdt(k, srch, dsth, ident):
    nc = k.nc
    s_ld = k.t["s_ld"]; s_v = k.t["s_v"]; s_s = k.t["s_s"]; s_pe = k.t["s_pe"]; s_st = k.t["s_st"]
    psT = k.t["psT"]
    tf = k.t["wp_f32"]; tb = k.t["wp_bf"]; to = k.t["wp_out"]
    for i in range(16):
        with nc.Block() as blk:
            @blk.sync
            def _(sync, i=i):
                sync.wait_ge(s_v.h, s_v.n)
                s_ld.inc(sync.dma_start(out=tf[:, 0:64],
                                        in_=srch.ap()[i * 128:(i + 1) * 128, :]), 16)

            @blk.vector
            def _(vector):
                vector.wait_ge(s_ld.h, s_ld.n)
                vector.wait_ge(s_pe.h, s_pe.n)
                vector.tensor_copy(tb[:, 0:64], tf[:, 0:64])
                s_v.inc(vector.drain(), 1)

            @blk.tensor
            def _(tensor):
                tensor.wait_ge(s_v.h, s_v.n)
                tensor.wait_ge(s_s.h, s_s.n)
                s_pe.inc(tensor.transpose(psT[0:64, 0:128], tb[:, 0:64], k.t["ident_sb"][:]), 1)

            @blk.scalar
            def _(scalar):
                scalar.wait_ge(s_pe.h, s_pe.n)
                scalar.wait_ge(s_st.h, s_st.n)
                scalar.copy(to[0:64, 0:128], psT[0:64, 0:128])
                s_s.inc(scalar.drain(), 1)

            @blk.gpsimd
            def _(gpsimd, i=i):
                gpsimd.wait_ge(s_s.h, s_s.n)
                s_st.inc(gpsimd.dma_start(out=dsth.ap()[0:64, i * 128:(i + 1) * 128],
                                          in_=to[0:64, 0:128]), 16)


def load_pg_vec(k, eng, srch, dst_sb, sem):
    for g in range(16):
        sem.inc(eng.dma_start(out=dst_sb[:, g:g + 1],
                              in_=srch.ap()[g * 128:(g + 1) * 128].unsqueeze(1)), 16)


def build_mamba_dir_setup(k, d, ins):
    nc = k.nc
    s_ld = k.t["s_ld"]; s_v = k.t["s_v"]; s_s = k.t["s_s"]
    with nc.Block() as blk:
        @blk.sync
        def _(sync):
            sync.wait_ge(s_v.h, s_v.n)
            sync.wait_ge(s_s.h, s_s.n)
            s_ld.inc(sync.dma_start(
                out=k.t["mb_A"][:],
                in_=ins[f"L0_attn_a_Alog_{d}"].ap().rearrange("(g p) n -> p g n", p=128)), 16)
            s_ld.inc(sync.dma_start(
                out=k.t["mb_convw"][:],
                in_=ins[f"L0_attn_a_convw_{d}"].ap().rearrange("(g p) c -> p g c", p=128)), 16)
            load_pg_vec(k, sync, ins[f"L0_attn_a_convb_{d}"], k.t["mb_convb"], s_ld)
            load_pg_vec(k, sync, ins[f"L0_attn_a_Dp_{d}"], k.t["mb_Dp"], s_ld)
            load_pg_vec(k, sync, ins[f"L0_attn_a_bdt_{d}"], k.t["mb_bdt"], s_ld)
            s_ld.inc(sync.dma_start(
                out=k.t["mb_gR"][:],
                in_=ins[f"L0_attn_a_ln_g_{d}"].ap().unsqueeze(0).broadcast_to([128, D])), 16)
            s_ld.inc(sync.dma_start(
                out=k.t["mb_bR"][:],
                in_=ins[f"L0_attn_a_ln_b_{d}"].ap().unsqueeze(0).broadcast_to([128, D])), 16)
            if d == 'b':
                s_ld.inc(sync.dma_start(
                    out=k.t["mb_bfR"][:],
                    in_=ins["L0_attn_a_bfuse"].ap().unsqueeze(0).broadcast_to([128, D])), 16)
            s_ld.inc(sync.dma_start(
                out=k.t["mb_wdtT"][0:64, :],
                in_=k.t[f"WdtT_{d}"].ap()[:, :]), 16)
            s_ld.inc(sync.dma_start(
                out=k.t["mb_wxT"][:],
                in_=k.t[f"WxT_{d}"].ap().rearrange("a p b -> p a b")), 16)

        @blk.vector
        def _(vector):
            vector.wait_ge(s_ld.h, s_ld.n)
            vector.memset(k.t["mb_tail3"][:], 0.0)
            vector.memset(k.t["mb_hcar"][:], 0.0)
            s_v.inc(vector.drain(), 1)

        @blk.scalar
        def _(scalar):
            scalar.wait_ge(s_ld.h, s_ld.n)
            scalar.activation(k.t["mb_A"][:], k.t["mb_A"][:], AF.Exp)
            scalar.drain()
            scalar.mul(k.t["mb_A"][:], k.t["mb_A"][:], -1.0)
            s_s.inc(scalar.drain(), 1)


def alloc_mamba(k):
    k.sb("mb_xlnTc", [128, GD, RM], BF16)
    k.sb("mb_winp0", [128, GD, 128], BF16)
    k.sb("mb_winp1", [128, GD, 128], BF16)
    k.sb("mb_upT", [128, GI, RM + 3], F32)
    k.sb("mb_tail3", [128, GI, 3], F32)
    k.sb("mb_zT", [128, GI, RM], F32)
    k.sb("mb_ucsT", [128, GI, RM], F32)
    k.sb("mb_dtT", [128, GI, RM], F32)
    k.sb("mb_xdblo", [128, RM], F32)
    k.sb("mb_dtpre", [128, RM], BF16)
    k.sb("mb_wdtT", [128, DI], BF16)
    k.sb("mb_wxT", [128, GI, 128], BF16)
    k.sb("mb_spt1", [128, RM], F32)
    k.sb("mb_spt2", [128, RM], F32)
    k.sb("mb_hcar", [128, GI, NS], F32)
    k.sb("mb_ysT", [128, GI, RM], F32)
    k.sb("mb_ypart", [128, RM], F32)
    k.sb("mb_scr", [128, 16384], F32)
    k.sb("mb_convw", [128, GI, 4], F32)
    k.sb("mb_convb", [128, GI], F32)
    k.sb("mb_Dp", [128, GI], F32)
    k.sb("mb_bdt", [128, GI], F32)
    k.sb("mb_A", [128, GI, NS], F32)
    k.sb("mb_gR", [128, D], F32)
    k.sb("mb_bR", [128, D], F32)
    k.sb("mb_bfR", [128, D], F32)
    k.sb("mb_cc1", [128, GI, RM], F32)
    k.sb("mb_yfbf", [128, GI, RM], BF16)
    k.sb("mb_lm", [128, 2]); k.sb("mb_ls1", [128, 2]); k.sb("mb_ls2", [128, 2])
    k.sb("mb_lvar", [128, 2]); k.sb("mb_lrstd", [128, 2])



MAMBA_W_NAMES = [
    "Win_f", "convw_f", "convb_f", "Wx_f", "Wdt_f", "bdt_f", "Alog_f", "Dp_f",
    "Wout_f", "ln_g_f", "ln_b_f",
    "Win_b", "convw_b", "convb_b", "Wx_b", "Wdt_b", "bdt_b", "Alog_b", "Dp_b",
    "Wout_b", "ln_g_b", "ln_b_b", "Wfuse", "bfuse",
]


def build_program_mamba(with_b=True, dbg=True, nch=NCHM):
    nc = bass.Bass("TRN2", target_bir_lowering=False)
    es = ExitStack()
    k = K(nc, es)
    ins = {}
    ins["x"] = k.dram("x", [T, D], F32, kind="ExternalInput")
    for nm in MAMBA_W_NAMES:
        full = "L0_attn_a_" + nm
        shp = {"Win": [2 * DI, D], "convw": [DI, 4], "convb": [DI], "Wx": [128, DI],
               "Wdt": [DI, 64], "bdt": [DI], "Alog": [DI, NS], "Dp": [DI],
               "Wout": [D, DI], "ln_g": [D], "ln_b": [D], "Wfuse": [D, 2 * DI // 2],
               "bfuse": [D]}[nm.rsplit("_", 1)[0] if nm not in ("Wfuse", "bfuse") else nm]
        ins[full] = k.dram(full, shp, F32, kind="ExternalInput")
    ident_in = k.dram("ident", [128, 128], BF16, kind="ExternalInput")

    k.dram("xlnT", [GD, 128, T], BF16)
    k.dram("x1", [T, D], F32, kind="ExternalOutput")
    k.dram("bc_scr", [64, RM], F32)
    for dd in "fb":
        k.dram(f"WinT_{dd}", [GD, 128, 2 * DI], BF16)
        k.dram(f"WoutT_{dd}", [GI, 128, D], BF16)
        k.dram(f"WxT_{dd}", [GI, 128, 128], BF16)
        k.dram(f"WdtT_{dd}", [64, DI], BF16)
    k.dram("WfuseT", [GI, 128, D], BF16)
    dbg_t = None
    if dbg:
        dbg_t = {}
        for dd in (("f", "b") if with_b else ("f",)):
            dbg_t[f"dbg_uc_{dd}"] = k.dram(f"dbg_uc_{dd}", [GI, 128, T], F32, kind="ExternalOutput")
            dbg_t[f"dbg_dt_{dd}"] = k.dram(f"dbg_dt_{dd}", [GI, 128, T], F32, kind="ExternalOutput")
            dbg_t[f"dbg_ys_{dd}"] = k.dram(f"dbg_ys_{dd}", [GI, 128, T], F32, kind="ExternalOutput")
            dbg_t[f"dbg_u_{dd}"] = k.dram(f"dbg_u_{dd}", [GI, 128, T], F32, kind="ExternalOutput")
            dbg_t[f"dbg_z_{dd}"] = k.dram(f"dbg_z_{dd}", [GI, 128, T], F32, kind="ExternalOutput")
        dbg_t["dbg_panel"] = k.dram("dbg_panel", [128, GD, 128], BF16, kind="ExternalOutput")
        dbg_t["dbg_xlnTc"] = k.dram("dbg_xlnTc", [128, GD, RM], BF16, kind="ExternalOutput")

    with es:
        for s in ["s_ld", "s_v", "s_s", "s_pe", "s_st", "s_g", "s_gd", "s_bc",
                  "s_xln", "s_pn0", "s_pn1", "s_rep0", "s_rep1", "s_wp", "s_x0",
                  "s_qk0", "s_qk1"]:
            k.sem(s)
        psT = k.ps("psT", [128, 2048], BF16)
        for i in range(4):
            k.ps(f"pb{i}", [128, 512], F32)
        ident_sb = k.sb("ident_sb", [128, 128], BF16)
        k.sb("identf", [128, 128], F32)
        k.sb("epsT", [128, 1], F32)
        k.push_scope()
        k.sb("wp_f32", [128, 4096], F32)
        k.sb("wp_bf", [128, 4096], BF16)
        k.sb("wp_out", [128, 4096], BF16)

        with nc.Block() as blk:
            @blk.sync
            def _(sync):
                k.t["s_ld"].inc(sync.dma_start(out=ident_sb[:], in_=ident_in.ap()[:, :]), 16)

            @blk.vector
            def _(vector):
                vector.wait_ge(k.t["s_ld"].h, k.t["s_ld"].n)
                vector.memset(k.t["epsT"][:], EPS)
                vector.tensor_copy(k.t["identf"][:], ident_sb[:])
                k.t["s_v"].inc(vector.drain(), 1)

        dirs = ("f", "b") if with_b else ("f",)
        trans = []
        for dd in dirs:
            trans += [(ins[f"L0_attn_a_Win_{dd}"], k.t[f"WinT_{dd}"], 2 * DI, D),
                      (ins[f"L0_attn_a_Wout_{dd}"], k.t[f"WoutT_{dd}"], D, DI),
                      (ins[f"L0_attn_a_Wx_{dd}"], k.t[f"WxT_{dd}"], 128, DI)]
        trans += [(ins["L0_attn_a_Wfuse"], k.t["WfuseT"], D, DI)]
        build_weight_prep(k, [], trans, ident_sb)
        for dd in dirs:
            build_prep_wdt(k, ins[f"L0_attn_a_Wdt_{dd}"], k.t[f"WdtT_{dd}"], ident_sb)

        build_ln_prepass(k, ins["x"].ap(), k.t["xlnT"], ident_sb, nch=nch)
        k.pop_scope()

        alloc_mamba(k)
        weights = {f"WinT_{dd}": k.t[f"WinT_{dd}"] for dd in dirs}
        weights.update({f"WoutT_{dd}": k.t[f"WoutT_{dd}"] for dd in dirs})
        weights["WfuseT"] = k.t["WfuseT"]
        for dd in dirs:
            build_mamba_dir_setup(k, dd, ins)
            build_mamba_dir(k, dd, weights, k.t["xlnT"],
                            ins["x"].ap() if dd == 'f' else k.t["x1"].ap(),
                            k.t["x1"], dbg=dbg_t, nch=nch)

        with nc.Block() as blk:
            @blk.sync
            def _(sync):
                sync.wait_ge(k.t["s_gd"].h, k.t["s_gd"].n)
                sync.wait_ge(k.t["s_st"].h, k.t["s_st"].n)

    return nc


def alloc_ffn(k, tag):
    t = {}
    def sb(name, shape, dt=F32):
        t[name] = k.sb(f"{tag}_{name}", shape, dt)
        return t[name]
    sb("hT", [128, GD, 512], BF16)
    sb("x1c", [128, 4, D], F32)
    sb("qT", [128, 4, 512], BF16)
    sb("keysT", [128, 8, 64], BF16)
    sb("fscr", [128, 8192], F32)
    sb("sxy", [128, 4, 2, 4, 8], F32)
    sb("ixyu", [128, 4, 2, 4, 8], U32)
    sb("ixyf", [128, 4, 2, 4, 8], F32)
    sb("sc8", [128, 4, 4, 8], F32)
    sb("pku", [128, 4, 4, 8], U32)
    sb("pkf", [128, 4, 4, 8], F32)
    sb("idx8f", [128, 4, 4, 8], F32)
    sb("idxi", [128, 4, 4, 8], I32)
    sb("sm", [128, 4, 4, 8], F32)
    sb("smbf", [128, 4, 4, 8], BF16)
    sb("mx8", [128, 16], F32)
    sb("nm8", [128, 16], F32)
    sb("sum8", [128, 16], F32)
    sb("rcp8", [128, 16], F32)
    sb("gate", [128, 32, 512], BF16)
    sb("siluWT", [128, 32, 512], BF16)
    sb("Ssum", [128, 32, 512], BF16)
    sb("Stmp", [128, 16, 512], BF16)
    sb("pnA0", [128, GD, 128], BF16)
    sb("pnA1", [128, GD, 128], BF16)
    sb("pnB0", [128, GD, 128], BF16)
    sb("pnB1", [128, GD, 128], BF16)
    sb("pnC0", [128, 512], BF16)
    sb("pnC1", [128, 512], BF16)
    sb("etmp", [128, 512], F32)
    sb("bgc", [128, 32], F32)
    sb("buc", [128, 32], F32)
    sb("bdR", [128, D], F32)
    sb("iota64", [128, 64], F32)
    sb("tlocf", [128, 4], F32)
    sb("zbf", [128, 8], BF16)
    sb("lm", [128, 4]); sb("ls1", [128, 4]); sb("ls2", [128, 4])
    sb("lvar", [128, 4]); sb("lrstd", [128, 4])
    sb("xnbf", [128, 4, D], BF16)
    return t


def build_ffn_setup(k, t, ins, iotain):
    nc = k.nc
    s_ld = k.t["s_ld"]; s_v = k.t["s_v"]; s_s = k.t["s_s"]
    with nc.Block() as blk:
        @blk.sync
        def _(sync):
            sync.wait_ge(s_v.h, s_v.n)
            sync.wait_ge(s_s.h, s_s.n)
            sync.wait_ge(k.t["s_st"].h, k.t["s_st"].n)
            s_ld.inc(sync.dma_start(out=t["keysT"][0:64, :, :],
                                    in_=k.t["keysT_dram"].ap().rearrange("p h n q -> n (p h) q")), 16)
            s_ld.inc(sync.dma_start(out=t["keysT"][64:128, :, :],
                                    in_=k.t["keysT_dram"].ap().rearrange("p h n q -> n (p h) q")), 16)
            for g in range(32):
                s_ld.inc(sync.dma_start(out=t["bgc"][:, g:g + 1],
                                        in_=ins["L0_ffn_bg"].ap()[g * 128:(g + 1) * 128].unsqueeze(1)), 16)
                s_ld.inc(sync.dma_start(out=t["buc"][:, g:g + 1],
                                        in_=ins["L0_ffn_bu"].ap()[g * 128:(g + 1) * 128].unsqueeze(1)), 16)
            s_ld.inc(sync.dma_start(out=t["bdR"][:],
                                    in_=ins["L0_ffn_bd"].ap().unsqueeze(0).broadcast_to([128, D])), 16)
            s_ld.inc(sync.dma_start(out=t["iota64"][:],
                                    in_=iotain.ap()[:, 0:64]), 16)
            s_ld.inc(sync.dma_start(out=t["tlocf"][:],
                                    in_=iotain.ap()[:, 64:68]), 16)

        @blk.vector
        def _(vector):
            vector.wait_ge(s_ld.h, s_ld.n)
            vector.memset(t["zbf"][:], 0.0)
            s_v.inc(vector.drain(), 1)


def build_ffn(k, t, ins, x_in, x_out, Sh_dram, first, dbg=None):
    nc = k.nc
    s_ld = k.t["s_ld"]; s_v = k.t["s_v"]; s_s = k.t["s_s"]; s_pe = k.t["s_pe"]
    s_st = k.t["s_st"]; s_g = k.t["s_g"]; s_gd = k.t["s_gd"]
    s_xln = k.t["s_xln"]; s_pn = [k.t["s_pn0"], k.t["s_pn1"]]
    s_rep = [k.t["s_rep0"], k.t["s_rep1"]]; s_wp = k.t["s_wp"]; s_x0 = k.t["s_x0"]
    psT = k.t["psT"]
    pb = [k.t[f"pb{i}"] for i in range(4)]
    ident = k.t["ident_sb"]
    RF = 512
    NCF = T // RF
    lntmp = {"m": t["lm"], "s1": t["ls1"], "s2": t["ls2"], "var": t["lvar"],
             "rstd": t["lrstd"], "sq": carve(t["fscr"], 0, 4096, (4096,)), "xn": None}
    simtok = carve(t["fscr"], 0, 2048, (4, 8, 64))
    simT = carve(t["fscr"], 4096, 4096, (8, 512))
    allsc = carve(t["fscr"], 2048, 256, (4, 64))
    allix = carve(t["fscr"], 2304, 256, (4, 64))
    oh = carve(t["fscr"], 4096, 2048, (4, 8, 64))
    pnl_done = []
    wp_done = [0] * 8

    WqH = ins["L0_ffn_Wq"]; WgH = k.t["Wg_bf"]; WuH = k.t["Wu_bf"]
    WdH = k.t["Wd_bf"]; ueH = k.t["ue_bf"]; deTH = k.t["deT"]
    WqbH = k.t["Wq_bf"]

    for c in range(NCF):
        t0 = c * RF
        with nc.Block() as blk:
            @blk.sync
            def _(sync, c=c):
                sync.wait_ge(s_v.h, s_v.n)
                sync.wait_ge(s_s.h, s_s.n)
                sync.wait_ge(s_gd.h, s_gd.n)
                s_x0.inc(sync.dma_start(out=t["x1c"][:], in_=x_in[t0:t0 + RF, :]
                                        .rearrange("(tt p) d -> p tt d", p=128)), 16)

            @blk.vector
            def _(vector, c=c):
                vector.wait_ge(s_x0.h, s_x0.n)
        ln_chunk(k, None, t["x1c"][:], t["xnbf"][:], 4, D, s_v, s_s, lntmp)
        with nc.Block() as blk:
            @blk.tensor
            def _(tensor, c=c):
                tensor.wait_ge(s_v.h, s_v.n)
                s_base_t = s_s.n
                for tt in range(4):
                    if tt >= 2:
                        tensor.wait_ge(s_s.h, s_base_t + tt - 1)
                    for dj in range(GD):
                        ins_ = tensor.transpose(
                            psT[:, (tt % 2) * 1024 + dj * 128:(tt % 2) * 1024 + (dj + 1) * 128],
                            t["xnbf"][:, tt, dj * 128:(dj + 1) * 128], ident[:])
                    s_pe.inc(ins_, 1)

            @blk.scalar
            def _(scalar, c=c):
                base = s_pe.n - 4
                for tt in range(4):
                    scalar.wait_ge(s_pe.h, base + tt + 1)
                    scalar.copy(t["hT"][:, :, tt * 128:(tt + 1) * 128],
                                psT[:, (tt % 2) * 1024:(tt % 2) * 1024 + GD * 128]
                                .rearrange("p (a b) -> p a b", a=GD))
                    s_s.inc(scalar.drain(), 1)

        with nc.Block() as blk:
            @blk.sync
            def _(sync, c=c):
                sync.wait_ge(s_s.h, s_s.n)
                pe_base0 = s_pe.n
                pnl_done.clear()
                for m in range(4):
                    if m >= 2:
                        sync.wait_ge(s_pe.h, pe_base0 + m - 1)
                    s_pn[m % 2].inc(sync.dma_start(
                        out=(t["pnA0"] if m % 2 == 0 else t["pnA1"])[:],
                        in_=WqbH.ap()[:, :, m * 128:(m + 1) * 128]
                        .rearrange("a p t -> p a t")), 16)
                    pnl_done.append(s_pn[m % 2].n)

            @blk.tensor
            def _(tensor, c=c):
                for m in range(4):
                    tensor.wait_ge(s_pn[m % 2].h, pnl_done[m])
                    for kd in range(GD):
                        ins_ = tensor.matmul(pb[m][0:128, 0:RF],
                                             (t["pnA0"] if m % 2 == 0 else t["pnA1"])[:, kd, :],
                                             t["hT"][:, kd, :],
                                             start=(kd == 0), stop=(kd == GD - 1))
                    s_pe.inc(ins_, 1)

            @blk.scalar
            def _(scalar, c=c):
                base = s_pe.n - 4
                for m in range(4):
                    scalar.wait_ge(s_pe.h, base + m + 1)
                    scalar.copy(t["qT"][:, m, :], pb[m][0:128, 0:RF])
                    s_s.inc(scalar.drain(), 1)

        with nc.Block() as blk:
            @blk.tensor
            def _(tensor, c=c):
                s_base = s_s.n
                tensor.wait_ge(s_s.h, s_base)
                for ph in range(8):
                    p_, hh = ph // 4, ph % 4
                    if ph >= 4:
                        tensor.wait_ge(s_s.h, s_base + (ph - 4) + 1)
                    mtile = 2 * p_ + hh // 2
                    rbase = (hh % 2) * 64
                    ins_ = tensor.matmul(pb[ph % 4][0:64, 0:RF],
                                         t["keysT"][rbase:rbase + 64, ph, :],
                                         t["qT"][rbase:rbase + 64, mtile, :],
                                         start=True, stop=True)
                    s_pe.inc(ins_, 1)

            @blk.scalar
            def _(scalar, c=c):
                base = s_pe.n - 8
                for ph in range(8):
                    scalar.wait_ge(s_pe.h, base + ph + 1)
                    scalar.copy(simT[0:64, ph, :], pb[ph % 4][0:64, 0:RF])
                    s_s.inc(scalar.drain(), 1)

        with nc.Block() as blk:
            @blk.tensor
            def _(tensor, c=c):
                tensor.wait_ge(s_s.h, s_s.n)
                s_base_t = s_s.n
                grp = 0
                for tsub in range(4):
                    for phg in range(2):
                        if grp >= 2:
                            tensor.wait_ge(s_s.h, s_base_t + grp - 1)
                        for j in range(4):
                            ph = phg * 4 + j
                            ins_ = tensor.transpose(
                                pb[grp % 2][0:128, j * 64:(j + 1) * 64],
                                simT[0:64, ph, tsub * 128:(tsub + 1) * 128],
                                k.t["identf"][0:64, 0:64])
                        s_pe.inc(ins_, 1)
                        grp += 1

            @blk.scalar
            def _(scalar, c=c):
                base = s_pe.n - 8
                grp = 0
                for tsub in range(4):
                    for phg in range(2):
                        scalar.wait_ge(s_pe.h, base + grp + 1)
                        scalar.copy(simtok[:, tsub, phg * 4:(phg + 1) * 4, :]
                                    if False else
                                    simtok[:, tsub, phg * 4:(phg + 1) * 4, :],
                                    pb[grp % 2][0:128, 0:256]
                                    .rearrange("p (a b) -> p a b", a=4))
                        s_s.inc(scalar.drain(), 1)
                        grp += 1

        with nc.Block() as blk:
            @blk.vector
            def _(vector, c=c):
                vector.wait_ge(s_s.h, s_s.n)
                for tsub in range(4):
                    for ph in range(8):
                        p_, hh = ph // 4, ph % 4
                        vector.max(t["sxy"][:, tsub, p_, hh, :],
                                   simtok[:, tsub, ph, :])
                vector.drain()
                for tsub in range(4):
                    for ph in range(8):
                        p_, hh = ph // 4, ph % 4
                        vector.max_index(t["ixyu"][:, tsub, p_, hh, :],
                                         t["sxy"][:, tsub, p_, hh, :],
                                         simtok[:, tsub, ph, :])
                vector.drain()
                vector.tensor_copy(t["ixyf"][:], t["ixyu"][:].bitcast(I32))
                vector.drain()
                s_v.inc(vector.drain(), 1)
                for tsub in range(4):
                    for hh in range(4):
                        vector.tensor_tensor(
                            allsc[:, hh, :].rearrange("p (i j) -> p i j", i=8),
                            t["sxy"][:, tsub, 0, hh, :].unsqueeze(2).broadcast_to([128, 8, 8]),
                            t["sxy"][:, tsub, 1, hh, :].unsqueeze(1).broadcast_to([128, 8, 8]),
                            ALU.add)
                    for hh in range(4):
                        vector.scalar_tensor_tensor(
                            out=allix[:, hh, :].rearrange("p (i j) -> p i j", i=8),
                            in0=t["ixyf"][:, tsub, 0, hh, :].unsqueeze(2).broadcast_to([128, 8, 8]),
                            scalar=64.0,
                            in1=t["ixyf"][:, tsub, 1, hh, :].unsqueeze(1).broadcast_to([128, 8, 8]),
                            op0=ALU.mult, op1=ALU.add)
                    vector.drain()
                    for hh in range(4):
                        vector.max(t["sc8"][:, tsub, hh, :], allsc[:, hh, :])
                    vector.drain()
                    for hh in range(4):
                        vector.max_index(t["pku"][:, tsub, hh, :], t["sc8"][:, tsub, hh, :],
                                         allsc[:, hh, :])
                    vector.drain()
                    vector.tensor_copy(t["pkf"][:, tsub], t["pku"][:, tsub].bitcast(I32))
                    vector.drain()
                    for hh in range(4):
                        vector.tensor_tensor(
                            oh[:, hh, :, :],
                            t["pkf"][:, tsub, hh, :].unsqueeze(2).broadcast_to([128, 8, 64]),
                            t["iota64"][:].unsqueeze(1).broadcast_to([128, 8, 64]),
                            ALU.is_equal)
                    vector.drain()
                    for hh in range(4):
                        vector.tensor_tensor(
                            oh[:, hh, :, :], oh[:, hh, :, :],
                            allix[:, hh, :].unsqueeze(1).broadcast_to([128, 8, 64]),
                            ALU.mult)
                    vector.drain()
                    vector.reduce_sum(t["idx8f"][:, tsub].rearrange("p a i -> p (a i)"),
                                      oh[:, 0:4, :, :].rearrange("p a i j -> p (a i) j"),
                                      axis=AX.X)
                    vector.drain()
                vector.reduce_max(t["mx8"][:].rearrange("p (a b) -> p a b", a=16),
                                  t["sc8"][:, :, :, :].rearrange("p a h i -> p (a h) i"),
                                  axis=AX.X)
                vector.drain()
                vector.tensor_scalar(out=t["nm8"][:], in0=t["mx8"][:], scalar1=-1.0,
                                     scalar2=None, op0=ALU.mult)
                s_v.inc(vector.drain(), 1)

            @blk.scalar
            def _(scalar, c=c):
                scalar.wait_ge(s_v.h, s_v.n)
                for tsub in range(4):
                    for hh in range(4):
                        scalar.activation(t["sm"][:, tsub, hh, :], t["sc8"][:, tsub, hh, :],
                                          AF.Exp, bias=t["nm8"][:, tsub * 4 + hh:tsub * 4 + hh + 1],
                                          scale=1.0)
                s_s.inc(scalar.drain(), 1)

            @blk.gpsimd
            def _(gpsimd, c=c):
                pass

        with nc.Block() as blk:
            @blk.vector
            def _(vector, c=c):
                vector.wait_ge(s_s.h, s_s.n)
                vector.reduce_sum(t["sum8"][:].rearrange("p (a b) -> p a b", a=16),
                                  t["sm"][:, :, :, :].rearrange("p a h i -> p (a h) i"),
                                  axis=AX.X)
                vector.drain()
                vector.reciprocal(t["rcp8"][:], t["sum8"][:])
                vector.drain()
                vector.tensor_tensor(
                    t["sm"][:].rearrange("p a h i -> p (a h) i"),
                    t["sm"][:].rearrange("p a h i -> p (a h) i"),
                    t["rcp8"][:].unsqueeze(2).broadcast_to([128, 16, 8]), ALU.mult)
                vector.drain()
                vector.tensor_copy(t["smbf"][:].rearrange("p a h i -> p (a h i)"),
                                   t["sm"][:].rearrange("p a h i -> p (a h i)"))
                for tsub in range(4):
                    vector.scalar_tensor_tensor(
                        out=t["idx8f"][:, tsub], in0=t["idx8f"][:, tsub], scalar=512.0,
                        in1=t["tlocf"][:, tsub:tsub + 1].unsqueeze(2).broadcast_to([128, 4, 8]),
                        op0=ALU.mult, op1=ALU.add)
                vector.drain()
                vector.tensor_copy(t["idxi"][:].rearrange("p a h i -> p (a h i)"),
                                   t["idx8f"][:].rearrange("p a h i -> p (a h i)"))
                s_v.inc(vector.drain(), 1)

        with nc.Block() as blk:
            @blk.gpsimd
            def _(gpsimd, c=c):
                gpsimd.wait_ge(s_v.h, s_v.n)
                for hh in range(4):
                    for tsub in range(4):
                        for i in range(8):
                            s_gd.inc(gpsimd.indirect_dma_start(
                                out=Sh_dram[hh].ap().rearrange("a b -> (a b)").unsqueeze(1),
                                out_offset=bass.IndirectOffsetOnAxis(
                                    t["idxi"][:, tsub, hh, i:i + 1].bitcast(U32), 0),
                                in_=t["smbf"][:, tsub, hh, i:i + 1],
                                in_offset=None), 16)

        with nc.Block() as blk:
            @blk.sync
            def _(sync, c=c):
                pe_base0 = s_pe.n
                pnl_done.clear()
                for et in range(32):
                    if et >= 2:
                        sync.wait_ge(s_pe.h, pe_base0 + et - 1)
                    s_pn[et % 2].inc(sync.dma_start(
                        out=(t["pnA0"] if et % 2 == 0 else t["pnA1"])[:],
                        in_=deTH.ap()[:, :, et * 128:(et + 1) * 128]
                        .rearrange("a p t -> p a t")), 16)
                    pnl_done.append(s_pn[et % 2].n)

            @blk.tensor
            def _(tensor, c=c):
                s_base = s_s.n
                for et in range(32):
                    tensor.wait_ge(s_pn[et % 2].h, pnl_done[et])
                    if et >= 2:
                        tensor.wait_ge(s_s.h, s_base + et - 1)
                    for kd in range(GD):
                        ins_ = tensor.matmul(pb[et % 2][0:128, 0:RF],
                                             (t["pnA0"] if et % 2 == 0 else t["pnA1"])[:, kd, :],
                                             t["hT"][:, kd, :],
                                             start=(kd == 0), stop=(kd == GD - 1))
                    s_pe.inc(ins_, 1)

            @blk.scalar
            def _(scalar, c=c):
                pe_base = s_pe.n - 32
                for et in range(32):
                    scalar.wait_ge(s_pe.h, pe_base + et + 1)
                    scalar.activation(t["siluWT"][:, et, :], pb[et % 2][0:128, 0:RF], AF.Silu)
                    s_s.inc(scalar.drain(), 1)

        with nc.Block() as blk:
            @blk.sync
            def _(sync, c=c):
                sync.wait_ge(s_gd.h, s_gd.n)
                s_rep[0].inc(sync.dma_start(
                    out=t["Ssum"][:], in_=Sh_dram[0].ap()
                    .rearrange("(a p) t -> p a t", p=128)), 16)
                wp_done[0] = s_rep[0].n
                vv = s_v.n
                for hh in range(1, 4):
                    for half in range(2):
                        if hh + half > 1:
                            sync.wait_ge(s_v.h, vv + (hh - 1) * 2 + half - 1 + 1)
                        s_rep[1].inc(sync.dma_start(
                            out=t["Stmp"][:], in_=Sh_dram[hh].ap()
                            .rearrange("(a p) t -> p a t", p=128)[:, half * 16:(half + 1) * 16, :]), 16)
                        wp_done[1 + (hh - 1) * 2 + half] = s_rep[1].n

            @blk.vector
            def _(vector, c=c):
                vector.wait_ge(s_rep[0].h, wp_done[0])
                i = 0
                for hh in range(1, 4):
                    for half in range(2):
                        i += 1
                        vector.wait_ge(s_rep[1].h, wp_done[i])
                        vector.tensor_add(t["Ssum"][:, half * 16:(half + 1) * 16, :],
                                          t["Ssum"][:, half * 16:(half + 1) * 16, :],
                                          t["Stmp"][:, 0:16, :])
                        s_v.inc(vector.drain(), 1)
                vector.wait_ge(s_s.h, s_s.n)
                vector.tensor_mul(t["Ssum"][:], t["Ssum"][:], t["siluWT"][:])
                s_v.inc(vector.drain(), 1)

            @blk.gpsimd
            def _(gpsimd, c=c):
                gpsimd.wait_ge(s_v.h, s_v.n)
                for hh in range(4):
                    for tsub in range(4):
                        for i in range(8):
                            s_gd.inc(gpsimd.indirect_dma_start(
                                out=Sh_dram[hh].ap().rearrange("a b -> (a b)").unsqueeze(1),
                                out_offset=bass.IndirectOffsetOnAxis(
                                    t["idxi"][:, tsub, hh, i:i + 1].bitcast(U32), 0),
                                in_=t["zbf"][:, 0:1],
                                in_offset=None), 16)

        with nc.Block() as blk:
            @blk.sync
            def _(sync, c=c):
                pe_base0 = s_pe.n
                pnl_done.clear()
                for dt_ in range(32):
                    if dt_ >= 1:
                        sync.wait_ge(s_pe.h, pe_base0 + 2 * dt_ - 1)
                    s_pn[dt_ % 2].inc(sync.dma_start(
                        out=(t["pnA0"] if dt_ % 2 == 0 else t["pnA1"])[:],
                        in_=WgH.ap()[:, :, dt_ * 128:(dt_ + 1) * 128]
                        .rearrange("a p t -> p a t")), 16)
                    s_pn[dt_ % 2].inc(sync.dma_start(
                        out=(t["pnB0"] if dt_ % 2 == 0 else t["pnB1"])[:],
                        in_=WuH.ap()[:, :, dt_ * 128:(dt_ + 1) * 128]
                        .rearrange("a p t -> p a t")), 16)
                    pnl_done.append(s_pn[dt_ % 2].n)

            @blk.tensor
            def _(tensor, c=c):
                s_base = s_s.n
                v_base = s_v.n
                for dt_ in range(32):
                    tensor.wait_ge(s_pn[dt_ % 2].h, pnl_done[dt_])
                    if dt_ >= 2:
                        tensor.wait_ge(s_s.h, s_base + dt_ - 1)
                        tensor.wait_ge(s_v.h, v_base + dt_ - 1)
                    for kd in range(GD):
                        ins_ = tensor.matmul(pb[2 * (dt_ % 2)][0:128, 0:RF],
                                             (t["pnA0"] if dt_ % 2 == 0 else t["pnA1"])[:, kd, :],
                                             t["hT"][:, kd, :],
                                             start=(kd == 0), stop=(kd == GD - 1))
                    for kd in range(GD):
                        ins_ = tensor.matmul(pb[2 * (dt_ % 2) + 1][0:128, 0:RF],
                                             (t["pnB0"] if dt_ % 2 == 0 else t["pnB1"])[:, kd, :],
                                             t["hT"][:, kd, :],
                                             start=(kd == 0), stop=(kd == GD - 1))
                    s_pe.inc(ins_, 2)

            @blk.scalar
            def _(scalar, c=c):
                pe_base = s_pe.n - 64
                v_base = s_v.n
                for dt_ in range(32):
                    scalar.wait_ge(s_pe.h, pe_base + 2 * (dt_ + 1))
                    if dt_ >= 1:
                        scalar.wait_ge(s_v.h, v_base + dt_)
                    scalar.activation(t["etmp"][:], pb[2 * (dt_ % 2)][0:128, 0:RF],
                                      AF.Silu, bias=t["bgc"][:, dt_:dt_ + 1])
                    s_s.inc(scalar.drain(), 1)

            @blk.vector
            def _(vector, c=c):
                s_base2 = s_s.n - 32
                for dt_ in range(32):
                    vector.wait_ge(s_s.h, s_base2 + dt_ + 1)
                    vector.scalar_tensor_tensor(
                        out=t["gate"][:, dt_, :], in0=pb[2 * (dt_ % 2) + 1][0:128, 0:RF],
                        scalar=t["buc"][:, dt_:dt_ + 1], in1=t["etmp"][:],
                        op0=ALU.add, op1=ALU.mult)
                    s_v.inc(vector.drain(), 1)

        with nc.Block() as blk:
            @blk.sync
            def _(sync, c=c):
                pe_base0 = s_pe.n
                pnl_done.clear()
                idx = 0
                for dch in range(2):
                    for kd in range(64):
                        if idx >= 2:
                            sync.wait_ge(s_pe.h, pe_base0 + idx - 1)
                        srcH = WdH if kd < 32 else ueH
                        kk = kd % 32
                        s_pn[idx % 2].inc(sync.dma_start(
                            out=(t["pnC0"] if idx % 2 == 0 else t["pnC1"])[:],
                            in_=srcH.ap()[kk, :, dch * 512:(dch + 1) * 512]), 16)
                        pnl_done.append(s_pn[idx % 2].n)
                        idx += 1

            @blk.tensor
            def _(tensor, c=c):
                v_base = s_v.n
                idx = 0
                for dch in range(2):
                    if dch == 1:
                        tensor.wait_ge(s_v.h, v_base + 1)
                    for kd in range(64):
                        tensor.wait_ge(s_pn[idx % 2].h, pnl_done[idx])
                        lhs = t["gate"] if kd < 32 else t["Ssum"]
                        kk = kd % 32
                        for tsub in range(4):
                            ins_ = tensor.matmul(
                                pb[tsub][0:128, 0:512],
                                lhs[:, kk, tsub * 128:(tsub + 1) * 128],
                                (t["pnC0"] if idx % 2 == 0 else t["pnC1"])[:],
                                start=(kd == 0), stop=(kd == 63))
                        s_pe.inc(ins_, 1)
                        idx += 1

            @blk.vector
            def _(vector, c=c):
                pe_base = s_pe.n - 128
                for dch in range(2):
                    vector.wait_ge(s_pe.h, pe_base + 64 * (dch + 1))
                    for tsub in range(4):
                        vector.tensor_add(t["x1c"][:, tsub, dch * 512:(dch + 1) * 512],
                                          t["x1c"][:, tsub, dch * 512:(dch + 1) * 512],
                                          pb[tsub][0:128, 0:512])
                    vector.drain()
                    for tsub in range(4):
                        vector.tensor_add(t["x1c"][:, tsub, dch * 512:(dch + 1) * 512],
                                          t["x1c"][:, tsub, dch * 512:(dch + 1) * 512],
                                          t["bdR"][:, dch * 512:(dch + 1) * 512])
                    s_v.inc(vector.drain(), 1)

            @blk.gpsimd
            def _(gpsimd, c=c):
                gpsimd.wait_ge(s_v.h, s_v.n)
                s_gd.inc(gpsimd.dma_start(
                    out=x_out[t0:t0 + RF, :].rearrange("(tt p) d -> p tt d", p=128),
                    in_=t["x1c"][:]), 16)
                if dbg is not None:
                    s_gd.inc(gpsimd.dma_start(
                        out=dbg["dbg_gate"].ap()[:, :, t0:t0 + RF].rearrange("a p t -> p a t"),
                        in_=t["gate"][:]), 16)
                    s_gd.inc(gpsimd.dma_start(
                        out=dbg["dbg_sfin"].ap()[:, :, t0:t0 + RF].rearrange("a p t -> p a t"),
                        in_=t["Ssum"][:]), 16)
                    s_gd.inc(gpsimd.dma_start(
                        out=dbg["dbg_idx"].ap()[t0:t0 + RF, :].rearrange("(tt p) x -> p tt x", p=128)
                        .rearrange("p tt (h i) -> p tt h i", h=4),
                        in_=t["idx8f"][:]), 16)
                    s_gd.inc(gpsimd.dma_start(
                        out=dbg["dbg_sm"].ap()[t0:t0 + RF, :].rearrange("(tt p) x -> p tt x", p=128)
                        .rearrange("p tt (h i) -> p tt h i", h=4),
                        in_=t["sm"][:]), 16)

    with nc.Block() as blk:
        @blk.sync
        def _(sync):
            sync.wait_ge(s_gd.h, s_gd.n)


def build_prep_keys(k, srch, dsth, ident):
    nc = k.nc
    s_ld = k.t["s_ld"]; s_v = k.t["s_v"]; s_s = k.t["s_s"]; s_pe = k.t["s_pe"]; s_st = k.t["s_st"]
    psT = k.t["psT"]
    tf = k.t["wp_f32"]; tb = k.t["wp_bf"]; to = k.t["wp_out"]
    for p_ in range(2):
        for h in range(4):
            with nc.Block() as blk:
                @blk.sync
                def _(sync, p_=p_, h=h):
                    sync.wait_ge(s_v.h, s_v.n)
                    s_ld.inc(sync.dma_start(out=tf[0:64, 0:64],
                                            in_=srch.ap()[h, :, p_, :]), 16)

                @blk.vector
                def _(vector):
                    vector.wait_ge(s_ld.h, s_ld.n)
                    vector.wait_ge(s_pe.h, s_pe.n)
                    vector.tensor_copy(tb[0:64, 0:64], tf[0:64, 0:64])
                    s_v.inc(vector.drain(), 1)

                @blk.tensor
                def _(tensor):
                    tensor.wait_ge(s_v.h, s_v.n)
                    tensor.wait_ge(s_s.h, s_s.n)
                    s_pe.inc(tensor.transpose(psT[0:64, 0:64], tb[0:64, 0:64],
                                              k.t["ident_sb"][0:64, 0:64]), 1)

                @blk.scalar
                def _(scalar):
                    scalar.wait_ge(s_pe.h, s_pe.n)
                    scalar.wait_ge(s_st.h, s_st.n)
                    scalar.copy(to[0:64, 0:64], psT[0:64, 0:64])
                    s_s.inc(scalar.drain(), 1)

                @blk.gpsimd
                def _(gpsimd, p_=p_, h=h):
                    gpsimd.wait_ge(s_s.h, s_s.n)
                    s_st.inc(gpsimd.dma_start(out=dsth.ap()[p_, h, :, :],
                                              in_=to[0:64, 0:64]), 16)


FFN_W_NAMES = ["Wg", "bg", "Wu", "bu", "Wd", "bd", "Wq", "keys", "down_embed", "up_embed"]


def build_zero_sh(k, Sh):
    nc = k.nc
    s_v = k.t["s_v"]; s_st = k.t["s_st"]
    zz = k.t["wp_bf"]
    with nc.Block() as blk:
        @blk.vector
        def _(vector):
            vector.memset(zz[:, 0:512], 0.0)
            s_v.inc(vector.drain(), 1)

        @blk.gpsimd
        def _(gpsimd):
            gpsimd.wait_ge(s_v.h, s_v.n)
            for hh in range(4):
                for a in range(32):
                    s_st.inc(gpsimd.dma_start(
                        out=Sh[hh].ap().rearrange("(a p) t -> p a t", p=128)[:, a, :],
                        in_=zz[:, 0:512].bitcast(BF16)[:, 0:512]), 16)


def build_program(phases, dbg=True, nch=NCHM, x1_ext=False):
    nc = bass.Bass("TRN2", target_bir_lowering=False)
    es = ExitStack()
    k = K(nc, es)
    ins = {}
    ins["x"] = k.dram("x", [T, D], F32, kind="ExternalInput")
    for nm in MAMBA_W_NAMES:
        full = "L0_attn_a_" + nm
        shp = {"Win": [2 * DI, D], "convw": [DI, 4], "convb": [DI], "Wx": [128, DI],
               "Wdt": [DI, 64], "bdt": [DI], "Alog": [DI, NS], "Dp": [DI],
               "Wout": [D, DI], "ln_g": [D], "ln_b": [D], "Wfuse": [D, DI],
               "bfuse": [D]}[nm.rsplit("_", 1)[0] if nm not in ("Wfuse", "bfuse") else nm]
        ins[full] = k.dram(full, shp, F32, kind="ExternalInput")
    for nm in FFN_W_NAMES:
        full = "L0_ffn_" + nm
        shp = {"Wg": [D, 4096], "bg": [4096], "Wu": [D, 4096], "bu": [4096],
               "Wd": [4096, D], "bd": [D], "Wq": [D, 512], "keys": [4, 64, 2, 64],
               "down_embed": [4096, D], "up_embed": [4096, D]}[nm]
        ins[full] = k.dram(full, shp, F32, kind="ExternalInput")
    for nm, shp in [("W_attn", [D, 3 * D]), ("b_attn", [3 * D]), ("W_proj", [D, D]),
                    ("b_proj", [D]), ("temperature", [1])]:
        ins["L0_attn_b_" + nm] = k.dram("L0_attn_b_" + nm, shp, F32, kind="ExternalInput")
    ident_in = k.dram("ident", [128, 128], BF16, kind="ExternalInput")
    iotain = k.dram("iotain", [128, 84], F32, kind="ExternalInput")
    k.t["iotain_in"] = iotain
    k.t["iotaT_in"] = k.dram("iotaT", [T], F32, kind="ExternalInput")

    k.dram("xlnT", [GD, 128, T], BF16)
    if x1_ext:
        x1h = k.dram("x1", [T, D], F32, kind="ExternalInput")
    else:
        x1h = k.dram("x1", [T, D], F32, kind="ExternalOutput" if "ffn1" not in phases else "Internal")
    if "ffn1" not in phases and "attn" in phases:
        x2h = k.dram("x2", [T, D], F32, kind="ExternalInput")
    else:
        x2h = k.dram("x2", [T, D], F32,
                     kind="Internal" if "attn" in phases else "ExternalOutput")
    k.dram("bc_scr", [64, RM], F32)
    for dd in "fb":
        k.dram(f"WinT_{dd}", [GD, 128, 2 * DI], BF16)
        k.dram(f"WoutT_{dd}", [GI, 128, D], BF16)
        k.dram(f"WxT_{dd}", [GI, 128, 128], BF16)
        k.dram(f"WdtT_{dd}", [64, DI], BF16)
    k.dram("WfuseT", [GI, 128, D], BF16)
    k.dram("Wg_bf", [GD, 128, 4096], BF16)
    k.dram("Wu_bf", [GD, 128, 4096], BF16)
    k.dram("Wd_bf", [32, 128, D], BF16)
    k.dram("Wq_bf", [GD, 128, 512], BF16)
    k.dram("ue_bf", [32, 128, D], BF16)
    k.dram("deT", [GD, 128, 4096], BF16)
    k.dram("keysT_dram", [2, 4, 64, 64], BF16)
    Sh = [k.dram(f"Sh{i}", [4096, 512], BF16) for i in range(4)]
    k.dram("x2lnT", [GD, 128, T], BF16)
    k.dram("qkT", [16, 128, T], BF16)
    k.dram("Vtok", [T, D], BF16)
    k.dram("attnT", [GD, 128, T], BF16)
    k.dram("Wattn_bf", [GD, 128, 3 * D], BF16)
    k.dram("Wproj_bf", [GD, 128, D], BF16)
    x3h = k.dram("x3", [T, D], F32,
                 kind="Internal" if "ffn2" in phases else "ExternalOutput")
    x4h = k.dram("x4", [T, D], F32, kind="ExternalOutput") if "ffn2" in phases else None

    dbg_t = None
    if dbg:
        dbg_t = {}
        if "mamba" in phases:
            for dd in ("f", "b"):
                for nmm in ("uc", "dt", "ys", "u", "z"):
                    dbg_t[f"dbg_{nmm}_{dd}"] = k.dram(f"dbg_{nmm}_{dd}", [GI, 128, T], F32,
                                                      kind="ExternalOutput")
        if "ffn1" in phases:
            dbg_t["dbg_gate"] = k.dram("dbg_gate", [32, 128, T], BF16, kind="ExternalOutput")
            dbg_t["dbg_sfin"] = k.dram("dbg_sfin", [32, 128, T], BF16, kind="ExternalOutput")
            dbg_t["dbg_idx"] = k.dram("dbg_idx", [T, 32], F32, kind="ExternalOutput")
            dbg_t["dbg_sm"] = k.dram("dbg_sm", [T, 32], F32, kind="ExternalOutput")

    with es:
        for s in ["s_ld", "s_v", "s_s", "s_pe", "s_st", "s_g", "s_gd", "s_bc",
                  "s_xln", "s_pn0", "s_pn1", "s_rep0", "s_rep1", "s_wp", "s_x0",
                  "s_qk0", "s_qk1"]:
            k.sem(s)
        psT = k.ps("psT", [128, 2048], BF16)
        for i in range(4):
            k.ps(f"pb{i}", [128, 512], F32)
        ident_sb = k.sb("ident_sb", [128, 128], BF16)
        k.sb("identf", [128, 128], F32)
        k.sb("epsT", [128, 1], F32)
        k.push_scope()
        k.sb("wp_f32", [128, 4096], F32)
        k.sb("wp_bf", [128, 4096], BF16)
        k.sb("wp_out", [128, 4096], BF16)

        with nc.Block() as blk:
            @blk.sync
            def _(sync):
                k.t["s_ld"].inc(sync.dma_start(out=ident_sb[:], in_=ident_in.ap()[:, :]), 16)

            @blk.vector
            def _(vector):
                vector.wait_ge(k.t["s_ld"].h, k.t["s_ld"].n)
                vector.memset(k.t["epsT"][:], EPS)
                vector.tensor_copy(k.t["identf"][:], ident_sb[:])
                k.t["s_v"].inc(vector.drain(), 1)

        trans = []
        casts = []
        if "mamba" in phases:
            for dd in "fb":
                trans += [(ins[f"L0_attn_a_Win_{dd}"], k.t[f"WinT_{dd}"], 2 * DI, D),
                          (ins[f"L0_attn_a_Wout_{dd}"], k.t[f"WoutT_{dd}"], D, DI),
                          (ins[f"L0_attn_a_Wx_{dd}"], k.t[f"WxT_{dd}"], 128, DI)]
            trans += [(ins["L0_attn_a_Wfuse"], k.t["WfuseT"], D, DI)]
        if "attn" in phases:
            casts += [(ins["L0_attn_b_W_attn"], k.t["Wattn_bf"], D, 3 * D),
                      (ins["L0_attn_b_W_proj"], k.t["Wproj_bf"], D, D)]
        if "ffn1" in phases:
            trans += [(ins["L0_ffn_down_embed"], k.t["deT"], 4096, D)]
            casts += [(ins["L0_ffn_Wg"], k.t["Wg_bf"], D, 4096),
                      (ins["L0_ffn_Wu"], k.t["Wu_bf"], D, 4096),
                      (ins["L0_ffn_Wd"], k.t["Wd_bf"], 4096, D),
                      (ins["L0_ffn_Wq"], k.t["Wq_bf"], D, 512),
                      (ins["L0_ffn_up_embed"], k.t["ue_bf"], 4096, D)]
        build_weight_prep(k, casts, trans, ident_sb)
        if "mamba" in phases:
            for dd in "fb":
                build_prep_wdt(k, ins[f"L0_attn_a_Wdt_{dd}"], k.t[f"WdtT_{dd}"], ident_sb)
        if "ffn1" in phases:
            build_prep_keys(k, ins["L0_ffn_keys"], k.t["keysT_dram"], ident_sb)
            build_zero_sh(k, Sh)

        if "mamba" in phases:
            build_ln_prepass(k, ins["x"].ap(), k.t["xlnT"], ident_sb, nch=nch)
        k.pop_scope()

        if "mamba" in phases:
            k.push_scope()
            alloc_mamba(k)
            weights = {}
            for dd in "fb":
                weights[f"WinT_{dd}"] = k.t[f"WinT_{dd}"]
                weights[f"WoutT_{dd}"] = k.t[f"WoutT_{dd}"]
            weights["WfuseT"] = k.t["WfuseT"]
            for dd in ("f", "b"):
                build_mamba_dir_setup(k, dd, ins)
                build_mamba_dir(k, dd, weights, k.t["xlnT"],
                                ins["x"].ap() if dd == 'f' else x1h.ap(),
                                x1h, dbg=dbg_t, nch=nch)
            k.pop_scope()

        if "ffn1" in phases:
            k.push_scope()
            t = alloc_ffn(k, "f1")
            build_ffn_setup(k, t, ins, iotain)
            build_ffn(k, t, ins, x1h.ap(), x2h.ap(), Sh, True, dbg=dbg_t)
            k.pop_scope()

        if "attn" in phases:
            k.push_scope()
            ta = alloc_attn(k)
            build_attn(k, ta, ins, x2h.ap(), x3h.ap(), dbg=dbg_t)
            k.pop_scope()

        if "ffn2" in phases:
            k.push_scope()
            t2 = alloc_ffn(k, "f2")
            build_ffn_setup(k, t2, ins, iotain)
            build_ffn(k, t2, ins, x3h.ap(), x4h.ap(), Sh, False, dbg=None)
            k.pop_scope()

        with nc.Block() as blk:
            @blk.sync
            def _(sync):
                sync.wait_ge(k.t["s_gd"].h, k.t["s_gd"].n)
                sync.wait_ge(k.t["s_st"].h, k.t["s_st"].n)

    return nc


ALIBI_START = 2.0 ** (-2.0 ** (-(np.log2(16) - 3)))
ALIBI_SLOPES = [ALIBI_START * ALIBI_START ** i for i in range(16)]


def alloc_attn(k):
    t = {}
    def sb(name, shape, dt=F32):
        t[name] = k.sb("at_" + name, shape, dt)
        return t[name]
    sb("xlnTc", [128, GD, 512], BF16)
    sb("x2c", [128, 4, D], F32)
    sb("xnbf", [128, 4, D], BF16)
    sb("brow", [1, 3 * D], F32)
    sb("browb", [1, 3 * D], BF16)
    sb("bprow", [1, D], F32)
    sb("bprowb", [1, D], BF16)
    sb("onesr", [1, 512], BF16)
    sb("qscale", [128, 1], F32)
    sb("traw", [128, 1], F32)
    sb("kposR", [128, T], F32)
    sb("tslope", [128, 16], F32)
    sb("alibiF", [128, T], F32)
    sb("scores_sb", [128, T], F32)
    sb("Qp", [128, 128], BF16)
    sb("Kfull", [128, T], BF16)
    sb("Vfull", [128, 32, 64], BF16)
    sb("probs", [128, T], BF16)
    sb("probsT_sb", [128, 2, 128], BF16)
    sb("pnq0", [128, GD, 128], BF16)
    sb("pnq1", [128, GD, 128], BF16)
    sb("pnv", [128, GD, 512], BF16)
    sb("qk_ev", [128, 2, 512], BF16)
    sb("vev", [128, 2, 4, 512], BF16)
    sb("mparts", [128, 32], F32)
    sb("lparts", [128, 32], F32)
    sb("masks", [128, 4, 512], F32)
    sb("mrow", [128, 1], F32)
    sb("btile", [128, 1], F32)
    sb("lrow", [128, 1], F32)
    sb("lpart", [128, 1], F32)
    sb("lrec", [128, 1], F32)
    sb("atile", [128, 128], BF16)
    sb("arng", [128, 1], F32)
    sb("lsq", [128, 4096], F32)
    sb("lm", [128, 4]); sb("ls1", [128, 4]); sb("ls2", [128, 4])
    sb("lvar", [128, 4]); sb("lrstd", [128, 4])
    return t


def build_attn(k, t, ins, x_in, x_out, dbg=None):
    nc = k.nc
    s_ld = k.t["s_ld"]; s_v = k.t["s_v"]; s_s = k.t["s_s"]; s_pe = k.t["s_pe"]
    s_st = k.t["s_st"]; s_g = k.t["s_g"]; s_gd = k.t["s_gd"]
    s_xln = k.t["s_xln"]; s_pn = [k.t["s_pn0"], k.t["s_pn1"]]
    s_rep = [k.t["s_rep0"], k.t["s_rep1"]]; s_wp = k.t["s_wp"]; s_x0 = k.t["s_x0"]
    psT = k.t["psT"]
    pb = [k.t[f"pb{i}"] for i in range(4)]
    ident = k.t["ident_sb"]
    x2lnT = k.t["x2lnT"]; qkT = k.t["qkT"]; Vtok = k.t["Vtok"]; attnT = k.t["attnT"]
    WattnH = k.t["Wattn_bf"]; WprojH = k.t["Wproj_bf"]
    lntmp = {"m": t["lm"], "s1": t["ls1"], "s2": t["ls2"], "var": t["lvar"],
             "rstd": t["lrstd"], "sq": t["lsq"][:, 0:4096], "xn": None}
    pnl_done = []
    wp_done = [0] * 8

    with nc.Block() as blk:
        @blk.sync
        def _(sync):
            sync.wait_ge(s_v.h, s_v.n)
            sync.wait_ge(s_s.h, s_s.n)
            sync.wait_ge(s_gd.h, s_gd.n)
            s_ld.inc(sync.dma_start(out=t["brow"][0:1, :],
                                    in_=ins["L0_attn_b_b_attn"].ap().unsqueeze(0)), 16)
            s_ld.inc(sync.dma_start(out=t["bprow"][0:1, :],
                                    in_=ins["L0_attn_b_b_proj"].ap().unsqueeze(0)), 16)
            s_ld.inc(sync.dma_start(out=t["qscale"][:],
                                    in_=ins["L0_attn_b_temperature"].ap()
                                    .unsqueeze(0).broadcast_to([128, 1])), 16)
            s_ld.inc(sync.dma_start(out=t["traw"][:],
                                    in_=ins["L0_attn_b_temperature"].ap()
                                    .unsqueeze(0).broadcast_to([128, 1])), 16)
            s_ld.inc(sync.dma_start(out=t["kposR"][:],
                                    in_=k.t["iotaT_in"].ap().unsqueeze(0)
                                    .broadcast_to([128, T])), 16)
            s_ld.inc(sync.dma_start(out=t["arng"][:],
                                    in_=k.t["iotaT_in"].ap()[0:128].unsqueeze(1)), 16)
            s_ld.inc(sync.dma_start(out=t["tslope"][:],
                                    in_=k.t["iotain_in"].ap()[:, 68:84]), 16)

        @blk.vector
        def _(vector):
            vector.wait_ge(s_ld.h, s_ld.n)
            vector.tensor_scalar(out=t["qscale"][:], in0=t["qscale"][:], scalar1=8.0,
                                 scalar2=None, op0=ALU.mult)
            vector.drain()
            vector.reciprocal(t["qscale"][:], t["qscale"][:])
            vector.drain()
            vector.memset(t["onesr"][0:1, :], 1.0)
            vector.memset(t["masks"][:], 0.0)
            vector.tensor_scalar(out=t["tslope"][:], in0=t["tslope"][:],
                                 scalar1=t["traw"][:, 0:1], scalar2=8.0,
                                 op0=ALU.mult, op1=ALU.mult)
            vector.drain()
            vector.tensor_copy(t["browb"][0:1, :], t["brow"][0:1, :])
            vector.tensor_copy(t["bprowb"][0:1, :], t["bprow"][0:1, :])
            s_v.inc(vector.drain(), 1)

    with nc.Block() as blk:
        @blk.gpsimd
        def _(gpsimd):
            gpsimd.wait_ge(s_v.h, s_v.n)
            for m_ in range(4):
                gpsimd.affine_select(t["masks"][:, m_, :], t["masks"][:, m_, :],
                                     pattern=[[-1, 512]], compare_op=ALU.is_ge,
                                     fill=-1e30, base=m_ * 128, channel_multiplier=1)
            s_g.inc(gpsimd.drain(), 1)
    g_tgt = s_g.n

    RF = 512
    for c in range(T // RF):
        t0 = c * RF
        with nc.Block() as blk:
            @blk.sync
            def _(sync, c=c):
                sync.wait_ge(s_st.h, s_st.n)
                sync.wait_ge(s_v.h, s_v.n)
                s_x0.inc(sync.dma_start(out=t["x2c"][:], in_=x_in[t0:t0 + RF, :]
                                        .rearrange("(tt p) d -> p tt d", p=128)), 16)

            @blk.vector
            def _(vector, c=c):
                vector.wait_ge(s_x0.h, s_x0.n)
        ln_chunk(k, None, t["x2c"][:], t["xnbf"][:], 4, D, s_v, s_s, lntmp)
        with nc.Block() as blk:
            @blk.tensor
            def _(tensor, c=c):
                tensor.wait_ge(s_v.h, s_v.n)
                s_base_t = s_s.n
                for tt in range(4):
                    if tt >= 2:
                        tensor.wait_ge(s_s.h, s_base_t + tt - 1)
                    for dj in range(GD):
                        ins_ = tensor.transpose(
                            psT[:, (tt % 2) * 1024 + dj * 128:(tt % 2) * 1024 + (dj + 1) * 128],
                            t["xnbf"][:, tt, dj * 128:(dj + 1) * 128], ident[:])
                    s_pe.inc(ins_, 1)

            @blk.scalar
            def _(scalar, c=c):
                base = s_pe.n - 4
                for tt in range(4):
                    scalar.wait_ge(s_pe.h, base + tt + 1)
                    scalar.copy(t["xlnTc"][:, :, tt * 128:(tt + 1) * 128],
                                psT[:, (tt % 2) * 1024:(tt % 2) * 1024 + GD * 128]
                                .rearrange("p (a b) -> p a b", a=GD))
                    s_s.inc(scalar.drain(), 1)

            @blk.gpsimd
            def _(gpsimd, c=c):
                gpsimd.wait_ge(s_s.h, s_s.n)
                s_st.inc(gpsimd.dma_start(
                    out=x2lnT.ap()[:, :, t0:t0 + RF].rearrange("a p t -> p a t"),
                    in_=t["xlnTc"][:]), 16)

        with nc.Block() as blk:
            @blk.sync
            def _(sync, c=c):
                pe_base0 = s_pe.n
                pnl_done.clear()
                for ft in range(16):
                    if ft >= 2:
                        sync.wait_ge(s_pe.h, pe_base0 + ft - 1)
                    s_pn[ft % 2].inc(sync.dma_start(
                        out=(t["pnq0"] if ft % 2 == 0 else t["pnq1"])[:],
                        in_=WattnH.ap()[:, :, ft * 128:(ft + 1) * 128]
                        .rearrange("a p m -> p a m")), 16)
                    pnl_done.append(s_pn[ft % 2].n)

            @blk.tensor
            def _(tensor, c=c):
                s_base = s_s.n
                for ft in range(16):
                    tensor.wait_ge(s_pn[ft % 2].h, pnl_done[ft])
                    if ft >= 2:
                        tensor.wait_ge(s_s.h, s_base + ft - 1)
                    for kd in range(GD):
                        ins_ = tensor.matmul(pb[ft % 2][0:128, 0:RF],
                                             (t["pnq0"] if ft % 2 == 0 else t["pnq1"])[:, kd, :],
                                             t["xlnTc"][:, kd, :],
                                             start=(kd == 0), stop=False)
                    ins_ = tensor.matmul(pb[ft % 2][0:128, 0:RF],
                                         t["browb"][0:1, ft * 128:(ft + 1) * 128],
                                         t["onesr"][0:1, 0:RF],
                                         start=False, stop=True)
                    s_pe.inc(ins_, 1)

            @blk.scalar
            def _(scalar, c=c):
                pe_base = s_pe.n - 16
                qk_b = [k.t["s_qk0"].n, k.t["s_qk1"].n]
                for ft in range(16):
                    scalar.wait_ge(s_pe.h, pe_base + ft + 1)
                    if ft >= 2:
                        scalar.wait_ge(k.t[f"s_qk{ft % 2}"].h,
                                       qk_b[ft % 2] + 16 * (ft // 2))
                    else:
                        scalar.wait_ge(k.t[f"s_qk{ft % 2}"].h, qk_b[ft % 2])
                    scalar.copy(t["qk_ev"][:, ft % 2, :], pb[ft % 2][0:128, 0:RF])
                    s_s.inc(scalar.drain(), 1)

            @blk.gpsimd
            def _(gpsimd, c=c):
                s_base = s_s.n - 16
                for ft in range(16):
                    gpsimd.wait_ge(s_s.h, s_base + ft + 1)
                    k.t[f"s_qk{ft % 2}"].inc(gpsimd.dma_start(
                        out=qkT.ap()[ft, :, t0:t0 + RF], in_=t["qk_ev"][:, ft % 2, :]), 16)

        with nc.Block() as blk:
            @blk.sync
            def _(sync, c=c):
                pe_base0 = s_pe.n
                wp_done[0] = 0
                for vch in range(2):
                    if vch == 1:
                        sync.wait_ge(s_pe.h, pe_base0 + 4)
                    s_wp.inc(sync.dma_start(
                        out=t["pnv"][:], in_=WattnH.ap()[:, :, 2 * D + vch * 512:2 * D + (vch + 1) * 512]
                        .rearrange("a p m -> p a m")), 16)
                    wp_done[vch] = s_wp.n

            @blk.tensor
            def _(tensor, c=c):
                v_base = s_v.n
                for vch in range(2):
                    tensor.wait_ge(s_wp.h, wp_done[vch])
                    if vch == 1:
                        tensor.wait_ge(s_v.h, v_base + 4)
                    for tsub in range(4):
                        for kd in range(GD):
                            ins_ = tensor.matmul(pb[tsub][0:128, 0:512],
                                                 t["xlnTc"][:, kd, tsub * 128:(tsub + 1) * 128],
                                                 t["pnv"][:, kd, :],
                                                 start=(kd == 0), stop=False)
                        ins_ = tensor.matmul(pb[tsub][0:128, 0:512],
                                             t["onesr"][0:1, tsub * 128:(tsub + 1) * 128],
                                             t["browb"][0:1, 2 * D + vch * 512:2 * D + (vch + 1) * 512],
                                             start=False, stop=True)
                        s_pe.inc(ins_, 1)

            @blk.vector
            def _(vector, c=c):
                pe_base = s_pe.n - 8
                vector.wait_ge(s_gd.h, s_gd.n)
                i = 0
                for vch in range(2):
                    for tsub in range(4):
                        i += 1
                        vector.wait_ge(s_pe.h, pe_base + i)
                        vector.tensor_copy(t["vev"][:, vch, tsub, :],
                                           pb[tsub][0:128, 0:512])
                        s_v.inc(vector.drain(), 1)

            @blk.gpsimd
            def _(gpsimd, c=c):
                v_base = s_v.n - 8
                i = 0
                for vch in range(2):
                    for tsub in range(4):
                        i += 1
                        gpsimd.wait_ge(s_v.h, v_base + i)
                        s_gd.inc(gpsimd.dma_start(
                            out=Vtok.ap()[t0 + tsub * 128:t0 + (tsub + 1) * 128,
                                          vch * 512:(vch + 1) * 512],
                            in_=t["vev"][:, vch, tsub, :]), 16)

    with nc.Block() as blk:
        @blk.sync
        def _(sync):
            sync.wait_ge(s_gd.h, s_gd.n)
            sync.wait_ge(k.t["s_qk0"].h, k.t["s_qk0"].n)
            sync.wait_ge(k.t["s_qk1"].h, k.t["s_qk1"].n)

    for h in range(16):
        ktile_f = 8 + h // 2
        krow = (h % 2) * 64
        kf_tgt = [0]
        with nc.Block() as blk:
            @blk.sync
            def _(sync, h=h):
                sync.wait_ge(s_v.h, s_v.n)
                sync.wait_ge(s_s.h, s_s.n)
                s_rep[0].inc(sync.dma_start(
                    out=t["Kfull"][0:64, :],
                    in_=qkT.ap()[ktile_f, krow:krow + 64, :]), 16)
                s_rep[0].inc(sync.dma_start(
                    out=t["Vfull"][:],
                    in_=Vtok.ap()[:, h * 64:(h + 1) * 64]
                    .rearrange("(a p) v -> p a v", p=128)), 16)

            @blk.vector
            def _(vector, h=h):
                vector.wait_ge(s_rep[0].h, s_rep[0].n)
                vector.wait_ge(s_g.h, g_tgt)
                vector.tensor_scalar(out=t["alibiF"][:], in0=t["kposR"][:],
                                     scalar1=t["tslope"][:, h:h + 1], scalar2=None,
                                     op0=ALU.mult)
                s_v.inc(vector.drain(), 1)
        kf_tgt[0] = s_v.n

        for q in range(32):
            q0 = q * 128
            kmax = q0 + 128
            nkb = (kmax + 511) // 512
            with nc.Block() as blk:
                b_pe = s_pe.n
                b_s = s_s.n
                b_v = s_v.n
                ntot = sum((min(512, kmax - kb * 512) + 127) // 128 for kb in range(nkb))
                qp_tgt = [0]

                @blk.sync
                def _(sync, h=h, q=q):
                    sync.wait_ge(s_s.h, s_s.n)
                    s_xln.inc(sync.dma_start(
                        out=t["Qp"][0:64, :],
                        in_=qkT.ap()[h // 2, (h % 2) * 64:(h % 2) * 64 + 64,
                                     q * 128:(q + 1) * 128]), 16)
                    qp_tgt[0] = s_xln.n

                @blk.tensor
                def _(tensor, h=h, q=q, nkb=nkb, kmax=kmax,
                      b_v=b_v, b_s=b_s, kf=kf_tgt[0]):
                    tensor.wait_ge(s_v.h, kf)
                    tensor.wait_ge(s_xln.h, qp_tgt[0])
                    tensor.wait_ge(s_s.h, b_s)
                    for kb in range(nkb):
                        n = min(512, kmax - kb * 512)
                        if kb >= 2:
                            tensor.wait_ge(s_s.h, b_s + kb - 1)
                        ins_ = tensor.matmul(pb[kb % 2][0:128, 0:n],
                                             t["Qp"][0:64, :],
                                             t["Kfull"][0:64, kb * 512:kb * 512 + n],
                                             start=True, stop=True)
                        s_pe.inc(ins_, 1)
                    ntot = sum((min(512, kmax - kb * 512) + 127) // 128 for kb in range(nkb))
                    tiles = []
                    for kb in range(nkb):
                        n = min(512, kmax - kb * 512)
                        for j in range((n + 127) // 128):
                            tiles.append((kb, j, min(128, n - j * 128)))

                    def emit_T(jj):
                        kb, j, nn = tiles[jj]
                        if jj == 0:
                            tensor.wait_ge(s_v.h, b_v + nkb + 2)
                        if jj >= 2:
                            tensor.wait_ge(s_s.h, b_s + 2 * nkb + jj - 1)
                        ins_ = tensor.transpose(
                            psT[0:nn, (jj % 2) * 1024:(jj % 2) * 1024 + 128],
                            t["probs"][:, kb * 512 + j * 128:kb * 512 + j * 128 + nn],
                            ident[:])
                        s_pe.inc(ins_, 1)

                    def emit_PV(jj):
                        kb, j, nn = tiles[jj]
                        tensor.wait_ge(s_s.h, b_s + 2 * nkb + jj + 1)
                        ins_ = tensor.matmul(pb[2][0:64, 0:128],
                                             t["Vfull"][0:nn, kb * 4 + j, :],
                                             t["probsT_sb"][0:nn, jj % 2, :],
                                             start=(jj == 0), stop=(jj == ntot - 1))
                        s_pe.inc(ins_, 1)

                    emit_T(0)
                    for jj in range(ntot - 1):
                        emit_T(jj + 1)
                        emit_PV(jj)
                    emit_PV(ntot - 1)

                @blk.vector
                def _(vector, h=h, q=q, nkb=nkb, kmax=kmax,
                      b_pe=b_pe, b_s=b_s):
                    mvar = q % 4
                    for kb in range(nkb):
                        n = min(512, kmax - kb * 512)
                        vector.wait_ge(s_pe.h, b_pe + kb + 1)
                        vector.tensor_add(pb[kb % 2][0:128, 0:n],
                                          pb[kb % 2][0:128, 0:n],
                                          t["alibiF"][:, kb * 512:kb * 512 + n])
                        vector.drain()
                        if kb == nkb - 1:
                            vector.tensor_add(pb[kb % 2][0:128, 0:n],
                                              pb[kb % 2][0:128, 0:n],
                                              t["masks"][:, mvar, 0:n])
                            vector.drain()
                        vector.reduce_max(t["mparts"][:, kb:kb + 1],
                                          pb[kb % 2][0:128, 0:n], axis=AX.X)
                        s_v.inc(vector.drain(), 1)
                    vector.reduce_max(t["mrow"][:], t["mparts"][:, 0:nkb], axis=AX.X)
                    vector.drain()
                    vector.tensor_scalar(out=t["btile"][:], in0=t["mrow"][:],
                                         scalar1=t["qscale"][:, 0:1], scalar2=-1.0,
                                         op0=ALU.mult, op1=ALU.mult)
                    s_v.inc(vector.drain(), 1)
                    vector.wait_ge(s_s.h, b_s + 2 * nkb)
                    vector.reduce_sum(t["lrow"][:], t["lparts"][:, 0:nkb], axis=AX.X)
                    vector.drain()
                    vector.reciprocal(t["lrec"][:], t["lrow"][:])
                    vector.drain()
                    vector.tensor_scalar(out=t["probs"][:, 0:kmax], in0=t["probs"][:, 0:kmax],
                                         scalar1=t["lrec"][:, 0:1], scalar2=None,
                                         op0=ALU.mult)
                    s_v.inc(vector.drain(), 1)

                @blk.scalar
                def _(scalar, h=h, q=q, nkb=nkb, kmax=kmax,
                      b_pe=b_pe, b_v=b_v):
                    for kb in range(nkb):
                        n = min(512, kmax - kb * 512)
                        scalar.wait_ge(s_v.h, b_v + kb + 1)
                        scalar.copy(t["scores_sb"][:, kb * 512:kb * 512 + n],
                                    pb[kb % 2][0:128, 0:n])
                        s_s.inc(scalar.drain(), 1)
                    scalar.wait_ge(s_v.h, b_v + nkb + 1)
                    for kb in range(nkb):
                        n = min(512, kmax - kb * 512)
                        scalar.activation(t["probs"][:, kb * 512:kb * 512 + n],
                                          t["scores_sb"][:, kb * 512:kb * 512 + n], AF.Exp,
                                          scale=t["qscale"][:, 0:1],
                                          bias=t["btile"][:, 0:1],
                                          accum_out=t["lparts"][:, kb:kb + 1])
                        s_s.inc(scalar.drain(), 1)
                    jj = 0
                    ntot = sum((min(512, kmax - kb * 512) + 127) // 128 for kb in range(nkb))
                    for kb in range(nkb):
                        n = min(512, kmax - kb * 512)
                        for j in range((n + 127) // 128):
                            nn = min(128, n - j * 128)
                            scalar.wait_ge(s_pe.h,
                                           b_pe + nkb + (1 if jj == 0 else 2 * jj))
                            scalar.copy(t["probsT_sb"][0:nn, jj % 2, :],
                                        psT[0:nn, (jj % 2) * 1024:(jj % 2) * 1024 + 128])
                            s_s.inc(scalar.drain(), 1)
                            jj += 1
                    scalar.wait_ge(s_pe.h, b_pe + nkb + 2 * ntot)
                    scalar.wait_ge(s_gd.h, s_gd.n)
                    scalar.copy(t["atile"][0:64, :], pb[2][0:64, 0:128])
                    s_s.inc(scalar.drain(), 1)

                @blk.gpsimd
                def _(gpsimd, h=h, q=q, nkb=nkb, kmax=kmax, b_s=b_s):
                    ntot = sum((min(512, kmax - kb * 512) + 127) // 128 for kb in range(nkb))
                    gpsimd.wait_ge(s_s.h, b_s + 2 * nkb + ntot + 1)
                    s_gd.inc(gpsimd.dma_start(
                        out=attnT.ap()[h // 2, (h % 2) * 64:(h % 2) * 64 + 64,
                                       q * 128:(q + 1) * 128],
                        in_=t["atile"][0:64, :]), 16)

    for c in range(T // 512):
        t0 = c * 512
        with nc.Block() as blk:
            @blk.sync
            def _(sync, c=c):
                if c == 0:
                    sync.wait_ge(s_st.h, s_st.n)
                sync.wait_ge(s_gd.h, s_gd.n)
                sync.wait_ge(s_v.h, s_v.n)
                s_xln.inc(sync.dma_start(
                    out=t["xlnTc"][:], in_=attnT.ap()[:, :, t0:t0 + 512]
                    .rearrange("a p t -> p a t")), 16)
                s_x0.inc(sync.dma_start(out=t["x2c"][:], in_=x_in[t0:t0 + 512, :]
                                        .rearrange("(tt p) d -> p tt d", p=128)), 16)
                wp_done[0] = 0
                pe_b = s_pe.n
                for dch in range(2):
                    if dch == 1:
                        sync.wait_ge(s_pe.h, pe_b + 4)
                    s_wp.inc(sync.dma_start(
                        out=t["pnv"][:], in_=WprojH.ap()[:, :, dch * 512:(dch + 1) * 512]
                        .rearrange("a p m -> p a m")), 16)
                    wp_done[dch] = s_wp.n

            @blk.tensor
            def _(tensor, c=c):
                tensor.wait_ge(s_xln.h, s_xln.n)
                v_b = s_v.n
                for dch in range(2):
                    tensor.wait_ge(s_wp.h, wp_done[dch])
                    if dch == 1:
                        tensor.wait_ge(s_v.h, v_b + 1)
                    for tsub in range(4):
                        for kd in range(GD):
                            ins_ = tensor.matmul(pb[tsub][0:128, 0:512],
                                                 t["xlnTc"][:, kd, tsub * 128:(tsub + 1) * 128],
                                                 t["pnv"][:, kd, :],
                                                 start=(kd == 0), stop=False)
                        ins_ = tensor.matmul(pb[tsub][0:128, 0:512],
                                             t["onesr"][0:1, tsub * 128:(tsub + 1) * 128],
                                             t["bprowb"][0:1, dch * 512:(dch + 1) * 512],
                                             start=False, stop=True)
                        s_pe.inc(ins_, 1)

            @blk.vector
            def _(vector, c=c):
                pe_base = s_pe.n - 8
                vector.wait_ge(s_x0.h, s_x0.n)
                for dch in range(2):
                    vector.wait_ge(s_pe.h, pe_base + 4 * (dch + 1))
                    for tsub in range(4):
                        vector.tensor_add(t["x2c"][:, tsub, dch * 512:(dch + 1) * 512],
                                          t["x2c"][:, tsub, dch * 512:(dch + 1) * 512],
                                          pb[tsub][0:128, 0:512])
                    s_v.inc(vector.drain(), 1)

            @blk.gpsimd
            def _(gpsimd, c=c):
                gpsimd.wait_ge(s_v.h, s_v.n)
                s_gd.inc(gpsimd.dma_start(
                    out=x_out[t0:t0 + 512, :].rearrange("(tt p) d -> p tt d", p=128),
                    in_=t["x2c"][:]), 16)


_NC_CACHE = {}


def _build_full():
    if "nc" not in _NC_CACHE:
        _NC_CACHE["nc"] = build_program(["mamba", "ffn1", "attn", "ffn2"],
                                        dbg=False, nch=NCHM)
    return _NC_CACHE["nc"]


def kernel(**inputs):
    nc = _build_full()
    x = np.ascontiguousarray(np.asarray(inputs["L0_attn_a_x"], np.float32))
    io = np.zeros((128, 84), np.float32)
    io[:, 0:64] = np.arange(64)[None, :]
    for ts in range(4):
        io[:, 64 + ts] = ts * 128 + np.arange(128)
    io[:, 68:84] = np.asarray(ALIBI_SLOPES, np.float32)[None, :]
    common = {"ident": np.eye(128, dtype=bfnp),
              "iotain": io,
              "iotaT": np.arange(T, dtype=np.float32)}
    for nm in MAMBA_W_NAMES:
        full = "L0_attn_a_" + nm
        common[full] = np.ascontiguousarray(np.asarray(inputs[full], np.float32))
    for nm in FFN_W_NAMES:
        full = "L0_ffn_" + nm
        common[full] = np.ascontiguousarray(np.asarray(inputs[full], np.float32))
    for nm in ["W_attn", "b_attn", "W_proj", "b_proj", "temperature"]:
        full = "L0_attn_b_" + nm
        common[full] = np.ascontiguousarray(np.asarray(inputs[full], np.float32))
    in_maps = []
    for i in range(8):
        m = dict(common)
        m["x"] = x[i]
        in_maps.append(m)
    res = run_bass_kernel_spmd(nc, in_maps, core_ids=list(range(8)))
    return np.stack([np.asarray(res.results[i]["x4"], np.float32)
                     for i in range(8)], 0)


# version 03
# speedup vs baseline: 1.1740x; 1.1740x over previous
import numpy as np
import ml_dtypes
from contextlib import ExitStack
import concourse.bass as bass
import concourse.mybir as mybir
from concourse.bass_utils import run_bass_kernel_spmd

F32 = mybir.dt.float32
BF16 = mybir.dt.bfloat16
U32 = mybir.dt.uint32
I32 = mybir.dt.int32
AF = mybir.ActivationFunctionType
ALU = mybir.AluOpType
AX = mybir.AxisListType
bfnp = ml_dtypes.bfloat16

T = 4096
D = 1024
DI = 2048
GI = 16
GD = 8
NS = 32
RM = 256
NCHM = T // RM
NQ = 4
QN = 8
EPS = 1e-5


class Sem:
    def __init__(self, es, nc, name):
        self.h = es.enter_context(nc.semaphore(name))
        self.n = 0

    def inc(self, instr, k=1):
        instr.then_inc(self.h, k)
        self.n += k
        return instr


class K:
    def __init__(self, nc, es):
        self.nc = nc
        self.es = es
        self.scopes = []
        self.t = {}

    def push_scope(self):
        self.scopes.append(ExitStack())

    def pop_scope(self):
        self.scopes.pop().close()

    def sb(self, name, shape, dt=F32):
        es = self.scopes[-1] if self.scopes else self.es
        h = es.enter_context(self.nc.sbuf_tensor(name, list(shape), dt))
        self.t[name] = h
        return h

    def ps(self, name, shape, dt=F32):
        h = self.es.enter_context(self.nc.psum_tensor(name, list(shape), dt))
        self.t[name] = h
        return h

    def dram(self, name, shape, dt=F32, kind="Internal"):
        h = self.nc.dram_tensor(name, list(shape), dt, kind=kind)
        self.t[name] = h
        return h

    def sem(self, name):
        s = Sem(self.es, self.nc, name)
        self.t[name] = s
        return s


def ln_chunk(k, blkmk, x_tm, out_bf, ntt, width, s_v, s_s, tmp, gain=None, bias=None):
    nc = k.nc
    inv = 1.0 / width
    with nc.Block() as blk:
        @blk.vector
        def _(vector):
            vector.tensor_mul(tmp["sq"][:, 0:ntt * width].rearrange("p (a b) -> p a b", a=ntt),
                              x_tm, x_tm)
            vector.reduce_sum(tmp["s1"][:, 0:ntt], x_tm, axis=AX.X)
            vector.drain()
            vector.reduce_sum(tmp["s2"][:, 0:ntt],
                              tmp["sq"][:, 0:ntt * width].rearrange("p (a b) -> p a b", a=ntt),
                              axis=AX.X)
            vector.tensor_scalar(out=tmp["m"][:, 0:ntt], in0=tmp["s1"][:, 0:ntt],
                                 scalar1=inv, scalar2=None, op0=ALU.mult)
            vector.drain()
            vector.tensor_mul(tmp["var"][:, 0:ntt], tmp["m"][:, 0:ntt], tmp["m"][:, 0:ntt])
            vector.drain()
            vector.scalar_tensor_tensor(out=tmp["var"][:, 0:ntt], in0=tmp["s2"][:, 0:ntt],
                                        scalar=inv, in1=tmp["var"][:, 0:ntt],
                                        op0=ALU.mult, op1=ALU.subtract)
            s_v.inc(vector.drain(), 1)

        @blk.scalar
        def _(scalar):
            scalar.wait_ge(s_v.h, s_v.n)
            scalar.activation(tmp["rstd"][:, 0:ntt], tmp["var"][:, 0:ntt], AF.Ln, bias=k.t["epsT"][:, 0:1])
            scalar.drain()
            scalar.activation(tmp["rstd"][:, 0:ntt], tmp["rstd"][:, 0:ntt], AF.Exp, scale=-0.5)
            s_s.inc(scalar.drain(), 1)

    with nc.Block() as blk2:
        @blk2.vector
        def _(vector):
            vector.wait_ge(s_s.h, s_s.n)
            for tt in range(ntt):
                dst = out_bf[:, tt, :] if gain is None else tmp["xn"][:, tt, :]
                vector.tensor_scalar(out=dst, in0=x_tm[:, tt, :],
                                     scalar1=tmp["m"][:, tt:tt + 1],
                                     scalar2=tmp["rstd"][:, tt:tt + 1],
                                     op0=ALU.subtract, op1=ALU.mult)
            if gain is not None:
                vector.drain()
                for tt in range(ntt):
                    vector.tensor_mul(tmp["xn"][:, tt, :], tmp["xn"][:, tt, :],
                                      gain)
                vector.drain()
                for tt in range(ntt):
                    vector.tensor_add(out_bf[:, tt, :], tmp["xn"][:, tt, :], bias)
            s_v.inc(vector.drain(), 1)


def transpose_tm_to_fm(k, in_bf, out_bf, ntt, nd, psT, ident, s_pe, s_s):
    nc = k.nc
    with nc.Block() as blk:
        @blk.tensor
        def _(tensor):
            tensor.wait_ge(s_s.h, s_s.n)
            for tt in range(ntt):
                for dj in range(nd):
                    ins = tensor.transpose(psT[:, dj * 128:(dj + 1) * 128],
                                           in_bf[:, tt, dj * 128:(dj + 1) * 128],
                                           ident[:])
                s_pe.inc(ins, 1)

        @blk.scalar
        def _(scalar):
            base = s_pe.n - ntt
            for tt in range(ntt):
                scalar.wait_ge(s_pe.h, base + tt + 1)
                scalar.copy(out_bf[:, :, tt * 128:(tt + 1) * 128],
                            psT[:, 0:nd * 128].rearrange("p (a b) -> p a b", a=nd))
                s_s.inc(scalar.drain(), 1)


def build_ln_prepass(k, x_in, xlnT_dram, ident, gain=None, bias=None, nch=NCHM):
    nc = k.nc
    s_ld = k.t["s_ld"]; s_v = k.t["s_v"]; s_s = k.t["s_s"]; s_pe = k.t["s_pe"]; s_st = k.t["s_st"]
    xc = k.sb("lp_xc", [128, 2, D], F32)
    xnbf = k.sb("lp_xnbf", [128, 2, D], BF16)
    xT = k.sb("lp_xT", [128, GD, 256], BF16)
    tmp = {"m": k.sb("lp_m", [128, 2]), "s1": k.sb("lp_s1", [128, 2]),
           "s2": k.sb("lp_s2", [128, 2]), "var": k.sb("lp_var", [128, 2]),
           "rstd": k.sb("lp_rstd", [128, 2]), "sq": k.sb("lp_sq", [128, 2 * D]),
           "xn": None}
    psT = k.t["psT"]
    for c in range(nch):
        with nc.Block() as blk:
            @blk.sync
            def _(sync, c=c):
                sync.wait_ge(s_st.h, s_st.n)
                s_ld.inc(sync.dma_start(
                    out=xc[:], in_=x_in[c * RM:(c + 1) * RM, :]
                    .rearrange("(tt p) d -> p tt d", p=128)), 16)

            @blk.vector
            def _(vector, c=c):
                vector.wait_ge(s_ld.h, s_ld.n)
        ln_chunk(k, None, xc[:], xnbf[:], 2, D, s_v, s_s, tmp)
        with nc.Block() as blk2:
            @blk2.tensor
            def _(tensor, c=c):
                tensor.wait_ge(s_v.h, s_v.n)
                tensor.wait_ge(s_s.h, s_s.n)
                for tt in range(2):
                    for dj in range(GD):
                        ins = tensor.transpose(
                            psT[:, (tt % 2) * 1024 + dj * 128:(tt % 2) * 1024 + (dj + 1) * 128],
                            xnbf[:, tt, dj * 128:(dj + 1) * 128], ident[:])
                    s_pe.inc(ins, 1)

            @blk2.scalar
            def _(scalar, c=c):
                base = s_pe.n - 2
                for tt in range(2):
                    scalar.wait_ge(s_pe.h, base + tt + 1)
                    scalar.copy(xT[:, :, tt * 128:(tt + 1) * 128],
                                psT[:, (tt % 2) * 1024:(tt % 2) * 1024 + GD * 128]
                                .rearrange("p (a b) -> p a b", a=GD))
                    s_s.inc(scalar.drain(), 1)

            @blk2.gpsimd
            def _(gpsimd, c=c):
                gpsimd.wait_ge(s_s.h, s_s.n)
                s_st.inc(gpsimd.dma_start(
                    out=xlnT_dram.ap()[:, :, c * RM:(c + 1) * RM]
                    .rearrange("a p t -> p a t"), in_=xT[:]), 16)


def carve(buf, lo, n, shape, dt=F32):
    ap = buf[:, lo:lo + n]
    if dt == BF16:
        ap = ap.bitcast(BF16)
    if len(shape) == 1:
        return ap
    return ap.rearrange("p (a b) -> p a b", a=shape[0]) if len(shape) == 2 else ap.rearrange(
        "p (a b c) -> p a b c", a=shape[0], b=shape[1])


def build_mamba_dir(k, d, weights, xlnT_dram, x0_in, x1_dram, dbg=None, nch=NCHM):
    nc = k.nc
    fwd = (d == 'f')
    s_ld = k.t["s_ld"]; s_v = k.t["s_v"]; s_s = k.t["s_s"]; s_pe = k.t["s_pe"]
    s_st = k.t["s_st"]; s_g = k.t["s_g"]; s_gd = k.t["s_gd"]; s_bc = k.t["s_bc"]
    s_xln = k.t["s_xln"]; s_pn = [k.t["s_pn0"], k.t["s_pn1"]]
    s_rep = [k.t["s_rep0"], k.t["s_rep1"]]; s_wp = k.t["s_wp"]; s_x0 = k.t["s_x0"]
    psT = k.t["psT"]
    pb = [k.t[f"pb{i}"] for i in range(4)]
    ident = k.t["ident_sb"]

    xlnTc = k.t["mb_xlnTc"]; winp = [k.t["mb_winp0"], k.t["mb_winp1"]]
    upT = k.t["mb_upT"]; tail3 = k.t["mb_tail3"]; zT = k.t["mb_zT"]
    ucsT = k.t["mb_ucsT"]; dtT = k.t["mb_dtT"]
    xdblo = k.t["mb_xdblo"]
    dtpre = k.t["mb_dtpre"]
    wdtT = k.t["mb_wdtT"]
    sp_t1 = k.t["mb_spt1"]; sp_t2 = k.t["mb_spt2"]
    hcar = k.t["mb_hcar"]; ysT = k.t["mb_ysT"]; ypart = k.t["mb_ypart"]
    scr = k.t["mb_scr"]
    convw = k.t["mb_convw"]; convb = k.t["mb_convb"]; Dp = k.t["mb_Dp"]
    bdt = k.t["mb_bdt"]; A_sb = k.t["mb_A"]
    gR = k.t["mb_gR"]; bR = k.t["mb_bR"]; bfR = k.t["mb_bfR"]
    cc1 = k.t["mb_cc1"]
    yfbf = k.t["mb_yfbf"]
    lntmp = {"m": k.t["mb_lm"], "s1": k.t["mb_ls1"], "s2": k.t["mb_ls2"],
             "var": k.t["mb_lvar"], "rstd": k.t["mb_lrstd"],
             "sq": carve(k.t["mb_scr"], 12288, 2048, (2048,)),
             "xn": carve(k.t["mb_scr"], 14336, 2048, (2, D))}
    bc_scr = k.t["bc_scr"]

    dA = [carve(scr, 0, 2048, (QN, RM)), carve(scr, 10240, 2048, (QN, RM))]
    wB = carve(scr, 2048, 2048, (QN, RM))
    hh = carve(scr, 4096, 2048, (QN, RM))
    Brep = [carve(scr, 6144, 2048, (QN, RM)), carve(scr, 12288, 2048, (QN, RM))]
    Crep = [carve(scr, 8192, 2048, (QN, RM)), carve(scr, 14336, 2048, (QN, RM))]
    mo = carve(scr, 0, 2048, (2, D))
    x0c = carve(scr, 2048, 2048, (2, D))
    woutp = carve(scr, 4096, 4096, (GI, 512), BF16)
    wfusep = carve(scr, 8192, 2048, (GD, 512), BF16)
    molnbf = carve(scr, 10240, 1024, (2, D), BF16)
    folnT = carve(scr, 11264, 1024, (GD, RM), BF16)
    ucbf = carve(scr, 12288, 2048, (GI, RM), BF16)
    convtmp = carve(scr, 0, 4096, (GI, RM))
    gtmp = carve(scr, 4096, 4096, (GI, RM))

    WinT = weights[f"WinT_{d}"]
    WoutT = weights[f"WoutT_{d}"]; WfuseT = weights["WfuseT"]
    WxT = k.t["mb_wxT"]

    cord = list(range(nch)) if fwd else list(range(NCHM - 1, NCHM - 1 - nch, -1))
    pnl_done = []
    rep_done = [0, 0, 0, 0]
    wp_done = [0, 0]
    pad0 = 3 if fwd else 0

    for ci, c in enumerate(cord):
        t0 = c * RM
        with nc.Block() as blk:
            @blk.sync
            def _(sync, c=c, ci=ci):
                sync.wait_ge(s_s.h, s_s.n)
                sync.wait_ge(s_gd.h, s_gd.n)
                sync.wait_ge(s_st.h, s_st.n)
                s_xln.inc(sync.dma_start(
                    out=xlnTc[:], in_=xlnT_dram.ap()[:, :, t0:t0 + RM]
                    .rearrange("a p t -> p a t")), 16)
                pnl_done.clear()
                pe_base = s_pe.n
                for et in range(32):
                    if et >= 2:
                        sync.wait_ge(s_pe.h, pe_base + et - 1)
                    s_pn[et % 2].inc(sync.dma_start(
                        out=winp[et % 2][:], in_=WinT.ap()[:, :, et * 128:(et + 1) * 128]
                        .rearrange("a p t -> p a t")), 16)
                    pnl_done.append(s_pn[et % 2].n)

            @blk.tensor
            def _(tensor, c=c):
                s_base = s_s.n
                tensor.wait_ge(s_xln.h, s_xln.n)
                for et in range(32):
                    tensor.wait_ge(s_pn[et % 2].h, pnl_done[et])
                    if et >= 2:
                        tensor.wait_ge(s_s.h, s_base + et - 1)
                    for kd in range(GD):
                        ins = tensor.matmul(pb[et % 2][0:128, 0:RM],
                                            winp[et % 2][:, kd, :],
                                            xlnTc[:, kd, :],
                                            start=(kd == 0), stop=(kd == GD - 1))
                    s_pe.inc(ins, 1)

            @blk.scalar
            def _(scalar, c=c, ci=ci):
                pe_base = s_pe.n - 32
                scalar.copy(upT[:, :, (0 if fwd else RM):(3 if fwd else RM + 3)][:, 0:GI, :],
                            tail3[:])
                for et in range(32):
                    scalar.wait_ge(s_pe.h, pe_base + et + 1)
                    if et < GI:
                        scalar.copy(upT[:, et, pad0:pad0 + RM], pb[et % 2][0:128, 0:RM])
                    else:
                        scalar.copy(zT[:, et - GI, :], pb[et % 2][0:128, 0:RM])
                    s_s.inc(scalar.drain(), 1)
                if fwd:
                    scalar.copy(tail3[:], upT[:, :, RM:RM + 3])
                else:
                    scalar.copy(tail3[:], upT[:, :, 0:3])
                s_s.inc(scalar.drain(), 1)

        if dbg is not None and "dbg_u_" + d in dbg:
            with nc.Block() as blkd:
                @blkd.gpsimd
                def _(gpsimd, c=c, t0=t0):
                    gpsimd.wait_ge(s_s.h, s_s.n)
                    s_gd.inc(gpsimd.dma_start(
                        out=dbg[f"dbg_u_{d}"].ap()[:, :, t0:t0 + RM].rearrange("a p t -> p a t"),
                        in_=upT[:, :, pad0:pad0 + RM]), 16)
                    s_gd.inc(gpsimd.dma_start(
                        out=dbg[f"dbg_z_{d}"].ap()[:, :, t0:t0 + RM].rearrange("a p t -> p a t"),
                        in_=zT[:]), 16)

        with nc.Block() as blk:
            @blk.gpsimd
            def _(gpsimd, c=c):
                gpsimd.wait_ge(s_s.h, s_s.n)
                ks = [2, 3]
                sel = (lambda kk: kk) if fwd else (lambda kk: 3 - kk)
                gpsimd.tensor_tensor(
                    ucsT[:], upT[:, :, 2:2 + RM],
                    convw[:, :, sel(2):sel(2) + 1].broadcast_to([128, GI, RM]), ALU.mult)
                gpsimd.drain()
                gpsimd.tensor_tensor(
                    gtmp, upT[:, :, 3:3 + RM],
                    convw[:, :, sel(3):sel(3) + 1].broadcast_to([128, GI, RM]), ALU.mult)
                gpsimd.drain()
                gpsimd.tensor_add(ucsT[:], ucsT[:], gtmp)
                s_g.inc(gpsimd.drain(), 1)

            @blk.vector
            def _(vector, c=c):
                vector.wait_ge(s_s.h, s_s.n)
                sel = (lambda kk: kk) if fwd else (lambda kk: 3 - kk)
                vector.tensor_tensor(
                    cc1[:], upT[:, :, 0:RM],
                    convw[:, :, sel(0):sel(0) + 1].broadcast_to([128, GI, RM]), ALU.mult)
                vector.drain()
                vector.tensor_tensor(
                    convtmp, upT[:, :, 1:1 + RM],
                    convw[:, :, sel(1):sel(1) + 1].broadcast_to([128, GI, RM]), ALU.mult)
                vector.drain()
                vector.tensor_add(cc1[:], cc1[:], convtmp)
                vector.wait_ge(s_g.h, s_g.n)
                vector.drain()
                vector.tensor_add(cc1[:], cc1[:], ucsT[:])
                vector.drain()
                vector.tensor_tensor(cc1[:], cc1[:],
                                     convb[:].unsqueeze(2).broadcast_to([128, GI, RM]),
                                     ALU.add)
                s_v.inc(vector.drain(), 1)

            @blk.scalar
            def _(scalar, c=c):
                scalar.wait_ge(s_v.h, s_v.n)
                scalar.activation(ucsT[:], cc1[:], AF.Silu)
                s_s.inc(scalar.drain(), 1)

        with nc.Block() as blk:
            @blk.vector
            def _(vector, c=c):
                vector.wait_ge(s_s.h, s_s.n)
                vector.tensor_copy(ucbf[:], ucsT[:])
                s_v.inc(vector.drain(), 1)

            @blk.tensor
            def _(tensor, c=c):
                tensor.wait_ge(s_v.h, s_v.n)
                for kg in range(GI):
                    ins = tensor.matmul(pb[2][0:128, 0:RM], WxT[:, kg, :], ucbf[:, kg, :],
                                        start=(kg == 0), stop=(kg == GI - 1))
                s_pe.inc(ins, 1)

            @blk.scalar
            def _(scalar, c=c):
                scalar.wait_ge(s_pe.h, s_pe.n)
                scalar.copy(dtpre[0:64, :], pb[2][0:64, 0:RM])
                scalar.copy(xdblo[64:128, :], pb[2][64:128, 0:RM])
                s_s.inc(scalar.drain(), 1)

            @blk.gpsimd
            def _(gpsimd, c=c):
                gpsimd.wait_ge(s_s.h, s_s.n)
                s_gd.inc(gpsimd.dma_start(out=bc_scr.ap()[:, :], in_=xdblo[64:128, :]), 16)

        with nc.Block() as blk:
            @blk.tensor
            def _(tensor, c=c):
                s_base = s_s.n
                for g in range(GI):
                    if g >= 2:
                        tensor.wait_ge(s_s.h, s_base + g - 1)
                    ins = tensor.matmul(pb[2 + g % 2][0:128, 0:RM],
                                        wdtT[0:64, g * 128:(g + 1) * 128],
                                        dtpre[0:64, :], start=True, stop=True)
                    s_pe.inc(ins, 1)

            @blk.scalar
            def _(scalar, c=c):
                pe_base = s_pe.n - GI
                v_base3 = s_v.n
                for g in range(GI):
                    scalar.wait_ge(s_pe.h, pe_base + g + 1)
                    if g >= 1:
                        scalar.wait_ge(s_v.h, v_base3 + g)
                    ps = pb[2 + g % 2][0:128, 0:RM]
                    scalar.activation(sp_t1[:], ps, AF.Abs, bias=bdt[:, g:g + 1])
                    scalar.drain()
                    scalar.activation(sp_t1[:], sp_t1[:], AF.Exp, scale=-1.0)
                    scalar.drain()
                    scalar.activation(sp_t1[:], sp_t1[:], AF.Ln, bias=1.0)
                    scalar.drain()
                    scalar.activation(sp_t2[:], ps, AF.Relu, bias=bdt[:, g:g + 1])
                    s_s.inc(scalar.drain(), 1)

            @blk.vector
            def _(vector, c=c):
                s_base = s_s.n - GI
                for g in range(GI):
                    vector.wait_ge(s_s.h, s_base + g + 1)
                    vector.tensor_add(dtT[:, g, :], sp_t1[:], sp_t2[:])
                    if g < GI - 1:
                        s_v.inc(vector.drain(), 1)
                vector.wait_ge(s_gd.h, s_gd.n)
                vector.drain()
                vector.tensor_mul(upT[:, :, 0:RM], dtT[:], ucsT[:])
                s_v.inc(vector.drain(), 1)

        wb_wT = upT[:, :, 0:RM]

        g_base = s_g.n
        s_base = s_s.n
        v_base = s_v.n
        ld_base = s_ld.n
        gd_base = s_gd.n
        with nc.Block() as blk:
            @blk.sync
            def _(sync, c=c):
                sync.wait_ge(s_gd.h, gd_base)
                for q in range(NQ):
                    if q >= 2:
                        sync.wait_ge(s_v.h, v_base + (q - 1) * GI)
                    s_rep[q % 2].inc(sync.dma_start(
                        out=Brep[q % 2][:],
                        in_=bc_scr.ap()[q * QN:(q + 1) * QN, :]
                        .unsqueeze(0).broadcast_to([128, QN, RM])), 16)
                    s_rep[q % 2].inc(sync.dma_start(
                        out=Crep[q % 2][:],
                        in_=bc_scr.ap()[NS + q * QN:NS + (q + 1) * QN, :]
                        .unsqueeze(0).broadcast_to([128, QN, RM])), 16)
                    rep_done[q] = s_rep[q % 2].n

            @blk.gpsimd
            def _(gpsimd, c=c):
                for q in range(NQ):
                    for g in range(GI):
                        u = q * GI + g
                        if u >= 2:
                            gpsimd.wait_ge(s_v.h, v_base + u - 1)
                        gpsimd.tensor_tensor(
                            dA[u % 2][:],
                            A_sb[:, g, q * QN:(q + 1) * QN].unsqueeze(2).broadcast_to([128, QN, RM]),
                            dtT[:, g, :].unsqueeze(1).broadcast_to([128, QN, RM]),
                            ALU.mult)
                        s_g.inc(gpsimd.drain(), 1)

            @blk.scalar
            def _(scalar, c=c):
                for q in range(NQ):
                    for g in range(GI):
                        u = q * GI + g
                        scalar.wait_ge(s_g.h, g_base + u + 1)
                        scalar.activation(dA[u % 2][:], dA[u % 2][:], AF.Exp)
                        s_s.inc(scalar.drain(), 1)

            @blk.vector
            def _(vector, c=c):
                for q in range(NQ):
                    vector.wait_ge(s_rep[q % 2].h, rep_done[q])
                    for g in range(GI):
                        u = q * GI + g
                        vector.wait_ge(s_s.h, s_base + u + 1)
                        vector.tensor_tensor(
                            wB[:], wb_wT[:, g, :].unsqueeze(1).broadcast_to([128, QN, RM]),
                            Brep[q % 2][:], ALU.mult)
                        vector.drain()
                        for j in range(QN):
                            n_ = q * QN + j
                            if fwd:
                                vector.tensor_tensor_scan(
                                    out=hh[:, j, :], data0=dA[u % 2][:, j, :],
                                    data1=wB[:, j, :], initial=hcar[:, g, n_:n_ + 1],
                                    op0=ALU.mult, op1=ALU.add)
                            else:
                                vector.tensor_tensor_scan(
                                    out=hh[:, j, ::-1], data0=dA[u % 2][:, j, ::-1],
                                    data1=wB[:, j, ::-1], initial=hcar[:, g, n_:n_ + 1],
                                    op0=ALU.mult, op1=ALU.add)
                        vector.drain()
                        edge = RM - 1 if fwd else 0
                        vector.tensor_copy(hcar[:, g, q * QN:(q + 1) * QN],
                                           hh[:, :, edge:edge + 1].rearrange("p a b -> p (a b)"))
                        vector.tensor_tensor(wB[:], hh[:], Crep[q % 2][:], ALU.mult)
                        vector.drain()
                        if q == 0:
                            vector.reduce_sum(ysT[:, g, :], wB[:].transpose([0, 2, 1]), axis=AX.X)
                        else:
                            vector.reduce_sum(ypart[:], wB[:].transpose([0, 2, 1]), axis=AX.X)
                            vector.drain()
                            vector.tensor_add(ysT[:, g, :], ysT[:, g, :], ypart[:])
                        s_v.inc(vector.drain(), 1)

        with nc.Block() as blk:
            @blk.scalar
            def _(scalar, c=c):
                scalar.wait_ge(s_v.h, s_v.n)
                scalar.activation(zT[:], zT[:], AF.Silu)
                s_s.inc(scalar.drain(), 1)

            @blk.vector
            def _(vector, c=c):
                vector.wait_ge(s_s.h, s_s.n)
                vector.tensor_tensor(cc1[:], ucsT[:],
                                     Dp[:].unsqueeze(2).broadcast_to([128, GI, RM]), ALU.mult)
                vector.drain()
                vector.tensor_add(cc1[:], cc1[:], ysT[:])
                vector.drain()
                vector.tensor_tensor(yfbf[:], cc1[:], zT[:], ALU.mult)
                s_v.inc(vector.drain(), 1)

        with nc.Block() as blk:
            @blk.sync
            def _(sync, c=c):
                sync.wait_ge(s_v.h, s_v.n)
                s_x0.inc(sync.dma_start(out=x0c, in_=x0_in[t0:t0 + RM, :]
                                        .rearrange("(tt p) d -> p tt d", p=128)), 16)

            @blk.tensor
            def _(tensor, c=c):
                pass

        with nc.Block() as blk:
            @blk.sync
            def _(sync, c=c):
                for ech in range(2):
                    if ech == 1:
                        sync.wait_ge(s_pe.h, s_pe.n + 2)
                    s_wp.inc(sync.dma_start(
                        out=woutp, in_=WoutT.ap()[:, :, ech * 512:(ech + 1) * 512]
                        .rearrange("a p t -> p a t")), 16)
                    wp_done[ech] = s_wp.n

            @blk.tensor
            def _(tensor, c=c):
                for ech in range(2):
                    tensor.wait_ge(s_wp.h, wp_done[ech])
                    for tsub in range(2):
                        for kg in range(GI):
                            ins = tensor.matmul(pb[ech * 2 + tsub][0:128, 0:512],
                                                yfbf[:, kg, tsub * 128:(tsub + 1) * 128],
                                                woutp[:, kg, :],
                                                start=(kg == 0), stop=(kg == GI - 1))
                        s_pe.inc(ins, 1)

            @blk.scalar
            def _(scalar, c=c):
                pe_base = s_pe.n - 4
                i = 0
                for ech in range(2):
                    for tsub in range(2):
                        i += 1
                        scalar.wait_ge(s_pe.h, pe_base + i)
                        scalar.copy(mo[:, tsub, ech * 512:(ech + 1) * 512],
                                    pb[ech * 2 + tsub][0:128, 0:512])
                        s_s.inc(scalar.drain(), 1)

        ln_chunk(k, None, mo, molnbf, 2, D, s_v, s_s, lntmp, gain=gR[:], bias=bR[:])
        with nc.Block() as blk:
            @blk.tensor
            def _(tensor, c=c):
                tensor.wait_ge(s_v.h, s_v.n)
                for tt in range(2):
                    for dj in range(GD):
                        ins = tensor.transpose(
                            psT[:, (tt % 2) * 1024 + dj * 128:(tt % 2) * 1024 + (dj + 1) * 128],
                            molnbf[:, tt, dj * 128:(dj + 1) * 128], ident[:])
                    s_pe.inc(ins, 1)

            @blk.scalar
            def _(scalar, c=c):
                base = s_pe.n - 2
                for tt in range(2):
                    scalar.wait_ge(s_pe.h, base + tt + 1)
                    scalar.copy(folnT[:, :, tt * 128:(tt + 1) * 128],
                                psT[:, (tt % 2) * 1024:(tt % 2) * 1024 + GD * 128]
                                .rearrange("p (a b) -> p a b", a=GD))
                    s_s.inc(scalar.drain(), 1)

        with nc.Block() as blk:
            @blk.sync
            def _(sync, c=c):
                for dch in range(2):
                    if dch == 1:
                        sync.wait_ge(s_pe.h, s_pe.n + 2)
                    s_wp.inc(sync.dma_start(
                        out=wfusep, in_=WfuseT.ap()[(0 if fwd else GD):(GD if fwd else 2 * GD),
                                                    :, dch * 512:(dch + 1) * 512]
                        .rearrange("a p t -> p a t")), 16)
                    wp_done[dch] = s_wp.n

            @blk.tensor
            def _(tensor, c=c):
                for dch in range(2):
                    tensor.wait_ge(s_wp.h, wp_done[dch])
                    for tsub in range(2):
                        for ke in range(GD):
                            ins = tensor.matmul(pb[dch * 2 + tsub][0:128, 0:512],
                                                folnT[:, ke, tsub * 128:(tsub + 1) * 128],
                                                wfusep[:, ke, :],
                                                start=(ke == 0), stop=(ke == GD - 1))
                        s_pe.inc(ins, 1)

            @blk.vector
            def _(vector, c=c):
                pe_base = s_pe.n - 4
                vector.wait_ge(s_x0.h, s_x0.n)
                i = 0
                for dch in range(2):
                    for tsub in range(2):
                        i += 1
                        vector.wait_ge(s_pe.h, pe_base + i)
                        vector.tensor_add(x0c[:, tsub, dch * 512:(dch + 1) * 512],
                                          x0c[:, tsub, dch * 512:(dch + 1) * 512],
                                          pb[dch * 2 + tsub][0:128, 0:512])
                if not fwd:
                    vector.drain()
                    for tsub in range(2):
                        vector.tensor_add(x0c[:, tsub, :], x0c[:, tsub, :], bfR[:])
                s_v.inc(vector.drain(), 1)

            @blk.gpsimd
            def _(gpsimd, c=c):
                gpsimd.wait_ge(s_v.h, s_v.n)
                s_gd.inc(gpsimd.dma_start(
                    out=x1_dram.ap()[t0:t0 + RM, :].rearrange("(tt p) d -> p tt d", p=128),
                    in_=x0c), 16)
                if dbg is not None:
                    s_gd.inc(gpsimd.dma_start(
                        out=dbg[f"dbg_uc_{d}"].ap()[:, :, t0:t0 + RM].rearrange("a p t -> p a t"),
                        in_=ucsT[:]), 16)
                    s_gd.inc(gpsimd.dma_start(
                        out=dbg[f"dbg_dt_{d}"].ap()[:, :, t0:t0 + RM].rearrange("a p t -> p a t"),
                        in_=dtT[:]), 16)
                    s_gd.inc(gpsimd.dma_start(
                        out=dbg[f"dbg_ys_{d}"].ap()[:, :, t0:t0 + RM].rearrange("a p t -> p a t"),
                        in_=ysT[:]), 16)

    with nc.Block() as blk:
        @blk.sync
        def _(sync):
            sync.wait_ge(s_gd.h, s_gd.n)


def build_weight_prep(k, casts, trans, ident):
    nc = k.nc
    s_ld = k.t["s_ld"]; s_v = k.t["s_v"]; s_s = k.t["s_s"]; s_pe = k.t["s_pe"]; s_st = k.t["s_st"]
    psT = k.t["psT"]
    tf = k.t["wp_f32"]; tb = k.t["wp_bf"]; to = k.t["wp_out"]
    ident = ident

    for (srch, dsth, rows, cols) in casts:
        for i in range(rows // 128):
            with nc.Block() as blk:
                @blk.sync
                def _(sync, i=i, srch=srch, cols=cols):
                    sync.wait_ge(s_v.h, s_v.n)
                    s_ld.inc(sync.dma_start(out=tf[:, 0:cols],
                                            in_=srch.ap()[i * 128:(i + 1) * 128, :]), 16)

                @blk.vector
                def _(vector, cols=cols):
                    vector.wait_ge(s_ld.h, s_ld.n)
                    vector.wait_ge(s_st.h, s_st.n)
                    vector.tensor_copy(tb[:, 0:cols], tf[:, 0:cols])
                    s_v.inc(vector.drain(), 1)

                @blk.gpsimd
                def _(gpsimd, i=i, dsth=dsth, cols=cols):
                    gpsimd.wait_ge(s_v.h, s_v.n)
                    s_st.inc(gpsimd.dma_start(out=dsth.ap()[i, :, 0:cols],
                                              in_=tb[:, 0:cols]), 16)

    for (srch, dsth, rows, cols) in trans:
        ncol = cols // 128
        for i in range(rows // 128):
            with nc.Block() as blk:
                @blk.sync
                def _(sync, i=i, srch=srch, cols=cols):
                    sync.wait_ge(s_v.h, s_v.n)
                    s_ld.inc(sync.dma_start(out=tf[:, 0:cols],
                                            in_=srch.ap()[i * 128:(i + 1) * 128, :]), 16)

                @blk.vector
                def _(vector, cols=cols):
                    vector.wait_ge(s_ld.h, s_ld.n)
                    vector.wait_ge(s_pe.h, s_pe.n)
                    vector.wait_ge(s_st.h, s_st.n)
                    vector.tensor_copy(tb[:, 0:cols], tf[:, 0:cols])
                    s_v.inc(vector.drain(), 1)

                @blk.tensor
                def _(tensor, ncol=ncol):
                    tensor.wait_ge(s_v.h, s_v.n)
                    s_base_t = s_s.n
                    for j in range(ncol):
                        if j >= 2:
                            tensor.wait_ge(s_s.h, s_base_t + j - 1)
                        ins = tensor.transpose(
                            psT[:, (j % 2) * 1024:(j % 2) * 1024 + 128],
                            tb[:, j * 128:(j + 1) * 128], ident[:])
                        s_pe.inc(ins, 1)

                @blk.scalar
                def _(scalar, ncol=ncol):
                    pe_base = s_pe.n - ncol
                    scalar.wait_ge(s_st.h, s_st.n)
                    for j in range(ncol):
                        scalar.wait_ge(s_pe.h, pe_base + j + 1)
                        scalar.copy(to[:, j * 128:(j + 1) * 128],
                                    psT[:, (j % 2) * 1024:(j % 2) * 1024 + 128])
                        s_s.inc(scalar.drain(), 1)

                @blk.gpsimd
                def _(gpsimd, i=i, dsth=dsth, ncol=ncol):
                    gpsimd.wait_ge(s_s.h, s_s.n)
                    s_st.inc(gpsimd.dma_start(
                        out=dsth.ap()[:, :, i * 128:(i + 1) * 128]
                        .rearrange("a p t -> p a t"),
                        in_=to[:, 0:ncol * 128].rearrange("p (a b) -> p a b", a=ncol)), 16)


def build_prep_wdt(k, srch, dsth, ident):
    nc = k.nc
    s_ld = k.t["s_ld"]; s_v = k.t["s_v"]; s_s = k.t["s_s"]; s_pe = k.t["s_pe"]; s_st = k.t["s_st"]
    psT = k.t["psT"]
    tf = k.t["wp_f32"]; tb = k.t["wp_bf"]; to = k.t["wp_out"]
    for i in range(16):
        with nc.Block() as blk:
            @blk.sync
            def _(sync, i=i):
                sync.wait_ge(s_v.h, s_v.n)
                s_ld.inc(sync.dma_start(out=tf[:, 0:64],
                                        in_=srch.ap()[i * 128:(i + 1) * 128, :]), 16)

            @blk.vector
            def _(vector):
                vector.wait_ge(s_ld.h, s_ld.n)
                vector.wait_ge(s_pe.h, s_pe.n)
                vector.tensor_copy(tb[:, 0:64], tf[:, 0:64])
                s_v.inc(vector.drain(), 1)

            @blk.tensor
            def _(tensor):
                tensor.wait_ge(s_v.h, s_v.n)
                tensor.wait_ge(s_s.h, s_s.n)
                s_pe.inc(tensor.transpose(psT[0:64, 0:128], tb[:, 0:64], k.t["ident_sb"][:]), 1)

            @blk.scalar
            def _(scalar):
                scalar.wait_ge(s_pe.h, s_pe.n)
                scalar.wait_ge(s_st.h, s_st.n)
                scalar.copy(to[0:64, 0:128], psT[0:64, 0:128])
                s_s.inc(scalar.drain(), 1)

            @blk.gpsimd
            def _(gpsimd, i=i):
                gpsimd.wait_ge(s_s.h, s_s.n)
                s_st.inc(gpsimd.dma_start(out=dsth.ap()[0:64, i * 128:(i + 1) * 128],
                                          in_=to[0:64, 0:128]), 16)


def load_pg_vec(k, eng, srch, dst_sb, sem):
    for g in range(16):
        sem.inc(eng.dma_start(out=dst_sb[:, g:g + 1],
                              in_=srch.ap()[g * 128:(g + 1) * 128].unsqueeze(1)), 16)


def build_mamba_dir_setup(k, d, ins):
    nc = k.nc
    s_ld = k.t["s_ld"]; s_v = k.t["s_v"]; s_s = k.t["s_s"]
    with nc.Block() as blk:
        @blk.sync
        def _(sync):
            sync.wait_ge(s_v.h, s_v.n)
            sync.wait_ge(s_s.h, s_s.n)
            s_ld.inc(sync.dma_start(
                out=k.t["mb_A"][:],
                in_=ins[f"L0_attn_a_Alog_{d}"].ap().rearrange("(g p) n -> p g n", p=128)), 16)
            s_ld.inc(sync.dma_start(
                out=k.t["mb_convw"][:],
                in_=ins[f"L0_attn_a_convw_{d}"].ap().rearrange("(g p) c -> p g c", p=128)), 16)
            load_pg_vec(k, sync, ins[f"L0_attn_a_convb_{d}"], k.t["mb_convb"], s_ld)
            load_pg_vec(k, sync, ins[f"L0_attn_a_Dp_{d}"], k.t["mb_Dp"], s_ld)
            load_pg_vec(k, sync, ins[f"L0_attn_a_bdt_{d}"], k.t["mb_bdt"], s_ld)
            s_ld.inc(sync.dma_start(
                out=k.t["mb_gR"][:],
                in_=ins[f"L0_attn_a_ln_g_{d}"].ap().unsqueeze(0).broadcast_to([128, D])), 16)
            s_ld.inc(sync.dma_start(
                out=k.t["mb_bR"][:],
                in_=ins[f"L0_attn_a_ln_b_{d}"].ap().unsqueeze(0).broadcast_to([128, D])), 16)
            if d == 'b':
                s_ld.inc(sync.dma_start(
                    out=k.t["mb_bfR"][:],
                    in_=ins["L0_attn_a_bfuse"].ap().unsqueeze(0).broadcast_to([128, D])), 16)
            s_ld.inc(sync.dma_start(
                out=k.t["mb_wdtT"][0:64, :],
                in_=k.t[f"WdtT_{d}"].ap()[:, :]), 16)
            s_ld.inc(sync.dma_start(
                out=k.t["mb_wxT"][:],
                in_=k.t[f"WxT_{d}"].ap().rearrange("a p b -> p a b")), 16)

        @blk.vector
        def _(vector):
            vector.wait_ge(s_ld.h, s_ld.n)
            vector.memset(k.t["mb_tail3"][:], 0.0)
            vector.memset(k.t["mb_hcar"][:], 0.0)
            s_v.inc(vector.drain(), 1)

        @blk.scalar
        def _(scalar):
            scalar.wait_ge(s_ld.h, s_ld.n)
            scalar.activation(k.t["mb_A"][:], k.t["mb_A"][:], AF.Exp)
            scalar.drain()
            scalar.mul(k.t["mb_A"][:], k.t["mb_A"][:], -1.0)
            s_s.inc(scalar.drain(), 1)


def alloc_mamba(k):
    k.sb("mb_xlnTc", [128, GD, RM], BF16)
    k.sb("mb_winp0", [128, GD, 128], BF16)
    k.sb("mb_winp1", [128, GD, 128], BF16)
    k.sb("mb_upT", [128, GI, RM + 3], F32)
    k.sb("mb_tail3", [128, GI, 3], F32)
    k.sb("mb_zT", [128, GI, RM], F32)
    k.sb("mb_ucsT", [128, GI, RM], F32)
    k.sb("mb_dtT", [128, GI, RM], F32)
    k.sb("mb_xdblo", [128, RM], F32)
    k.sb("mb_dtpre", [128, RM], BF16)
    k.sb("mb_wdtT", [128, DI], BF16)
    k.sb("mb_wxT", [128, GI, 128], BF16)
    k.sb("mb_spt1", [128, RM], F32)
    k.sb("mb_spt2", [128, RM], F32)
    k.sb("mb_hcar", [128, GI, NS], F32)
    k.sb("mb_ysT", [128, GI, RM], F32)
    k.sb("mb_ypart", [128, RM], F32)
    k.sb("mb_scr", [128, 16384], F32)
    k.sb("mb_convw", [128, GI, 4], F32)
    k.sb("mb_convb", [128, GI], F32)
    k.sb("mb_Dp", [128, GI], F32)
    k.sb("mb_bdt", [128, GI], F32)
    k.sb("mb_A", [128, GI, NS], F32)
    k.sb("mb_gR", [128, D], F32)
    k.sb("mb_bR", [128, D], F32)
    k.sb("mb_bfR", [128, D], F32)
    k.sb("mb_cc1", [128, GI, RM], F32)
    k.sb("mb_yfbf", [128, GI, RM], BF16)
    k.sb("mb_lm", [128, 2]); k.sb("mb_ls1", [128, 2]); k.sb("mb_ls2", [128, 2])
    k.sb("mb_lvar", [128, 2]); k.sb("mb_lrstd", [128, 2])



MAMBA_W_NAMES = [
    "Win_f", "convw_f", "convb_f", "Wx_f", "Wdt_f", "bdt_f", "Alog_f", "Dp_f",
    "Wout_f", "ln_g_f", "ln_b_f",
    "Win_b", "convw_b", "convb_b", "Wx_b", "Wdt_b", "bdt_b", "Alog_b", "Dp_b",
    "Wout_b", "ln_g_b", "ln_b_b", "Wfuse", "bfuse",
]


def build_program_mamba(with_b=True, dbg=True, nch=NCHM):
    nc = bass.Bass("TRN2", target_bir_lowering=False)
    es = ExitStack()
    k = K(nc, es)
    ins = {}
    ins["x"] = k.dram("x", [T, D], F32, kind="ExternalInput")
    for nm in MAMBA_W_NAMES:
        full = "L0_attn_a_" + nm
        shp = {"Win": [2 * DI, D], "convw": [DI, 4], "convb": [DI], "Wx": [128, DI],
               "Wdt": [DI, 64], "bdt": [DI], "Alog": [DI, NS], "Dp": [DI],
               "Wout": [D, DI], "ln_g": [D], "ln_b": [D], "Wfuse": [D, 2 * DI // 2],
               "bfuse": [D]}[nm.rsplit("_", 1)[0] if nm not in ("Wfuse", "bfuse") else nm]
        ins[full] = k.dram(full, shp, F32, kind="ExternalInput")
    ident_in = k.dram("ident", [128, 128], BF16, kind="ExternalInput")

    k.dram("xlnT", [GD, 128, T], BF16)
    k.dram("x1", [T, D], F32, kind="ExternalOutput")
    k.dram("bc_scr", [64, RM], F32)
    for dd in "fb":
        k.dram(f"WinT_{dd}", [GD, 128, 2 * DI], BF16)
        k.dram(f"WoutT_{dd}", [GI, 128, D], BF16)
        k.dram(f"WxT_{dd}", [GI, 128, 128], BF16)
        k.dram(f"WdtT_{dd}", [64, DI], BF16)
    k.dram("WfuseT", [GI, 128, D], BF16)
    dbg_t = None
    if dbg:
        dbg_t = {}
        for dd in (("f", "b") if with_b else ("f",)):
            dbg_t[f"dbg_uc_{dd}"] = k.dram(f"dbg_uc_{dd}", [GI, 128, T], F32, kind="ExternalOutput")
            dbg_t[f"dbg_dt_{dd}"] = k.dram(f"dbg_dt_{dd}", [GI, 128, T], F32, kind="ExternalOutput")
            dbg_t[f"dbg_ys_{dd}"] = k.dram(f"dbg_ys_{dd}", [GI, 128, T], F32, kind="ExternalOutput")
            dbg_t[f"dbg_u_{dd}"] = k.dram(f"dbg_u_{dd}", [GI, 128, T], F32, kind="ExternalOutput")
            dbg_t[f"dbg_z_{dd}"] = k.dram(f"dbg_z_{dd}", [GI, 128, T], F32, kind="ExternalOutput")
        dbg_t["dbg_panel"] = k.dram("dbg_panel", [128, GD, 128], BF16, kind="ExternalOutput")
        dbg_t["dbg_xlnTc"] = k.dram("dbg_xlnTc", [128, GD, RM], BF16, kind="ExternalOutput")

    with es:
        for s in ["s_ld", "s_v", "s_s", "s_pe", "s_st", "s_g", "s_gd", "s_bc",
                  "s_xln", "s_pn0", "s_pn1", "s_rep0", "s_rep1", "s_wp", "s_x0",
                  "s_qk0", "s_qk1", "s_qp0", "s_qp1", "s_pf0", "s_pf1",
                  "s_sc0", "s_sc1"]:
            k.sem(s)
        psT = k.ps("psT", [128, 2048], BF16)
        for i in range(6):
            k.ps(f"pb{i}", [128, 512], F32)
        ident_sb = k.sb("ident_sb", [128, 128], BF16)
        k.sb("identf", [128, 128], F32)
        k.sb("epsT", [128, 1], F32)
        k.push_scope()
        k.sb("wp_f32", [128, 4096], F32)
        k.sb("wp_bf", [128, 4096], BF16)
        k.sb("wp_out", [128, 4096], BF16)

        with nc.Block() as blk:
            @blk.sync
            def _(sync):
                k.t["s_ld"].inc(sync.dma_start(out=ident_sb[:], in_=ident_in.ap()[:, :]), 16)

            @blk.vector
            def _(vector):
                vector.wait_ge(k.t["s_ld"].h, k.t["s_ld"].n)
                vector.memset(k.t["epsT"][:], EPS)
                vector.tensor_copy(k.t["identf"][:], ident_sb[:])
                k.t["s_v"].inc(vector.drain(), 1)

        dirs = ("f", "b") if with_b else ("f",)
        trans = []
        for dd in dirs:
            trans += [(ins[f"L0_attn_a_Win_{dd}"], k.t[f"WinT_{dd}"], 2 * DI, D),
                      (ins[f"L0_attn_a_Wout_{dd}"], k.t[f"WoutT_{dd}"], D, DI),
                      (ins[f"L0_attn_a_Wx_{dd}"], k.t[f"WxT_{dd}"], 128, DI)]
        trans += [(ins["L0_attn_a_Wfuse"], k.t["WfuseT"], D, DI)]
        build_weight_prep(k, [], trans, ident_sb)
        for dd in dirs:
            build_prep_wdt(k, ins[f"L0_attn_a_Wdt_{dd}"], k.t[f"WdtT_{dd}"], ident_sb)

        build_ln_prepass(k, ins["x"].ap(), k.t["xlnT"], ident_sb, nch=nch)
        k.pop_scope()

        alloc_mamba(k)
        weights = {f"WinT_{dd}": k.t[f"WinT_{dd}"] for dd in dirs}
        weights.update({f"WoutT_{dd}": k.t[f"WoutT_{dd}"] for dd in dirs})
        weights["WfuseT"] = k.t["WfuseT"]
        for dd in dirs:
            build_mamba_dir_setup(k, dd, ins)
            build_mamba_dir(k, dd, weights, k.t["xlnT"],
                            ins["x"].ap() if dd == 'f' else k.t["x1"].ap(),
                            k.t["x1"], dbg=dbg_t, nch=nch)

        with nc.Block() as blk:
            @blk.sync
            def _(sync):
                sync.wait_ge(k.t["s_gd"].h, k.t["s_gd"].n)
                sync.wait_ge(k.t["s_st"].h, k.t["s_st"].n)

    return nc


def alloc_ffn(k, tag):
    t = {}
    def sb(name, shape, dt=F32):
        t[name] = k.sb(f"{tag}_{name}", shape, dt)
        return t[name]
    sb("hT", [128, 2, GD, 512], BF16)
    sb("x1c", [128, 2, 4, D], F32)
    sb("qT", [128, 4, 512], BF16)
    sb("keysT", [128, 8, 64], BF16)
    sb("fscr", [128, 8192], F32)
    sb("sxy", [128, 4, 2, 4, 8], F32)
    sb("ixyu", [128, 4, 2, 4, 8], U32)
    sb("ixyf", [128, 4, 2, 4, 8], F32)
    sb("sc8", [128, 4, 4, 8], F32)
    sb("pku", [128, 4, 4, 8], U32)
    sb("pkf", [128, 4, 4, 8], F32)
    sb("idx8f", [128, 4, 4, 8], F32)
    sb("idxi", [128, 2, 4, 4, 8], I32)
    sb("sm", [128, 4, 4, 8], F32)
    sb("smbf", [128, 2, 4, 4, 8], BF16)
    sb("mx8", [128, 16], F32)
    sb("nm8", [128, 16], F32)
    sb("sum8", [128, 16], F32)
    sb("rcp8", [128, 16], F32)
    sb("siluWT", [128, 32, 512], BF16)
    t["gate"] = t["siluWT"]
    sb("Ssum", [128, 32, 512], BF16)
    sb("Stmp", [128, 16, 512], BF16)
    sb("pnA0", [128, GD, 128], BF16)
    sb("pnA1", [128, GD, 128], BF16)
    sb("pnB0", [128, GD, 128], BF16)
    sb("pnB1", [128, GD, 128], BF16)
    sb("pnC0", [128, 512], BF16)
    sb("pnC1", [128, 512], BF16)
    sb("etmp", [128, 512], F32)
    sb("bgc", [128, 32], F32)
    sb("buc", [128, 32], F32)
    sb("bdR", [128, D], F32)
    sb("iota64", [128, 64], F32)
    sb("tlocf", [128, 4], F32)
    sb("zbf", [128, 8], BF16)
    sb("lm", [128, 4]); sb("ls1", [128, 4]); sb("ls2", [128, 4])
    sb("lvar", [128, 4]); sb("lrstd", [128, 4])
    sb("xnbf", [128, 4, D], BF16)
    sb("pnF0", [128, GD, 128], BF16)
    sb("pnF1", [128, GD, 128], BF16)
    return t


def build_ffn_setup(k, t, ins, iotain):
    nc = k.nc
    s_ld = k.t["s_ld"]; s_v = k.t["s_v"]; s_s = k.t["s_s"]
    with nc.Block() as blk:
        @blk.sync
        def _(sync):
            sync.wait_ge(s_v.h, s_v.n)
            sync.wait_ge(s_s.h, s_s.n)
            sync.wait_ge(k.t["s_st"].h, k.t["s_st"].n)
            s_ld.inc(sync.dma_start(out=t["keysT"][0:64, :, :],
                                    in_=k.t["keysT_dram"].ap().rearrange("p h n q -> n (p h) q")), 16)
            s_ld.inc(sync.dma_start(out=t["keysT"][64:128, :, :],
                                    in_=k.t["keysT_dram"].ap().rearrange("p h n q -> n (p h) q")), 16)
            for g in range(32):
                s_ld.inc(sync.dma_start(out=t["bgc"][:, g:g + 1],
                                        in_=ins["L0_ffn_bg"].ap()[g * 128:(g + 1) * 128].unsqueeze(1)), 16)
                s_ld.inc(sync.dma_start(out=t["buc"][:, g:g + 1],
                                        in_=ins["L0_ffn_bu"].ap()[g * 128:(g + 1) * 128].unsqueeze(1)), 16)
            s_ld.inc(sync.dma_start(out=t["bdR"][:],
                                    in_=ins["L0_ffn_bd"].ap().unsqueeze(0).broadcast_to([128, D])), 16)
            s_ld.inc(sync.dma_start(out=t["iota64"][:],
                                    in_=iotain.ap()[:, 0:64]), 16)
            s_ld.inc(sync.dma_start(out=t["tlocf"][:],
                                    in_=iotain.ap()[:, 64:68]), 16)

        @blk.vector
        def _(vector):
            vector.wait_ge(s_ld.h, s_ld.n)
            vector.memset(t["zbf"][:], 0.0)
            s_v.inc(vector.drain(), 1)


def build_ffn(k, t, ins, x_in, x_out, Sh_dram, first, dbg=None):
    nc = k.nc
    s_ld = k.t["s_ld"]; s_v = k.t["s_v"]; s_s = k.t["s_s"]; s_pe = k.t["s_pe"]
    s_st = k.t["s_st"]; s_g = k.t["s_g"]; s_gd = k.t["s_gd"]
    s_xln = k.t["s_xln"]; s_pn = [k.t["s_pn0"], k.t["s_pn1"]]
    s_pf = [k.t["s_pf0"], k.t["s_pf1"]]
    s_sc = [k.t["s_sc0"], k.t["s_sc1"]]
    s_rep = [k.t["s_rep0"], k.t["s_rep1"]]; s_wp = k.t["s_wp"]; s_x0 = k.t["s_x0"]
    psT = k.t["psT"]
    pb = [k.t[f"pb{i}"] for i in range(6)]
    ident = k.t["ident_sb"]
    RF = 512
    NCF = T // RF
    lntmp = {"m": t["lm"], "s1": t["ls1"], "s2": t["ls2"], "var": t["lvar"],
             "rstd": t["lrstd"], "sq": carve(t["fscr"], 0, 4096, (4096,)), "xn": None}
    simtok = carve(t["fscr"], 0, 2048, (4, 8, 64))
    simT = carve(t["fscr"], 4096, 4096, (8, 512))
    allsc = carve(t["fscr"], 2048, 256, (4, 64))
    allix = carve(t["fscr"], 2304, 256, (4, 64))
    oh = carve(t["fscr"], 4096, 2048, (4, 8, 64))
    pnl_done = []
    wp_done = [0] * 8

    WqH = ins["L0_ffn_Wq"]; WgH = k.t["Wg_bf"]; WuH = k.t["Wu_bf"]
    WdH = k.t["Wd_bf"]; ueH = k.t["ue_bf"]; deTH = k.t["deT"]
    WqbH = k.t["Wq_bf"]

    f5_done = {}
    sc_done = {}
    be_v_done = {}
    be_pe_done = {}
    f1b_pe_done = {}

    def front(c):
        par = c % 2
        t0 = c * RF
        x1cv = t["x1c"][:, par]
        xnbv = t["xnbf"]
        hTv = t["hT"][:, par]
        with nc.Block() as blk:
            @blk.sync
            def _(sync, c=c):
                if c == 0:
                    sync.wait_ge(s_v.h, s_v.n)
                    sync.wait_ge(s_s.h, s_s.n)
                    sync.wait_ge(s_st.h, s_st.n)
                sync.wait_ge(s_gd.h, s_gd.n)
                s_x0.inc(sync.dma_start(out=x1cv[:], in_=x_in[t0:t0 + RF, :]
                                        .rearrange("(tt p) d -> p tt d", p=128)), 16)

            @blk.vector
            def _(vector, c=c):
                vector.wait_ge(s_x0.h, s_x0.n)
                if c >= 1:
                    vector.wait_ge(s_pe.h, f1b_pe_done[c - 1])
        ln_chunk(k, None, x1cv[:], xnbv[:], 4, D, s_v, s_s, lntmp)
        with nc.Block() as blk:
            @blk.tensor
            def _(tensor, c=c):
                tensor.wait_ge(s_v.h, s_v.n)
                tensor.wait_ge(s_s.h, s_s.n)
                s_base_t = s_s.n
                for tt in range(4):
                    if tt >= 2:
                        tensor.wait_ge(s_s.h, s_base_t + tt - 1)
                    for dj in range(GD):
                        ins_ = tensor.transpose(
                            psT[:, (tt % 2) * 1024 + dj * 128:(tt % 2) * 1024 + (dj + 1) * 128],
                            xnbv[:, tt, dj * 128:(dj + 1) * 128], ident[:])
                    s_pe.inc(ins_, 1)

            @blk.scalar
            def _(scalar, c=c):
                base = s_pe.n - 4
                for tt in range(4):
                    scalar.wait_ge(s_pe.h, base + tt + 1)
                    scalar.copy(hTv[:, :, tt * 128:(tt + 1) * 128],
                                psT[:, (tt % 2) * 1024:(tt % 2) * 1024 + GD * 128]
                                .rearrange("p (a b) -> p a b", a=GD))
                    s_s.inc(scalar.drain(), 1)
        f1b_pe_done[c] = s_pe.n

        with nc.Block() as blk:
            @blk.sync
            def _(sync, c=c):
                sync.wait_ge(s_s.h, s_s.n)
                pe_base0 = s_pe.n
                pnl_done.clear()
                for m in range(4):
                    if m >= 2:
                        sync.wait_ge(s_pe.h, pe_base0 + m - 1)
                    s_pf[m % 2].inc(sync.dma_start(
                        out=(t["pnF0"] if m % 2 == 0 else t["pnF1"])[:],
                        in_=WqbH.ap()[:, :, m * 128:(m + 1) * 128]
                        .rearrange("a p t -> p a t")), 16)
                    pnl_done.append(s_pf[m % 2].n)

            @blk.tensor
            def _(tensor, c=c):
                s_base = s_s.n
                for m in range(4):
                    tensor.wait_ge(s_pf[m % 2].h, pnl_done[m])
                    if m >= 2:
                        tensor.wait_ge(s_s.h, s_base + m - 1)
                    for kd in range(GD):
                        ins_ = tensor.matmul(pb[4 + m % 2][0:128, 0:RF],
                                             (t["pnF0"] if m % 2 == 0 else t["pnF1"])[:, kd, :],
                                             hTv[:, kd, :],
                                             start=(kd == 0), stop=(kd == GD - 1))
                    s_pe.inc(ins_, 1)

            @blk.scalar
            def _(scalar, c=c):
                base = s_pe.n - 4
                for m in range(4):
                    scalar.wait_ge(s_pe.h, base + m + 1)
                    scalar.copy(t["qT"][:, m, :], pb[4 + m % 2][0:128, 0:RF])
                    s_s.inc(scalar.drain(), 1)

        with nc.Block() as blk:
            @blk.tensor
            def _(tensor, c=c):
                s_base = s_s.n
                tensor.wait_ge(s_s.h, s_base)
                for ph in range(8):
                    p_, hh = ph // 4, ph % 4
                    if ph >= 2:
                        tensor.wait_ge(s_s.h, s_base + (ph - 2) + 1)
                    mtile = 2 * p_ + hh // 2
                    rbase = (hh % 2) * 64
                    ins_ = tensor.matmul(pb[4 + ph % 2][0:64, 0:RF],
                                         t["keysT"][rbase:rbase + 64, ph, :],
                                         t["qT"][rbase:rbase + 64, mtile, :],
                                         start=True, stop=True)
                    s_pe.inc(ins_, 1)

            @blk.scalar
            def _(scalar, c=c):
                base = s_pe.n - 8
                for ph in range(8):
                    scalar.wait_ge(s_pe.h, base + ph + 1)
                    if ph == 0 and c >= 1:
                        scalar.wait_ge(s_v.h, f5_done[c - 1])
                    scalar.copy(simT[0:64, ph, :], pb[4 + ph % 2][0:64, 0:RF])
                    s_s.inc(scalar.drain(), 1)

        with nc.Block() as blk:
            @blk.tensor
            def _(tensor, c=c):
                tensor.wait_ge(s_s.h, s_s.n)
                s_base_t = s_s.n
                grp = 0
                for tsub in range(4):
                    for phg in range(2):
                        if grp >= 2:
                            tensor.wait_ge(s_s.h, s_base_t + grp - 1)
                        for j in range(4):
                            ph = phg * 4 + j
                            ins_ = tensor.transpose(
                                pb[4 + grp % 2][0:128, j * 64:(j + 1) * 64],
                                simT[0:64, ph, tsub * 128:(tsub + 1) * 128],
                                k.t["identf"][0:64, 0:64])
                        s_pe.inc(ins_, 1)
                        grp += 1

            @blk.scalar
            def _(scalar, c=c):
                base = s_pe.n - 8
                grp = 0
                for tsub in range(4):
                    for phg in range(2):
                        scalar.wait_ge(s_pe.h, base + grp + 1)
                        scalar.copy(simtok[:, tsub, phg * 4:(phg + 1) * 4, :],
                                    pb[4 + grp % 2][0:128, 0:256]
                                    .rearrange("p (a b) -> p a b", a=4))
                        s_s.inc(scalar.drain(), 1)
                        grp += 1

        with nc.Block() as blk:
            @blk.vector
            def _(vector, c=c, par=par):
                vector.wait_ge(s_s.h, s_s.n)
                vector.wait_ge(s_sc[par].h, s_sc[par].n)
                for tsub in range(4):
                    for ph in range(8):
                        p_, hh = ph // 4, ph % 4
                        vector.max(t["sxy"][:, tsub, p_, hh, :],
                                   simtok[:, tsub, ph, :])
                vector.drain()
                for tsub in range(4):
                    for ph in range(8):
                        p_, hh = ph // 4, ph % 4
                        vector.max_index(t["ixyu"][:, tsub, p_, hh, :],
                                         t["sxy"][:, tsub, p_, hh, :],
                                         simtok[:, tsub, ph, :])
                vector.drain()
                vector.tensor_copy(t["ixyf"][:], t["ixyu"][:].bitcast(I32))
                vector.drain()
                s_v.inc(vector.drain(), 1)
                for tsub in range(4):
                    for hh in range(4):
                        vector.tensor_tensor(
                            allsc[:, hh, :].rearrange("p (i j) -> p i j", i=8),
                            t["sxy"][:, tsub, 0, hh, :].unsqueeze(2).broadcast_to([128, 8, 8]),
                            t["sxy"][:, tsub, 1, hh, :].unsqueeze(1).broadcast_to([128, 8, 8]),
                            ALU.add)
                    for hh in range(4):
                        vector.scalar_tensor_tensor(
                            out=allix[:, hh, :].rearrange("p (i j) -> p i j", i=8),
                            in0=t["ixyf"][:, tsub, 0, hh, :].unsqueeze(2).broadcast_to([128, 8, 8]),
                            scalar=64.0,
                            in1=t["ixyf"][:, tsub, 1, hh, :].unsqueeze(1).broadcast_to([128, 8, 8]),
                            op0=ALU.mult, op1=ALU.add)
                    vector.drain()
                    for hh in range(4):
                        vector.max(t["sc8"][:, tsub, hh, :], allsc[:, hh, :])
                    vector.drain()
                    for hh in range(4):
                        vector.max_index(t["pku"][:, tsub, hh, :], t["sc8"][:, tsub, hh, :],
                                         allsc[:, hh, :])
                    vector.drain()
                    vector.tensor_copy(t["pkf"][:, tsub], t["pku"][:, tsub].bitcast(I32))
                    vector.drain()
                    for hh in range(4):
                        vector.tensor_tensor(
                            oh[:, hh, :, :],
                            t["pkf"][:, tsub, hh, :].unsqueeze(2).broadcast_to([128, 8, 64]),
                            t["iota64"][:].unsqueeze(1).broadcast_to([128, 8, 64]),
                            ALU.is_equal)
                    vector.drain()
                    for hh in range(4):
                        vector.tensor_tensor(
                            oh[:, hh, :, :], oh[:, hh, :, :],
                            allix[:, hh, :].unsqueeze(1).broadcast_to([128, 8, 64]),
                            ALU.mult)
                    vector.drain()
                    vector.reduce_sum(t["idx8f"][:, tsub].rearrange("p a i -> p (a i)"),
                                      oh[:, 0:4, :, :].rearrange("p a i j -> p (a i) j"),
                                      axis=AX.X)
                    vector.drain()
                vector.reduce_max(t["mx8"][:].rearrange("p (a b) -> p a b", a=16),
                                  t["sc8"][:, :, :, :].rearrange("p a h i -> p (a h) i"),
                                  axis=AX.X)
                vector.drain()
                vector.tensor_scalar(out=t["nm8"][:], in0=t["mx8"][:], scalar1=-1.0,
                                     scalar2=None, op0=ALU.mult)
                s_v.inc(vector.drain(), 1)

            @blk.scalar
            def _(scalar, c=c):
                scalar.wait_ge(s_v.h, s_v.n)
                for tsub in range(4):
                    for hh in range(4):
                        scalar.activation(t["sm"][:, tsub, hh, :], t["sc8"][:, tsub, hh, :],
                                          AF.Exp, bias=t["nm8"][:, tsub * 4 + hh:tsub * 4 + hh + 1],
                                          scale=1.0)
                s_s.inc(scalar.drain(), 1)

        with nc.Block() as blk:
            @blk.vector
            def _(vector, c=c, par=par):
                vector.wait_ge(s_s.h, s_s.n)
                vector.reduce_sum(t["sum8"][:].rearrange("p (a b) -> p a b", a=16),
                                  t["sm"][:, :, :, :].rearrange("p a h i -> p (a h) i"),
                                  axis=AX.X)
                vector.drain()
                vector.reciprocal(t["rcp8"][:], t["sum8"][:])
                vector.drain()
                vector.tensor_tensor(
                    t["sm"][:].rearrange("p a h i -> p (a h) i"),
                    t["sm"][:].rearrange("p a h i -> p (a h) i"),
                    t["rcp8"][:].unsqueeze(2).broadcast_to([128, 16, 8]), ALU.mult)
                vector.drain()
                vector.tensor_copy(t["smbf"][:, par].rearrange("p a h i -> p (a h i)"),
                                   t["sm"][:].rearrange("p a h i -> p (a h i)"))
                for tsub in range(4):
                    vector.scalar_tensor_tensor(
                        out=t["idx8f"][:, tsub], in0=t["idx8f"][:, tsub], scalar=512.0,
                        in1=t["tlocf"][:, tsub:tsub + 1].unsqueeze(2).broadcast_to([128, 4, 8]),
                        op0=ALU.mult, op1=ALU.add)
                vector.drain()
                vector.tensor_copy(t["idxi"][:, par].rearrange("p a h i -> p (a h i)"),
                                   t["idx8f"][:].rearrange("p a h i -> p (a h i)"))
                s_v.inc(vector.drain(), 1)
        f5_done[c] = s_v.n

        with nc.Block() as blk:
            @blk.gpsimd
            def _(gpsimd, c=c, par=par):
                gpsimd.wait_ge(s_v.h, s_v.n)
                for hh in range(4):
                    for tsub in range(4):
                        for i in range(8):
                            s_sc[par].inc(gpsimd.indirect_dma_start(
                                out=Sh_dram[par * 4 + hh].ap().rearrange("a b -> (a b)").unsqueeze(1),
                                out_offset=bass.IndirectOffsetOnAxis(
                                    t["idxi"][:, par, tsub, hh, i:i + 1].bitcast(U32), 0),
                                in_=t["smbf"][:, par, tsub, hh, i:i + 1],
                                in_offset=None), 16)
        sc_done[c] = s_sc[par].n

    def back(c):
        par = c % 2
        t0 = c * RF
        x1cv = t["x1c"][:, par]
        hTv = t["hT"][:, par]
        with nc.Block() as blk:
            @blk.sync
            def _(sync, c=c):
                pe_base0 = s_pe.n
                pnl_done.clear()
                for et in range(32):
                    if et >= 2:
                        sync.wait_ge(s_pe.h, pe_base0 + et - 1)
                    s_pn[et % 2].inc(sync.dma_start(
                        out=(t["pnA0"] if et % 2 == 0 else t["pnA1"])[:],
                        in_=deTH.ap()[:, :, et * 128:(et + 1) * 128]
                        .rearrange("a p t -> p a t")), 16)
                    pnl_done.append(s_pn[et % 2].n)

            @blk.tensor
            def _(tensor, c=c):
                s_base = s_s.n
                for et in range(32):
                    tensor.wait_ge(s_pn[et % 2].h, pnl_done[et])
                    if et < 2 and c >= 1:
                        tensor.wait_ge(s_v.h, be_v_done[c - 1])
                    if et >= 2:
                        tensor.wait_ge(s_s.h, s_base + et - 1)
                    for kd in range(GD):
                        ins_ = tensor.matmul(pb[et % 2][0:128, 0:RF],
                                             (t["pnA0"] if et % 2 == 0 else t["pnA1"])[:, kd, :],
                                             hTv[:, kd, :],
                                             start=(kd == 0), stop=(kd == GD - 1))
                    s_pe.inc(ins_, 1)

            @blk.scalar
            def _(scalar, c=c):
                pe_base = s_pe.n - 32
                for et in range(32):
                    scalar.wait_ge(s_pe.h, pe_base + et + 1)
                    scalar.activation(t["siluWT"][:, et, :], pb[et % 2][0:128, 0:RF], AF.Silu)
                    s_s.inc(scalar.drain(), 1)

        with nc.Block() as blk:
            @blk.sync
            def _(sync, c=c, par=par):
                sync.wait_ge(s_sc[par].h, sc_done[c])
                if c >= 1:
                    sync.wait_ge(s_pe.h, be_pe_done[c - 1])
                s_rep[0].inc(sync.dma_start(
                    out=t["Ssum"][:], in_=Sh_dram[par * 4].ap()
                    .rearrange("(a p) t -> p a t", p=128)), 16)
                wp_done[0] = s_rep[0].n
                vv = s_v.n
                for hh in range(1, 4):
                    for half in range(2):
                        if hh + half > 1:
                            sync.wait_ge(s_v.h, vv + (hh - 1) * 2 + half - 1 + 1)
                        s_rep[1].inc(sync.dma_start(
                            out=t["Stmp"][:], in_=Sh_dram[par * 4 + hh].ap()
                            .rearrange("(a p) t -> p a t", p=128)[:, half * 16:(half + 1) * 16, :]), 16)
                        wp_done[1 + (hh - 1) * 2 + half] = s_rep[1].n

            @blk.vector
            def _(vector, c=c):
                vector.wait_ge(s_rep[0].h, wp_done[0])
                i = 0
                for hh in range(1, 4):
                    for half in range(2):
                        i += 1
                        vector.wait_ge(s_rep[1].h, wp_done[i])
                        vector.tensor_add(t["Ssum"][:, half * 16:(half + 1) * 16, :],
                                          t["Ssum"][:, half * 16:(half + 1) * 16, :],
                                          t["Stmp"][:, 0:16, :])
                        s_v.inc(vector.drain(), 1)
                vector.wait_ge(s_s.h, s_s.n)
                vector.tensor_mul(t["Ssum"][:], t["Ssum"][:], t["siluWT"][:])
                s_v.inc(vector.drain(), 1)
        v_after_f7b = s_v.n

        with nc.Block() as blk:
            @blk.gpsimd
            def _(gpsimd, c=c, par=par, v_tgt=v_after_f7b):
                gpsimd.wait_ge(s_v.h, v_tgt)
                for hh in range(4):
                    for tsub in range(4):
                        for i in range(8):
                            s_sc[par].inc(gpsimd.indirect_dma_start(
                                out=Sh_dram[par * 4 + hh].ap().rearrange("a b -> (a b)").unsqueeze(1),
                                out_offset=bass.IndirectOffsetOnAxis(
                                    t["idxi"][:, par, tsub, hh, i:i + 1].bitcast(U32), 0),
                                in_=t["zbf"][:, 0:1],
                                in_offset=None), 16)

        with nc.Block() as blk:
            @blk.sync
            def _(sync, c=c):
                pe_base0 = s_pe.n
                pnl_done.clear()
                for dt_ in range(32):
                    if dt_ >= 1:
                        sync.wait_ge(s_pe.h, pe_base0 + 2 * dt_ - 1)
                    s_pn[dt_ % 2].inc(sync.dma_start(
                        out=(t["pnA0"] if dt_ % 2 == 0 else t["pnA1"])[:],
                        in_=WgH.ap()[:, :, dt_ * 128:(dt_ + 1) * 128]
                        .rearrange("a p t -> p a t")), 16)
                    s_pn[dt_ % 2].inc(sync.dma_start(
                        out=(t["pnB0"] if dt_ % 2 == 0 else t["pnB1"])[:],
                        in_=WuH.ap()[:, :, dt_ * 128:(dt_ + 1) * 128]
                        .rearrange("a p t -> p a t")), 16)
                    pnl_done.append(s_pn[dt_ % 2].n)

            @blk.tensor
            def _(tensor, c=c):
                s_base = s_s.n
                v_base = s_v.n
                for dt_ in range(32):
                    tensor.wait_ge(s_pn[dt_ % 2].h, pnl_done[dt_])
                    if dt_ < 2:
                        tensor.wait_ge(s_s.h, s_base)
                    if dt_ >= 2:
                        tensor.wait_ge(s_s.h, s_base + dt_ - 1)
                        tensor.wait_ge(s_v.h, v_base + dt_ - 1)
                    for kd in range(GD):
                        ins_ = tensor.matmul(pb[2 * (dt_ % 2)][0:128, 0:RF],
                                             (t["pnA0"] if dt_ % 2 == 0 else t["pnA1"])[:, kd, :],
                                             hTv[:, kd, :],
                                             start=(kd == 0), stop=(kd == GD - 1))
                    for kd in range(GD):
                        ins_ = tensor.matmul(pb[2 * (dt_ % 2) + 1][0:128, 0:RF],
                                             (t["pnB0"] if dt_ % 2 == 0 else t["pnB1"])[:, kd, :],
                                             hTv[:, kd, :],
                                             start=(kd == 0), stop=(kd == GD - 1))
                    s_pe.inc(ins_, 2)

            @blk.scalar
            def _(scalar, c=c):
                pe_base = s_pe.n - 64
                v_base = s_v.n
                for dt_ in range(32):
                    scalar.wait_ge(s_pe.h, pe_base + 2 * (dt_ + 1))
                    if dt_ >= 1:
                        scalar.wait_ge(s_v.h, v_base + dt_)
                    scalar.activation(t["etmp"][:], pb[2 * (dt_ % 2)][0:128, 0:RF],
                                      AF.Silu, bias=t["bgc"][:, dt_:dt_ + 1])
                    s_s.inc(scalar.drain(), 1)

            @blk.vector
            def _(vector, c=c):
                s_base2 = s_s.n - 32
                for dt_ in range(32):
                    vector.wait_ge(s_s.h, s_base2 + dt_ + 1)
                    vector.scalar_tensor_tensor(
                        out=t["gate"][:, dt_, :], in0=pb[2 * (dt_ % 2) + 1][0:128, 0:RF],
                        scalar=t["buc"][:, dt_:dt_ + 1], in1=t["etmp"][:],
                        op0=ALU.add, op1=ALU.mult)
                    s_v.inc(vector.drain(), 1)

        with nc.Block() as blk:
            @blk.sync
            def _(sync, c=c):
                pe_base0 = s_pe.n
                pnl_done.clear()
                idx = 0
                for dch in range(2):
                    for kd in range(64):
                        if idx >= 2:
                            sync.wait_ge(s_pe.h, pe_base0 + idx - 1)
                        srcH = WdH if kd < 32 else ueH
                        kk = kd % 32
                        s_pn[idx % 2].inc(sync.dma_start(
                            out=(t["pnC0"] if idx % 2 == 0 else t["pnC1"])[:],
                            in_=srcH.ap()[kk, :, dch * 512:(dch + 1) * 512]), 16)
                        pnl_done.append(s_pn[idx % 2].n)
                        idx += 1

            @blk.tensor
            def _(tensor, c=c):
                v_base = s_v.n
                idx = 0
                for dch in range(2):
                    if dch == 0:
                        tensor.wait_ge(s_v.h, v_base)
                    if dch == 1:
                        tensor.wait_ge(s_v.h, v_base + 1)
                    for kd in range(64):
                        tensor.wait_ge(s_pn[idx % 2].h, pnl_done[idx])
                        lhs = t["gate"] if kd < 32 else t["Ssum"]
                        kk = kd % 32
                        for tsub in range(4):
                            ins_ = tensor.matmul(
                                pb[tsub][0:128, 0:512],
                                lhs[:, kk, tsub * 128:(tsub + 1) * 128],
                                (t["pnC0"] if idx % 2 == 0 else t["pnC1"])[:],
                                start=(kd == 0), stop=(kd == 63))
                        s_pe.inc(ins_, 1)
                        idx += 1

            @blk.vector
            def _(vector, c=c):
                pe_base = s_pe.n - 128
                for dch in range(2):
                    vector.wait_ge(s_pe.h, pe_base + 64 * (dch + 1))
                    for tsub in range(4):
                        vector.tensor_add(x1cv[:, tsub, dch * 512:(dch + 1) * 512],
                                          x1cv[:, tsub, dch * 512:(dch + 1) * 512],
                                          pb[tsub][0:128, 0:512])
                    vector.drain()
                    for tsub in range(4):
                        vector.tensor_add(x1cv[:, tsub, dch * 512:(dch + 1) * 512],
                                          x1cv[:, tsub, dch * 512:(dch + 1) * 512],
                                          t["bdR"][:, dch * 512:(dch + 1) * 512])
                    s_v.inc(vector.drain(), 1)

            @blk.gpsimd
            def _(gpsimd, c=c):
                gpsimd.wait_ge(s_v.h, s_v.n)
                s_gd.inc(gpsimd.dma_start(
                    out=x_out[t0:t0 + RF, :].rearrange("(tt p) d -> p tt d", p=128),
                    in_=x1cv[:]), 16)
        be_pe_done[c] = s_pe.n
        be_v_done[c] = s_v.n

    for it in range(NCF + 1):
        if it < NCF:
            front(it)
        if it >= 1:
            back(it - 1)

    with nc.Block() as blk:
        @blk.sync
        def _(sync):
            sync.wait_ge(s_gd.h, s_gd.n)
            sync.wait_ge(s_sc[0].h, s_sc[0].n)
            sync.wait_ge(s_sc[1].h, s_sc[1].n)


def build_prep_keys(k, srch, dsth, ident):
    nc = k.nc
    s_ld = k.t["s_ld"]; s_v = k.t["s_v"]; s_s = k.t["s_s"]; s_pe = k.t["s_pe"]; s_st = k.t["s_st"]
    psT = k.t["psT"]
    tf = k.t["wp_f32"]; tb = k.t["wp_bf"]; to = k.t["wp_out"]
    for p_ in range(2):
        for h in range(4):
            with nc.Block() as blk:
                @blk.sync
                def _(sync, p_=p_, h=h):
                    sync.wait_ge(s_v.h, s_v.n)
                    s_ld.inc(sync.dma_start(out=tf[0:64, 0:64],
                                            in_=srch.ap()[h, :, p_, :]), 16)

                @blk.vector
                def _(vector):
                    vector.wait_ge(s_ld.h, s_ld.n)
                    vector.wait_ge(s_pe.h, s_pe.n)
                    vector.tensor_copy(tb[0:64, 0:64], tf[0:64, 0:64])
                    s_v.inc(vector.drain(), 1)

                @blk.tensor
                def _(tensor):
                    tensor.wait_ge(s_v.h, s_v.n)
                    tensor.wait_ge(s_s.h, s_s.n)
                    s_pe.inc(tensor.transpose(psT[0:64, 0:64], tb[0:64, 0:64],
                                              k.t["ident_sb"][0:64, 0:64]), 1)

                @blk.scalar
                def _(scalar):
                    scalar.wait_ge(s_pe.h, s_pe.n)
                    scalar.wait_ge(s_st.h, s_st.n)
                    scalar.copy(to[0:64, 0:64], psT[0:64, 0:64])
                    s_s.inc(scalar.drain(), 1)

                @blk.gpsimd
                def _(gpsimd, p_=p_, h=h):
                    gpsimd.wait_ge(s_s.h, s_s.n)
                    s_st.inc(gpsimd.dma_start(out=dsth.ap()[p_, h, :, :],
                                              in_=to[0:64, 0:64]), 16)


FFN_W_NAMES = ["Wg", "bg", "Wu", "bu", "Wd", "bd", "Wq", "keys", "down_embed", "up_embed"]


def build_zero_sh(k, Sh):
    nc = k.nc
    s_v = k.t["s_v"]; s_st = k.t["s_st"]
    zz = k.t["wp_bf"]
    with nc.Block() as blk:
        @blk.vector
        def _(vector):
            vector.memset(zz[:, 0:512], 0.0)
            s_v.inc(vector.drain(), 1)

        @blk.gpsimd
        def _(gpsimd):
            gpsimd.wait_ge(s_v.h, s_v.n)
            for hh in range(len(Sh)):
                for a in range(32):
                    s_st.inc(gpsimd.dma_start(
                        out=Sh[hh].ap().rearrange("(a p) t -> p a t", p=128)[:, a, :],
                        in_=zz[:, 0:512].bitcast(BF16)[:, 0:512]), 16)


def build_program(phases, dbg=True, nch=NCHM, x1_ext=False):
    nc = bass.Bass("TRN2", target_bir_lowering=False)
    es = ExitStack()
    k = K(nc, es)
    ins = {}
    ins["x"] = k.dram("x", [T, D], F32, kind="ExternalInput")
    for nm in MAMBA_W_NAMES:
        full = "L0_attn_a_" + nm
        shp = {"Win": [2 * DI, D], "convw": [DI, 4], "convb": [DI], "Wx": [128, DI],
               "Wdt": [DI, 64], "bdt": [DI], "Alog": [DI, NS], "Dp": [DI],
               "Wout": [D, DI], "ln_g": [D], "ln_b": [D], "Wfuse": [D, DI],
               "bfuse": [D]}[nm.rsplit("_", 1)[0] if nm not in ("Wfuse", "bfuse") else nm]
        ins[full] = k.dram(full, shp, F32, kind="ExternalInput")
    for nm in FFN_W_NAMES:
        full = "L0_ffn_" + nm
        shp = {"Wg": [D, 4096], "bg": [4096], "Wu": [D, 4096], "bu": [4096],
               "Wd": [4096, D], "bd": [D], "Wq": [D, 512], "keys": [4, 64, 2, 64],
               "down_embed": [4096, D], "up_embed": [4096, D]}[nm]
        ins[full] = k.dram(full, shp, F32, kind="ExternalInput")
    for nm, shp in [("W_attn", [D, 3 * D]), ("b_attn", [3 * D]), ("W_proj", [D, D]),
                    ("b_proj", [D]), ("temperature", [1])]:
        ins["L0_attn_b_" + nm] = k.dram("L0_attn_b_" + nm, shp, F32, kind="ExternalInput")
    ident_in = k.dram("ident", [128, 128], BF16, kind="ExternalInput")
    iotain = k.dram("iotain", [128, 84], F32, kind="ExternalInput")
    k.t["iotain_in"] = iotain
    k.t["iotaT_in"] = k.dram("iotaT", [T], F32, kind="ExternalInput")

    k.dram("xlnT", [GD, 128, T], BF16)
    if x1_ext:
        x1h = k.dram("x1", [T, D], F32, kind="ExternalInput")
    else:
        x1h = k.dram("x1", [T, D], F32, kind="ExternalOutput" if "ffn1" not in phases else "Internal")
    if "ffn1" not in phases and "attn" in phases:
        x2h = k.dram("x2", [T, D], F32, kind="ExternalInput")
    else:
        x2h = k.dram("x2", [T, D], F32,
                     kind="Internal" if "attn" in phases else "ExternalOutput")
    k.dram("bc_scr", [64, RM], F32)
    for dd in "fb":
        k.dram(f"WinT_{dd}", [GD, 128, 2 * DI], BF16)
        k.dram(f"WoutT_{dd}", [GI, 128, D], BF16)
        k.dram(f"WxT_{dd}", [GI, 128, 128], BF16)
        k.dram(f"WdtT_{dd}", [64, DI], BF16)
    k.dram("WfuseT", [GI, 128, D], BF16)
    k.dram("Wg_bf", [GD, 128, 4096], BF16)
    k.dram("Wu_bf", [GD, 128, 4096], BF16)
    k.dram("Wd_bf", [32, 128, D], BF16)
    k.dram("Wq_bf", [GD, 128, 512], BF16)
    k.dram("ue_bf", [32, 128, D], BF16)
    k.dram("deT", [GD, 128, 4096], BF16)
    k.dram("keysT_dram", [2, 4, 64, 64], BF16)
    Sh = [k.dram(f"Sh{i}", [4096, 512], BF16) for i in range(8)]
    k.dram("x2lnT", [GD, 128, T], BF16)
    k.dram("qkT", [16, 128, T], BF16)
    k.dram("Vtok", [T, D], BF16)
    k.dram("attnT", [GD, 128, T], BF16)
    k.dram("Wattn_bf", [GD, 128, 3 * D], BF16)
    k.dram("Wproj_bf", [GD, 128, D], BF16)
    x3h = k.dram("x3", [T, D], F32,
                 kind="Internal" if "ffn2" in phases else "ExternalOutput")
    x4h = k.dram("x4", [T, D], F32, kind="ExternalOutput") if "ffn2" in phases else None

    dbg_t = None
    if dbg:
        dbg_t = {}
        if "mamba" in phases:
            for dd in ("f", "b"):
                for nmm in ("uc", "dt", "ys", "u", "z"):
                    dbg_t[f"dbg_{nmm}_{dd}"] = k.dram(f"dbg_{nmm}_{dd}", [GI, 128, T], F32,
                                                      kind="ExternalOutput")
        if "ffn1" in phases:
            dbg_t["dbg_gate"] = k.dram("dbg_gate", [32, 128, T], BF16, kind="ExternalOutput")
            dbg_t["dbg_sfin"] = k.dram("dbg_sfin", [32, 128, T], BF16, kind="ExternalOutput")
            dbg_t["dbg_idx"] = k.dram("dbg_idx", [T, 32], F32, kind="ExternalOutput")
            dbg_t["dbg_sm"] = k.dram("dbg_sm", [T, 32], F32, kind="ExternalOutput")

    with es:
        for s in ["s_ld", "s_v", "s_s", "s_pe", "s_st", "s_g", "s_gd", "s_bc",
                  "s_xln", "s_pn0", "s_pn1", "s_rep0", "s_rep1", "s_wp", "s_x0",
                  "s_qk0", "s_qk1", "s_qp0", "s_qp1", "s_pf0", "s_pf1",
                  "s_sc0", "s_sc1"]:
            k.sem(s)
        psT = k.ps("psT", [128, 2048], BF16)
        for i in range(6):
            k.ps(f"pb{i}", [128, 512], F32)
        ident_sb = k.sb("ident_sb", [128, 128], BF16)
        k.sb("identf", [128, 128], F32)
        k.sb("epsT", [128, 1], F32)
        k.push_scope()
        k.sb("wp_f32", [128, 4096], F32)
        k.sb("wp_bf", [128, 4096], BF16)
        k.sb("wp_out", [128, 4096], BF16)

        with nc.Block() as blk:
            @blk.sync
            def _(sync):
                k.t["s_ld"].inc(sync.dma_start(out=ident_sb[:], in_=ident_in.ap()[:, :]), 16)

            @blk.vector
            def _(vector):
                vector.wait_ge(k.t["s_ld"].h, k.t["s_ld"].n)
                vector.memset(k.t["epsT"][:], EPS)
                vector.tensor_copy(k.t["identf"][:], ident_sb[:])
                k.t["s_v"].inc(vector.drain(), 1)

        trans = []
        casts = []
        if "mamba" in phases:
            for dd in "fb":
                trans += [(ins[f"L0_attn_a_Win_{dd}"], k.t[f"WinT_{dd}"], 2 * DI, D),
                          (ins[f"L0_attn_a_Wout_{dd}"], k.t[f"WoutT_{dd}"], D, DI),
                          (ins[f"L0_attn_a_Wx_{dd}"], k.t[f"WxT_{dd}"], 128, DI)]
            trans += [(ins["L0_attn_a_Wfuse"], k.t["WfuseT"], D, DI)]
        if "attn" in phases:
            casts += [(ins["L0_attn_b_W_attn"], k.t["Wattn_bf"], D, 3 * D),
                      (ins["L0_attn_b_W_proj"], k.t["Wproj_bf"], D, D)]
        if "ffn1" in phases:
            trans += [(ins["L0_ffn_down_embed"], k.t["deT"], 4096, D)]
            casts += [(ins["L0_ffn_Wg"], k.t["Wg_bf"], D, 4096),
                      (ins["L0_ffn_Wu"], k.t["Wu_bf"], D, 4096),
                      (ins["L0_ffn_Wd"], k.t["Wd_bf"], 4096, D),
                      (ins["L0_ffn_Wq"], k.t["Wq_bf"], D, 512),
                      (ins["L0_ffn_up_embed"], k.t["ue_bf"], 4096, D)]
        build_weight_prep(k, casts, trans, ident_sb)
        if "mamba" in phases:
            for dd in "fb":
                build_prep_wdt(k, ins[f"L0_attn_a_Wdt_{dd}"], k.t[f"WdtT_{dd}"], ident_sb)
        if "ffn1" in phases:
            build_prep_keys(k, ins["L0_ffn_keys"], k.t["keysT_dram"], ident_sb)
            build_zero_sh(k, Sh)

        if "mamba" in phases:
            build_ln_prepass(k, ins["x"].ap(), k.t["xlnT"], ident_sb, nch=nch)
        k.pop_scope()

        if "mamba" in phases:
            k.push_scope()
            alloc_mamba(k)
            weights = {}
            for dd in "fb":
                weights[f"WinT_{dd}"] = k.t[f"WinT_{dd}"]
                weights[f"WoutT_{dd}"] = k.t[f"WoutT_{dd}"]
            weights["WfuseT"] = k.t["WfuseT"]
            for dd in ("f", "b"):
                build_mamba_dir_setup(k, dd, ins)
                build_mamba_dir(k, dd, weights, k.t["xlnT"],
                                ins["x"].ap() if dd == 'f' else x1h.ap(),
                                x1h, dbg=dbg_t, nch=nch)
            k.pop_scope()

        if "ffn1" in phases:
            k.push_scope()
            t = alloc_ffn(k, "f1")
            build_ffn_setup(k, t, ins, iotain)
            build_ffn(k, t, ins, x1h.ap(), x2h.ap(), Sh, True, dbg=dbg_t)
            k.pop_scope()

        if "attn" in phases:
            k.push_scope()
            ta = alloc_attn(k)
            build_attn(k, ta, ins, x2h.ap(), x3h.ap(), dbg=dbg_t)
            k.pop_scope()

        if "ffn2" in phases:
            k.push_scope()
            t2 = alloc_ffn(k, "f2")
            build_ffn_setup(k, t2, ins, iotain)
            build_ffn(k, t2, ins, x3h.ap(), x4h.ap(), Sh, False, dbg=None)
            k.pop_scope()

        with nc.Block() as blk:
            @blk.sync
            def _(sync):
                sync.wait_ge(k.t["s_gd"].h, k.t["s_gd"].n)
                sync.wait_ge(k.t["s_st"].h, k.t["s_st"].n)

    return nc


ALIBI_START = 2.0 ** (-2.0 ** (-(np.log2(16) - 3)))
ALIBI_SLOPES = [ALIBI_START * ALIBI_START ** i for i in range(16)]


def alloc_attn(k):
    t = {}
    def sb(name, shape, dt=F32):
        t[name] = k.sb("at_" + name, shape, dt)
        return t[name]
    sb("xlnTc", [128, GD, 512], BF16)
    sb("x2c", [128, 4, D], F32)
    sb("xnbf", [128, 4, D], BF16)
    sb("brow", [1, 3 * D], F32)
    sb("browb", [1, 3 * D], BF16)
    sb("bprow", [1, D], F32)
    sb("bprowb", [1, D], BF16)
    sb("onesr", [1, 512], BF16)
    sb("qscale", [128, 1], F32)
    sb("traw", [128, 1], F32)
    sb("kposR", [128, T], F32)
    sb("tslope", [128, 16], F32)
    sb("alibiF", [128, T], F32)
    sb("scores_sb", [128, T], F32)
    sb("Qp", [128, 2, 128], BF16)
    sb("Kfull", [128, T], BF16)
    sb("Vfull", [128, 32, 64], BF16)
    sb("probs", [128, 2, T], BF16)
    sb("probsT_sb", [128, 2, 128], BF16)
    sb("pnq0", [128, GD, 128], BF16)
    sb("pnq1", [128, GD, 128], BF16)
    sb("pnv", [128, GD, 512], BF16)
    sb("qk_ev", [128, 2, 512], BF16)
    sb("vev", [128, 2, 4, 512], BF16)
    sb("mparts", [128, 32], F32)
    sb("lparts", [128, 32], F32)
    sb("masks", [128, 4, 512], F32)
    sb("mrow", [128, 1], F32)
    sb("btile", [128, 1], F32)
    sb("lrow", [128, 1], F32)
    sb("lpart", [128, 1], F32)
    sb("lrec", [128, 1], F32)
    sb("atile", [128, 2, 128], BF16)
    sb("lsq", [128, 4096], F32)
    sb("lm", [128, 4]); sb("ls1", [128, 4]); sb("ls2", [128, 4])
    sb("lvar", [128, 4]); sb("lrstd", [128, 4])
    return t


def build_attn(k, t, ins, x_in, x_out, dbg=None):
    nc = k.nc
    s_ld = k.t["s_ld"]; s_v = k.t["s_v"]; s_s = k.t["s_s"]; s_pe = k.t["s_pe"]
    s_st = k.t["s_st"]; s_g = k.t["s_g"]; s_gd = k.t["s_gd"]
    s_xln = k.t["s_xln"]; s_pn = [k.t["s_pn0"], k.t["s_pn1"]]
    s_rep = [k.t["s_rep0"], k.t["s_rep1"]]; s_wp = k.t["s_wp"]; s_x0 = k.t["s_x0"]
    psT = k.t["psT"]
    pb = [k.t[f"pb{i}"] for i in range(4)]
    ident = k.t["ident_sb"]
    x2lnT = k.t["x2lnT"]; qkT = k.t["qkT"]; Vtok = k.t["Vtok"]; attnT = k.t["attnT"]
    WattnH = k.t["Wattn_bf"]; WprojH = k.t["Wproj_bf"]
    lntmp = {"m": t["lm"], "s1": t["ls1"], "s2": t["ls2"], "var": t["lvar"],
             "rstd": t["lrstd"], "sq": t["lsq"][:, 0:4096], "xn": None}
    pnl_done = []
    wp_done = [0] * 8

    with nc.Block() as blk:
        @blk.sync
        def _(sync):
            sync.wait_ge(s_v.h, s_v.n)
            sync.wait_ge(s_s.h, s_s.n)
            sync.wait_ge(s_gd.h, s_gd.n)
            s_ld.inc(sync.dma_start(out=t["brow"][0:1, :],
                                    in_=ins["L0_attn_b_b_attn"].ap().unsqueeze(0)), 16)
            s_ld.inc(sync.dma_start(out=t["bprow"][0:1, :],
                                    in_=ins["L0_attn_b_b_proj"].ap().unsqueeze(0)), 16)
            s_ld.inc(sync.dma_start(out=t["qscale"][:],
                                    in_=ins["L0_attn_b_temperature"].ap()
                                    .unsqueeze(0).broadcast_to([128, 1])), 16)
            s_ld.inc(sync.dma_start(out=t["traw"][:],
                                    in_=ins["L0_attn_b_temperature"].ap()
                                    .unsqueeze(0).broadcast_to([128, 1])), 16)
            s_ld.inc(sync.dma_start(out=t["kposR"][:],
                                    in_=k.t["iotaT_in"].ap().unsqueeze(0)
                                    .broadcast_to([128, T])), 16)
            s_ld.inc(sync.dma_start(out=t["tslope"][:],
                                    in_=k.t["iotain_in"].ap()[:, 68:84]), 16)

        @blk.vector
        def _(vector):
            vector.wait_ge(s_ld.h, s_ld.n)
            vector.tensor_scalar(out=t["qscale"][:], in0=t["qscale"][:], scalar1=8.0,
                                 scalar2=None, op0=ALU.mult)
            vector.drain()
            vector.reciprocal(t["qscale"][:], t["qscale"][:])
            vector.drain()
            vector.memset(t["onesr"][0:1, :], 1.0)
            vector.memset(t["masks"][:], 0.0)
            vector.tensor_scalar(out=t["tslope"][:], in0=t["tslope"][:],
                                 scalar1=t["traw"][:, 0:1], scalar2=8.0,
                                 op0=ALU.mult, op1=ALU.mult)
            vector.drain()
            vector.tensor_copy(t["browb"][0:1, :], t["brow"][0:1, :])
            vector.tensor_copy(t["bprowb"][0:1, :], t["bprow"][0:1, :])
            s_v.inc(vector.drain(), 1)

    with nc.Block() as blk:
        @blk.gpsimd
        def _(gpsimd):
            gpsimd.wait_ge(s_v.h, s_v.n)
            for m_ in range(4):
                gpsimd.affine_select(t["masks"][:, m_, :], t["masks"][:, m_, :],
                                     pattern=[[-1, 512]], compare_op=ALU.is_ge,
                                     fill=-1e30, base=m_ * 128, channel_multiplier=1)
            s_g.inc(gpsimd.drain(), 1)
    g_tgt = s_g.n

    RF = 512
    for c in range(T // RF):
        t0 = c * RF
        with nc.Block() as blk:
            @blk.sync
            def _(sync, c=c):
                sync.wait_ge(s_st.h, s_st.n)
                sync.wait_ge(s_v.h, s_v.n)
                s_x0.inc(sync.dma_start(out=t["x2c"][:], in_=x_in[t0:t0 + RF, :]
                                        .rearrange("(tt p) d -> p tt d", p=128)), 16)

            @blk.vector
            def _(vector, c=c):
                vector.wait_ge(s_x0.h, s_x0.n)
        ln_chunk(k, None, t["x2c"][:], t["xnbf"][:], 4, D, s_v, s_s, lntmp)
        with nc.Block() as blk:
            @blk.tensor
            def _(tensor, c=c):
                tensor.wait_ge(s_v.h, s_v.n)
                s_base_t = s_s.n
                for tt in range(4):
                    if tt >= 2:
                        tensor.wait_ge(s_s.h, s_base_t + tt - 1)
                    for dj in range(GD):
                        ins_ = tensor.transpose(
                            psT[:, (tt % 2) * 1024 + dj * 128:(tt % 2) * 1024 + (dj + 1) * 128],
                            t["xnbf"][:, tt, dj * 128:(dj + 1) * 128], ident[:])
                    s_pe.inc(ins_, 1)

            @blk.scalar
            def _(scalar, c=c):
                base = s_pe.n - 4
                for tt in range(4):
                    scalar.wait_ge(s_pe.h, base + tt + 1)
                    scalar.copy(t["xlnTc"][:, :, tt * 128:(tt + 1) * 128],
                                psT[:, (tt % 2) * 1024:(tt % 2) * 1024 + GD * 128]
                                .rearrange("p (a b) -> p a b", a=GD))
                    s_s.inc(scalar.drain(), 1)

        with nc.Block() as blk:
            @blk.sync
            def _(sync, c=c):
                pe_base0 = s_pe.n
                pnl_done.clear()
                for ft in range(16):
                    if ft >= 2:
                        sync.wait_ge(s_pe.h, pe_base0 + ft - 1)
                    s_pn[ft % 2].inc(sync.dma_start(
                        out=(t["pnq0"] if ft % 2 == 0 else t["pnq1"])[:],
                        in_=WattnH.ap()[:, :, ft * 128:(ft + 1) * 128]
                        .rearrange("a p m -> p a m")), 16)
                    pnl_done.append(s_pn[ft % 2].n)

            @blk.tensor
            def _(tensor, c=c):
                s_base = s_s.n
                for ft in range(16):
                    tensor.wait_ge(s_pn[ft % 2].h, pnl_done[ft])
                    if ft >= 2:
                        tensor.wait_ge(s_s.h, s_base + ft - 1)
                    for kd in range(GD):
                        ins_ = tensor.matmul(pb[ft % 2][0:128, 0:RF],
                                             (t["pnq0"] if ft % 2 == 0 else t["pnq1"])[:, kd, :],
                                             t["xlnTc"][:, kd, :],
                                             start=(kd == 0), stop=False)
                    ins_ = tensor.matmul(pb[ft % 2][0:128, 0:RF],
                                         t["browb"][0:1, ft * 128:(ft + 1) * 128],
                                         t["onesr"][0:1, 0:RF],
                                         start=False, stop=True)
                    s_pe.inc(ins_, 1)

            @blk.scalar
            def _(scalar, c=c):
                pe_base = s_pe.n - 16
                qk_b = [k.t["s_qk0"].n, k.t["s_qk1"].n]
                for ft in range(16):
                    scalar.wait_ge(s_pe.h, pe_base + ft + 1)
                    if ft >= 2:
                        scalar.wait_ge(k.t[f"s_qk{ft % 2}"].h,
                                       qk_b[ft % 2] + 16 * (ft // 2))
                    else:
                        scalar.wait_ge(k.t[f"s_qk{ft % 2}"].h, qk_b[ft % 2])
                    scalar.copy(t["qk_ev"][:, ft % 2, :], pb[ft % 2][0:128, 0:RF])
                    s_s.inc(scalar.drain(), 1)

            @blk.gpsimd
            def _(gpsimd, c=c):
                s_base = s_s.n - 16
                for ft in range(16):
                    gpsimd.wait_ge(s_s.h, s_base + ft + 1)
                    k.t[f"s_qk{ft % 2}"].inc(gpsimd.dma_start(
                        out=qkT.ap()[ft, :, t0:t0 + RF], in_=t["qk_ev"][:, ft % 2, :]), 16)

        with nc.Block() as blk:
            @blk.sync
            def _(sync, c=c):
                pe_base0 = s_pe.n
                wp_done[0] = 0
                for vch in range(2):
                    if vch == 1:
                        sync.wait_ge(s_pe.h, pe_base0 + 4)
                    s_wp.inc(sync.dma_start(
                        out=t["pnv"][:], in_=WattnH.ap()[:, :, 2 * D + vch * 512:2 * D + (vch + 1) * 512]
                        .rearrange("a p m -> p a m")), 16)
                    wp_done[vch] = s_wp.n

            @blk.tensor
            def _(tensor, c=c):
                v_base = s_v.n
                for vch in range(2):
                    tensor.wait_ge(s_wp.h, wp_done[vch])
                    if vch == 1:
                        tensor.wait_ge(s_v.h, v_base + 4)
                    for tsub in range(4):
                        for kd in range(GD):
                            ins_ = tensor.matmul(pb[tsub][0:128, 0:512],
                                                 t["xlnTc"][:, kd, tsub * 128:(tsub + 1) * 128],
                                                 t["pnv"][:, kd, :],
                                                 start=(kd == 0), stop=False)
                        ins_ = tensor.matmul(pb[tsub][0:128, 0:512],
                                             t["onesr"][0:1, tsub * 128:(tsub + 1) * 128],
                                             t["browb"][0:1, 2 * D + vch * 512:2 * D + (vch + 1) * 512],
                                             start=False, stop=True)
                        s_pe.inc(ins_, 1)

            @blk.vector
            def _(vector, c=c):
                pe_base = s_pe.n - 8
                vector.wait_ge(s_gd.h, s_gd.n)
                i = 0
                for vch in range(2):
                    for tsub in range(4):
                        i += 1
                        vector.wait_ge(s_pe.h, pe_base + i)
                        vector.tensor_copy(t["vev"][:, vch, tsub, :],
                                           pb[tsub][0:128, 0:512])
                        s_v.inc(vector.drain(), 1)

            @blk.gpsimd
            def _(gpsimd, c=c):
                v_base = s_v.n - 8
                i = 0
                for vch in range(2):
                    for tsub in range(4):
                        i += 1
                        gpsimd.wait_ge(s_v.h, v_base + i)
                        s_gd.inc(gpsimd.dma_start(
                            out=Vtok.ap()[t0 + tsub * 128:t0 + (tsub + 1) * 128,
                                          vch * 512:(vch + 1) * 512],
                            in_=t["vev"][:, vch, tsub, :]), 16)

    with nc.Block() as blk:
        @blk.sync
        def _(sync):
            sync.wait_ge(s_gd.h, s_gd.n)
            sync.wait_ge(k.t["s_qk0"].h, k.t["s_qk0"].n)
            sync.wait_ge(k.t["s_qk1"].h, k.t["s_qk1"].n)

    u_scores_done = []
    u_T_done = []
    u_copies_done = []
    u_atile_done = []
    s_qk = [k.t["s_qp0"], k.t["s_qp1"]]

    for h in range(16):
        ktile_f = 8 + h // 2
        krow = (h % 2) * 64
        kf_tgt = [0]
        with nc.Block() as blk:
            @blk.sync
            def _(sync, h=h):
                sync.wait_ge(s_v.h, s_v.n)
                sync.wait_ge(s_s.h, s_s.n)
                s_rep[0].inc(sync.dma_start(
                    out=t["Kfull"][0:64, :],
                    in_=qkT.ap()[ktile_f, krow:krow + 64, :]), 16)
                s_rep[0].inc(sync.dma_start(
                    out=t["Vfull"][:],
                    in_=Vtok.ap()[:, h * 64:(h + 1) * 64]
                    .rearrange("(a p) v -> p a v", p=128)), 16)

            @blk.vector
            def _(vector, h=h):
                vector.wait_ge(s_rep[0].h, s_rep[0].n)
                vector.wait_ge(s_g.h, g_tgt)
                vector.tensor_scalar(out=t["alibiF"][:], in0=t["kposR"][:],
                                     scalar1=t["tslope"][:, h:h + 1], scalar2=None,
                                     op0=ALU.mult)
                s_v.inc(vector.drain(), 1)
        kf_tgt[0] = s_v.n

        for q in range(32):
            u = h * 32 + q
            par = u % 2
            q0 = q * 128
            kmax = q0 + 128
            nkb = (kmax + 511) // 512
            with nc.Block() as blk:
                b_pe = s_pe.n
                b_s = s_s.n
                b_v = s_v.n
                ntot = sum((min(512, kmax - kb * 512) + 127) // 128 for kb in range(nkb))
                qp_tgt = [0]

                @blk.sync
                def _(sync, h=h, q=q, u=u, par=par):
                    if u >= 2:
                        sync.wait_ge(s_pe.h, u_scores_done[u - 2])
                    s_qk[par].inc(sync.dma_start(
                        out=t["Qp"][0:64, par, :],
                        in_=qkT.ap()[h // 2, (h % 2) * 64:(h % 2) * 64 + 64,
                                     q * 128:(q + 1) * 128]), 16)
                    qp_tgt[0] = s_qk[par].n

                @blk.tensor
                def _(tensor, h=h, q=q, u=u, par=par, nkb=nkb, kmax=kmax,
                      b_pe=b_pe, b_v=b_v, b_s=b_s, kf=kf_tgt[0]):
                    tensor.wait_ge(s_v.h, kf)
                    tensor.wait_ge(s_qk[par].h, qp_tgt[0])
                    if u >= 1:
                        tensor.wait_ge(s_s.h, u_copies_done[u - 1])
                    for kb in range(nkb):
                        n = min(512, kmax - kb * 512)
                        if kb >= 2:
                            tensor.wait_ge(s_s.h, b_s + kb - 1)
                        ins_ = tensor.matmul(pb[kb % 2][0:128, 0:n],
                                             t["Qp"][0:64, par, :],
                                             t["Kfull"][0:64, kb * 512:kb * 512 + n],
                                             start=True, stop=True)
                        s_pe.inc(ins_, 1)
                    u_scores_done.append(s_pe.n)
                    ntot = sum((min(512, kmax - kb * 512) + 127) // 128 for kb in range(nkb))
                    tiles = []
                    for kb in range(nkb):
                        n = min(512, kmax - kb * 512)
                        for j in range((n + 127) // 128):
                            tiles.append((kb, j, min(128, n - j * 128)))

                    def emit_T(jj):
                        kb, j, nn = tiles[jj]
                        if jj == 0:
                            tensor.wait_ge(s_v.h, b_v + nkb + 2)
                        if jj >= 2:
                            tensor.wait_ge(s_v.h, b_v + nkb + jj + 1)
                        ins_ = tensor.transpose(
                            psT[0:nn, (jj % 2) * 1024:(jj % 2) * 1024 + 128],
                            t["probs"][:, par, kb * 512 + j * 128:kb * 512 + j * 128 + nn],
                            ident[:])
                        s_pe.inc(ins_, 1)

                    def emit_PV(jj):
                        kb, j, nn = tiles[jj]
                        tensor.wait_ge(s_v.h, b_v + nkb + jj + 3)
                        if jj == 0 and u >= 2:
                            tensor.wait_ge(s_s.h, u_atile_done[u - 2])
                        ins_ = tensor.matmul(pb[2 + par][0:64, 0:128],
                                             t["Vfull"][0:nn, kb * 4 + j, :],
                                             t["probsT_sb"][0:nn, jj % 2, :],
                                             start=(jj == 0), stop=(jj == ntot - 1))
                        s_pe.inc(ins_, 1)

                    emit_T(0)
                    u_T_done.append(s_pe.n)
                    for jj in range(ntot - 1):
                        emit_T(jj + 1)
                        u_T_done[-1] = s_pe.n
                        emit_PV(jj)
                    emit_PV(ntot - 1)

                @blk.vector
                def _(vector, h=h, q=q, u=u, par=par, nkb=nkb, kmax=kmax,
                      b_pe=b_pe, b_s=b_s):
                    mvar = q % 4
                    for kb in range(nkb):
                        n = min(512, kmax - kb * 512)
                        vector.wait_ge(s_pe.h, b_pe + kb + 1)
                        vector.tensor_add(pb[kb % 2][0:128, 0:n],
                                          pb[kb % 2][0:128, 0:n],
                                          t["alibiF"][:, kb * 512:kb * 512 + n])
                        vector.drain()
                        if kb == nkb - 1:
                            vector.tensor_add(pb[kb % 2][0:128, 0:n],
                                              pb[kb % 2][0:128, 0:n],
                                              t["masks"][:, mvar, 0:n])
                            vector.drain()
                        vector.reduce_max(t["mparts"][:, kb:kb + 1],
                                          pb[kb % 2][0:128, 0:n], axis=AX.X)
                        s_v.inc(vector.drain(), 1)
                    vector.reduce_max(t["mrow"][:], t["mparts"][:, 0:nkb], axis=AX.X)
                    vector.drain()
                    vector.tensor_scalar(out=t["btile"][:], in0=t["mrow"][:],
                                         scalar1=t["qscale"][:, 0:1], scalar2=-1.0,
                                         op0=ALU.mult, op1=ALU.mult)
                    s_v.inc(vector.drain(), 1)
                    vector.wait_ge(s_s.h, b_s + 2 * nkb)
                    vector.reduce_sum(t["lrow"][:], t["lparts"][:, 0:nkb], axis=AX.X)
                    vector.drain()
                    vector.reciprocal(t["lrec"][:], t["lrow"][:])
                    vector.drain()
                    vector.tensor_scalar(out=t["probs"][:, par, 0:kmax],
                                         in0=t["probs"][:, par, 0:kmax],
                                         scalar1=t["lrec"][:, 0:1], scalar2=None,
                                         op0=ALU.mult)
                    s_v.inc(vector.drain(), 1)
                    jj = 0
                    for kb in range(nkb):
                        n = min(512, kmax - kb * 512)
                        for j in range((n + 127) // 128):
                            nn = min(128, n - j * 128)
                            vector.wait_ge(s_pe.h,
                                           b_pe + nkb + (1 if jj == 0 else 2 * jj))
                            vector.tensor_copy(t["probsT_sb"][0:nn, jj % 2, :],
                                               psT[0:nn, (jj % 2) * 1024:(jj % 2) * 1024 + 128])
                            s_v.inc(vector.drain(), 1)
                            jj += 1

                @blk.scalar
                def _(scalar, h=h, q=q, u=u, par=par, nkb=nkb, kmax=kmax,
                      b_pe=b_pe, b_v=b_v):
                    for kb in range(nkb):
                        n = min(512, kmax - kb * 512)
                        scalar.wait_ge(s_v.h, b_v + kb + 1)
                        scalar.copy(t["scores_sb"][:, kb * 512:kb * 512 + n],
                                    pb[kb % 2][0:128, 0:n])
                        s_s.inc(scalar.drain(), 1)
                    u_copies_done.append(s_s.n)
                    scalar.wait_ge(s_v.h, b_v + nkb + 1)
                    if u >= 2:
                        scalar.wait_ge(s_pe.h, u_T_done[u - 2])
                    for kb in range(nkb):
                        n = min(512, kmax - kb * 512)
                        scalar.activation(t["probs"][:, par, kb * 512:kb * 512 + n],
                                          t["scores_sb"][:, kb * 512:kb * 512 + n], AF.Exp,
                                          scale=t["qscale"][:, 0:1],
                                          bias=t["btile"][:, 0:1],
                                          accum_out=t["lparts"][:, kb:kb + 1])
                        s_s.inc(scalar.drain(), 1)
                    ntot = sum((min(512, kmax - kb * 512) + 127) // 128 for kb in range(nkb))
                    scalar.wait_ge(s_pe.h, b_pe + nkb + 2 * ntot)
                    scalar.wait_ge(s_gd.h, s_gd.n)
                    scalar.copy(t["atile"][0:64, par, :], pb[2 + par][0:64, 0:128])
                    s_s.inc(scalar.drain(), 1)
                    u_atile_done.append(s_s.n)

                @blk.gpsimd
                def _(gpsimd, h=h, q=q, u=u, par=par, nkb=nkb, b_s=b_s):
                    gpsimd.wait_ge(s_s.h, b_s + 2 * nkb + 1)
                    s_gd.inc(gpsimd.dma_start(
                        out=attnT.ap()[h // 2, (h % 2) * 64:(h % 2) * 64 + 64,
                                       q * 128:(q + 1) * 128],
                        in_=t["atile"][0:64, par, :]), 16)

    for c in range(T // 512):
        t0 = c * 512
        with nc.Block() as blk:
            @blk.sync
            def _(sync, c=c):
                sync.wait_ge(s_gd.h, s_gd.n)
                sync.wait_ge(s_v.h, s_v.n)
                s_xln.inc(sync.dma_start(
                    out=t["xlnTc"][:], in_=attnT.ap()[:, :, t0:t0 + 512]
                    .rearrange("a p t -> p a t")), 16)
                s_x0.inc(sync.dma_start(out=t["x2c"][:], in_=x_in[t0:t0 + 512, :]
                                        .rearrange("(tt p) d -> p tt d", p=128)), 16)
                wp_done[0] = 0
                pe_b = s_pe.n
                for dch in range(2):
                    if dch == 1:
                        sync.wait_ge(s_pe.h, pe_b + 4)
                    s_wp.inc(sync.dma_start(
                        out=t["pnv"][:], in_=WprojH.ap()[:, :, dch * 512:(dch + 1) * 512]
                        .rearrange("a p m -> p a m")), 16)
                    wp_done[dch] = s_wp.n

            @blk.tensor
            def _(tensor, c=c):
                tensor.wait_ge(s_xln.h, s_xln.n)
                v_b = s_v.n
                for dch in range(2):
                    tensor.wait_ge(s_wp.h, wp_done[dch])
                    if dch == 1:
                        tensor.wait_ge(s_v.h, v_b + 1)
                    for tsub in range(4):
                        for kd in range(GD):
                            ins_ = tensor.matmul(pb[tsub][0:128, 0:512],
                                                 t["xlnTc"][:, kd, tsub * 128:(tsub + 1) * 128],
                                                 t["pnv"][:, kd, :],
                                                 start=(kd == 0), stop=False)
                        ins_ = tensor.matmul(pb[tsub][0:128, 0:512],
                                             t["onesr"][0:1, tsub * 128:(tsub + 1) * 128],
                                             t["bprowb"][0:1, dch * 512:(dch + 1) * 512],
                                             start=False, stop=True)
                        s_pe.inc(ins_, 1)

            @blk.vector
            def _(vector, c=c):
                pe_base = s_pe.n - 8
                vector.wait_ge(s_x0.h, s_x0.n)
                for dch in range(2):
                    vector.wait_ge(s_pe.h, pe_base + 4 * (dch + 1))
                    for tsub in range(4):
                        vector.tensor_add(t["x2c"][:, tsub, dch * 512:(dch + 1) * 512],
                                          t["x2c"][:, tsub, dch * 512:(dch + 1) * 512],
                                          pb[tsub][0:128, 0:512])
                    s_v.inc(vector.drain(), 1)

            @blk.gpsimd
            def _(gpsimd, c=c):
                gpsimd.wait_ge(s_v.h, s_v.n)
                s_gd.inc(gpsimd.dma_start(
                    out=x_out[t0:t0 + 512, :].rearrange("(tt p) d -> p tt d", p=128),
                    in_=t["x2c"][:]), 16)


_NC_CACHE = {}


def _build_full():
    if "nc" not in _NC_CACHE:
        _NC_CACHE["nc"] = build_program(["mamba", "ffn1", "attn", "ffn2"],
                                        dbg=False, nch=NCHM)
    return _NC_CACHE["nc"]


def kernel(**inputs):
    nc = _build_full()
    x = np.ascontiguousarray(np.asarray(inputs["L0_attn_a_x"], np.float32))
    io = np.zeros((128, 84), np.float32)
    io[:, 0:64] = np.arange(64)[None, :]
    for ts in range(4):
        io[:, 64 + ts] = ts * 128 + np.arange(128)
    io[:, 68:84] = np.asarray(ALIBI_SLOPES, np.float32)[None, :]
    common = {"ident": np.eye(128, dtype=bfnp),
              "iotain": io,
              "iotaT": np.arange(T, dtype=np.float32)}
    for nm in MAMBA_W_NAMES:
        full = "L0_attn_a_" + nm
        common[full] = np.ascontiguousarray(np.asarray(inputs[full], np.float32))
    for nm in FFN_W_NAMES:
        full = "L0_ffn_" + nm
        common[full] = np.ascontiguousarray(np.asarray(inputs[full], np.float32))
    for nm in ["W_attn", "b_attn", "W_proj", "b_proj", "temperature"]:
        full = "L0_attn_b_" + nm
        common[full] = np.ascontiguousarray(np.asarray(inputs[full], np.float32))
    in_maps = []
    for i in range(8):
        m = dict(common)
        m["x"] = x[i]
        in_maps.append(m)
    res = run_bass_kernel_spmd(nc, in_maps, core_ids=list(range(8)))
    return np.stack([np.asarray(res.results[i]["x4"], np.float32)
                     for i in range(8)], 0)


# version 04
# speedup vs baseline: 1.2336x; 1.0508x over previous
import numpy as np
import ml_dtypes
from contextlib import ExitStack
import concourse.bass as bass
import concourse.mybir as mybir
from concourse.bass_utils import run_bass_kernel_spmd

F32 = mybir.dt.float32
BF16 = mybir.dt.bfloat16
U32 = mybir.dt.uint32
I32 = mybir.dt.int32
AF = mybir.ActivationFunctionType
ALU = mybir.AluOpType
AX = mybir.AxisListType
bfnp = ml_dtypes.bfloat16

T = 4096
D = 1024
DI = 2048
GI = 16
GD = 8
NS = 32
RM = 256
NCHM = T // RM
NQ = 4
QN = 8
EPS = 1e-5


class Sem:
    def __init__(self, es, nc, name):
        self.h = es.enter_context(nc.semaphore(name))
        self.n = 0

    def inc(self, instr, k=1):
        instr.then_inc(self.h, k)
        self.n += k
        return instr


class K:
    def __init__(self, nc, es):
        self.nc = nc
        self.es = es
        self.scopes = []
        self.t = {}

    def push_scope(self):
        self.scopes.append(ExitStack())

    def pop_scope(self):
        self.scopes.pop().close()

    def sb(self, name, shape, dt=F32):
        es = self.scopes[-1] if self.scopes else self.es
        h = es.enter_context(self.nc.sbuf_tensor(name, list(shape), dt))
        self.t[name] = h
        return h

    def ps(self, name, shape, dt=F32):
        h = self.es.enter_context(self.nc.psum_tensor(name, list(shape), dt))
        self.t[name] = h
        return h

    def dram(self, name, shape, dt=F32, kind="Internal"):
        h = self.nc.dram_tensor(name, list(shape), dt, kind=kind)
        self.t[name] = h
        return h

    def sem(self, name):
        s = Sem(self.es, self.nc, name)
        self.t[name] = s
        return s


def ln_chunk(k, blkmk, x_tm, out_bf, ntt, width, s_v, s_s, tmp, gain=None, bias=None):
    nc = k.nc
    inv = 1.0 / width
    with nc.Block() as blk:
        @blk.vector
        def _(vector):
            vector.tensor_mul(tmp["sq"][:, 0:ntt * width].rearrange("p (a b) -> p a b", a=ntt),
                              x_tm, x_tm)
            vector.reduce_sum(tmp["s1"][:, 0:ntt], x_tm, axis=AX.X)
            vector.drain()
            vector.reduce_sum(tmp["s2"][:, 0:ntt],
                              tmp["sq"][:, 0:ntt * width].rearrange("p (a b) -> p a b", a=ntt),
                              axis=AX.X)
            vector.tensor_scalar(out=tmp["m"][:, 0:ntt], in0=tmp["s1"][:, 0:ntt],
                                 scalar1=inv, scalar2=None, op0=ALU.mult)
            vector.drain()
            vector.tensor_mul(tmp["var"][:, 0:ntt], tmp["m"][:, 0:ntt], tmp["m"][:, 0:ntt])
            vector.drain()
            vector.scalar_tensor_tensor(out=tmp["var"][:, 0:ntt], in0=tmp["s2"][:, 0:ntt],
                                        scalar=inv, in1=tmp["var"][:, 0:ntt],
                                        op0=ALU.mult, op1=ALU.subtract)
            s_v.inc(vector.drain(), 1)

        @blk.scalar
        def _(scalar):
            scalar.wait_ge(s_v.h, s_v.n)
            scalar.activation(tmp["rstd"][:, 0:ntt], tmp["var"][:, 0:ntt], AF.Ln, bias=k.t["epsT"][:, 0:1])
            scalar.drain()
            scalar.activation(tmp["rstd"][:, 0:ntt], tmp["rstd"][:, 0:ntt], AF.Exp, scale=-0.5)
            s_s.inc(scalar.drain(), 1)

    with nc.Block() as blk2:
        @blk2.vector
        def _(vector):
            vector.wait_ge(s_s.h, s_s.n)
            for tt in range(ntt):
                dst = out_bf[:, tt, :] if gain is None else tmp["xn"][:, tt, :]
                vector.tensor_scalar(out=dst, in0=x_tm[:, tt, :],
                                     scalar1=tmp["m"][:, tt:tt + 1],
                                     scalar2=tmp["rstd"][:, tt:tt + 1],
                                     op0=ALU.subtract, op1=ALU.mult)
            if gain is not None:
                vector.drain()
                for tt in range(ntt):
                    vector.tensor_mul(tmp["xn"][:, tt, :], tmp["xn"][:, tt, :],
                                      gain)
                vector.drain()
                for tt in range(ntt):
                    vector.tensor_add(out_bf[:, tt, :], tmp["xn"][:, tt, :], bias)
            s_v.inc(vector.drain(), 1)


def transpose_tm_to_fm(k, in_bf, out_bf, ntt, nd, psT, ident, s_pe, s_s):
    nc = k.nc
    with nc.Block() as blk:
        @blk.tensor
        def _(tensor):
            tensor.wait_ge(s_s.h, s_s.n)
            for tt in range(ntt):
                for dj in range(nd):
                    ins = tensor.transpose(psT[:, dj * 128:(dj + 1) * 128],
                                           in_bf[:, tt, dj * 128:(dj + 1) * 128],
                                           ident[:])
                s_pe.inc(ins, 1)

        @blk.scalar
        def _(scalar):
            base = s_pe.n - ntt
            for tt in range(ntt):
                scalar.wait_ge(s_pe.h, base + tt + 1)
                scalar.copy(out_bf[:, :, tt * 128:(tt + 1) * 128],
                            psT[:, 0:nd * 128].rearrange("p (a b) -> p a b", a=nd))
                s_s.inc(scalar.drain(), 1)


def build_ln_prepass(k, x_in, xlnT_dram, ident, gain=None, bias=None, nch=NCHM,
                     tag="lp", scr=None):
    nc = k.nc
    s_ld = k.t["s_ld"]; s_v = k.t["s_v"]; s_s = k.t["s_s"]; s_pe = k.t["s_pe"]; s_st = k.t["s_st"]
    s_gd = k.t["s_gd"]
    if scr is None:
        xc = k.sb(tag + "_xc", [128, 2, D], F32)
        xnbf = k.sb(tag + "_xnbf", [128, 2, D], BF16)
        xT = k.sb(tag + "_xT", [128, GD, 256], BF16)
        tmp = {"m": k.sb(tag + "_m", [128, 2]), "s1": k.sb(tag + "_s1", [128, 2]),
               "s2": k.sb(tag + "_s2", [128, 2]), "var": k.sb(tag + "_var", [128, 2]),
               "rstd": k.sb(tag + "_rstd", [128, 2]), "sq": k.sb(tag + "_sq", [128, 2 * D]),
               "xn": None}
    else:
        xc = carve(scr, 0, 2048, (2, D))
        xnbf = carve(scr, 4096, 1024, (2, D), BF16)
        xT = carve(scr, 5120, 1024, (GD, 256), BF16)
        tmp = {"m": carve(scr, 6144, 2, (2,)), "s1": carve(scr, 6146, 2, (2,)),
               "s2": carve(scr, 6148, 2, (2,)), "var": carve(scr, 6150, 2, (2,)),
               "rstd": carve(scr, 6152, 2, (2,)), "sq": carve(scr, 2048, 2048, (2 * D,)),
               "xn": None}
    psT = k.t["psT"]
    for c in range(nch):
        with nc.Block() as blk:
            @blk.sync
            def _(sync, c=c):
                sync.wait_ge(s_st.h, s_st.n)
                if c == 0:
                    sync.wait_ge(s_gd.h, s_gd.n)
                    sync.wait_ge(s_v.h, s_v.n)
                    sync.wait_ge(s_s.h, s_s.n)
                s_ld.inc(sync.dma_start(
                    out=xc[:], in_=x_in[c * RM:(c + 1) * RM, :]
                    .rearrange("(tt p) d -> p tt d", p=128)), 16)

            @blk.vector
            def _(vector, c=c):
                vector.wait_ge(s_ld.h, s_ld.n)
        ln_chunk(k, None, xc[:], xnbf[:], 2, D, s_v, s_s, tmp)
        with nc.Block() as blk2:
            @blk2.tensor
            def _(tensor, c=c):
                tensor.wait_ge(s_v.h, s_v.n)
                tensor.wait_ge(s_s.h, s_s.n)
                for tt in range(2):
                    for dj in range(GD):
                        ins = tensor.transpose(
                            psT[:, (tt % 2) * 1024 + dj * 128:(tt % 2) * 1024 + (dj + 1) * 128],
                            xnbf[:, tt, dj * 128:(dj + 1) * 128], ident[:])
                    s_pe.inc(ins, 1)

            @blk2.scalar
            def _(scalar, c=c):
                base = s_pe.n - 2
                for tt in range(2):
                    scalar.wait_ge(s_pe.h, base + tt + 1)
                    scalar.copy(xT[:, :, tt * 128:(tt + 1) * 128],
                                psT[:, (tt % 2) * 1024:(tt % 2) * 1024 + GD * 128]
                                .rearrange("p (a b) -> p a b", a=GD))
                    s_s.inc(scalar.drain(), 1)

            @blk2.gpsimd
            def _(gpsimd, c=c):
                gpsimd.wait_ge(s_s.h, s_s.n)
                s_st.inc(gpsimd.dma_start(
                    out=xlnT_dram.ap()[:, :, c * RM:(c + 1) * RM]
                    .rearrange("a p t -> p a t"), in_=xT[:]), 16)


def carve(buf, lo, n, shape, dt=F32):
    ap = buf[:, lo:lo + n]
    if dt == BF16:
        ap = ap.bitcast(BF16)
    if len(shape) == 1:
        return ap
    return ap.rearrange("p (a b) -> p a b", a=shape[0]) if len(shape) == 2 else ap.rearrange(
        "p (a b c) -> p a b c", a=shape[0], b=shape[1])


def build_mamba_dir(k, d, weights, xlnT_dram, x0_in, x1_dram, dbg=None, nch=NCHM):
    nc = k.nc
    fwd = (d == 'f')
    s_ld = k.t["s_ld"]; s_v = k.t["s_v"]; s_s = k.t["s_s"]; s_pe = k.t["s_pe"]
    s_st = k.t["s_st"]; s_g = k.t["s_g"]; s_gd = k.t["s_gd"]; s_bc = k.t["s_bc"]
    s_xln = k.t["s_xln"]; s_pn = [k.t["s_pn0"], k.t["s_pn1"]]
    s_rep = [k.t["s_rep0"], k.t["s_rep1"]]; s_wp = k.t["s_wp"]; s_x0 = k.t["s_x0"]
    psT = k.t["psT"]
    pb = [k.t[f"pb{i}"] for i in range(4)]
    ident = k.t["ident_sb"]

    xlnTc = k.t["mb_xlnTc"]; winp = [k.t["mb_winp0"], k.t["mb_winp1"]]
    upT = k.t["mb_upT"]; tail3 = k.t["mb_tail3"]; zT = k.t["mb_zT"]
    ucsT = k.t["mb_ucsT"]; dtT = k.t["mb_dtT"]
    xdblo = k.t["mb_xdblo"]
    dtpre = k.t["mb_dtpre"]
    wdtT = k.t["mb_wdtT"]
    sp_t1 = k.t["mb_spt1"]; sp_t2 = k.t["mb_spt2"]
    hcar = k.t["mb_hcar"]; ysT = k.t["mb_ysT"]; ypart = k.t["mb_ypart"]
    scr = k.t["mb_scr"]
    convw = k.t["mb_convw"]; convb = k.t["mb_convb"]; Dp = k.t["mb_Dp"]
    bdt = k.t["mb_bdt"]; A_sb = k.t["mb_A"]
    gR = k.t["mb_gR"]; bR = k.t["mb_bR"]; bfR = k.t["mb_bfR"]
    cc1 = k.t["mb_cc1"]
    yfbf = k.t["mb_yfbf"]
    lntmp = {"m": k.t["mb_lm"], "s1": k.t["mb_ls1"], "s2": k.t["mb_ls2"],
             "var": k.t["mb_lvar"], "rstd": k.t["mb_lrstd"],
             "sq": carve(k.t["mb_scr"], 12288, 2048, (2048,)),
             "xn": carve(k.t["mb_scr"], 14336, 2048, (2, D))}
    bc_scr = k.t["bc_scr"]

    dA = [carve(scr, 0, 2048, (QN, RM)), carve(scr, 10240, 2048, (QN, RM))]
    wB = carve(scr, 2048, 2048, (QN, RM))
    hh = carve(scr, 4096, 2048, (QN, RM))
    Brep = [carve(scr, 6144, 2048, (QN, RM)), carve(scr, 12288, 2048, (QN, RM))]
    Crep = [carve(scr, 8192, 2048, (QN, RM)), carve(scr, 14336, 2048, (QN, RM))]
    mo = carve(scr, 0, 2048, (2, D))
    x0c = carve(scr, 2048, 2048, (2, D))
    woutp = carve(scr, 4096, 4096, (GI, 512), BF16)
    wfusep = carve(scr, 8192, 2048, (GD, 512), BF16)
    molnbf = carve(scr, 10240, 1024, (2, D), BF16)
    folnT = carve(scr, 11264, 1024, (GD, RM), BF16)
    ucbf = carve(scr, 12288, 2048, (GI, RM), BF16)
    convtmp = carve(scr, 0, 4096, (GI, RM))
    gtmp = carve(scr, 4096, 4096, (GI, RM))

    WinT = weights[f"WinT_{d}"]
    WoutT = weights[f"WoutT_{d}"]; WfuseT = weights["WfuseT"]
    WxT = k.t["mb_wxT"]

    cord = list(range(nch)) if fwd else list(range(NCHM - 1, NCHM - 1 - nch, -1))
    pnl_done = []
    rep_done = [0, 0, 0, 0]
    wp_done = [0, 0]
    pad0 = 3 if fwd else 0

    for ci, c in enumerate(cord):
        t0 = c * RM
        with nc.Block() as blk:
            @blk.sync
            def _(sync, c=c, ci=ci):
                sync.wait_ge(s_s.h, s_s.n)
                sync.wait_ge(s_gd.h, s_gd.n)
                sync.wait_ge(s_st.h, s_st.n)
                s_xln.inc(sync.dma_start(
                    out=xlnTc[:], in_=xlnT_dram.ap()[:, :, t0:t0 + RM]
                    .rearrange("a p t -> p a t")), 16)
                pnl_done.clear()
                pe_base = s_pe.n
                for et in range(32):
                    if et >= 2:
                        sync.wait_ge(s_pe.h, pe_base + et - 1)
                    s_pn[et % 2].inc(sync.dma_start(
                        out=winp[et % 2][:], in_=WinT.ap()[:, :, et * 128:(et + 1) * 128]
                        .rearrange("a p t -> p a t")), 16)
                    pnl_done.append(s_pn[et % 2].n)

            @blk.tensor
            def _(tensor, c=c):
                s_base = s_s.n
                tensor.wait_ge(s_xln.h, s_xln.n)
                for et in range(32):
                    tensor.wait_ge(s_pn[et % 2].h, pnl_done[et])
                    if et >= 2:
                        tensor.wait_ge(s_s.h, s_base + et - 1)
                    for kd in range(GD):
                        ins = tensor.matmul(pb[et % 2][0:128, 0:RM],
                                            winp[et % 2][:, kd, :],
                                            xlnTc[:, kd, :],
                                            start=(kd == 0), stop=(kd == GD - 1))
                    s_pe.inc(ins, 1)

            @blk.scalar
            def _(scalar, c=c, ci=ci):
                pe_base = s_pe.n - 32
                scalar.copy(upT[:, :, (0 if fwd else RM):(3 if fwd else RM + 3)][:, 0:GI, :],
                            tail3[:])
                for et in range(32):
                    scalar.wait_ge(s_pe.h, pe_base + et + 1)
                    if et < GI:
                        scalar.copy(upT[:, et, pad0:pad0 + RM], pb[et % 2][0:128, 0:RM])
                    else:
                        scalar.copy(zT[:, et - GI, :], pb[et % 2][0:128, 0:RM])
                    s_s.inc(scalar.drain(), 1)
                if fwd:
                    scalar.copy(tail3[:], upT[:, :, RM:RM + 3])
                else:
                    scalar.copy(tail3[:], upT[:, :, 0:3])
                s_s.inc(scalar.drain(), 1)

        if dbg is not None and "dbg_u_" + d in dbg:
            with nc.Block() as blkd:
                @blkd.gpsimd
                def _(gpsimd, c=c, t0=t0):
                    gpsimd.wait_ge(s_s.h, s_s.n)
                    s_gd.inc(gpsimd.dma_start(
                        out=dbg[f"dbg_u_{d}"].ap()[:, :, t0:t0 + RM].rearrange("a p t -> p a t"),
                        in_=upT[:, :, pad0:pad0 + RM]), 16)
                    s_gd.inc(gpsimd.dma_start(
                        out=dbg[f"dbg_z_{d}"].ap()[:, :, t0:t0 + RM].rearrange("a p t -> p a t"),
                        in_=zT[:]), 16)

        with nc.Block() as blk:
            @blk.gpsimd
            def _(gpsimd, c=c):
                gpsimd.wait_ge(s_s.h, s_s.n)
                ks = [2, 3]
                sel = (lambda kk: kk) if fwd else (lambda kk: 3 - kk)
                gpsimd.tensor_tensor(
                    ucsT[:], upT[:, :, 2:2 + RM],
                    convw[:, :, sel(2):sel(2) + 1].broadcast_to([128, GI, RM]), ALU.mult)
                gpsimd.drain()
                gpsimd.tensor_tensor(
                    gtmp, upT[:, :, 3:3 + RM],
                    convw[:, :, sel(3):sel(3) + 1].broadcast_to([128, GI, RM]), ALU.mult)
                gpsimd.drain()
                gpsimd.tensor_add(ucsT[:], ucsT[:], gtmp)
                s_g.inc(gpsimd.drain(), 1)

            @blk.vector
            def _(vector, c=c):
                vector.wait_ge(s_s.h, s_s.n)
                sel = (lambda kk: kk) if fwd else (lambda kk: 3 - kk)
                vector.tensor_tensor(
                    cc1[:], upT[:, :, 0:RM],
                    convw[:, :, sel(0):sel(0) + 1].broadcast_to([128, GI, RM]), ALU.mult)
                vector.drain()
                vector.tensor_tensor(
                    convtmp, upT[:, :, 1:1 + RM],
                    convw[:, :, sel(1):sel(1) + 1].broadcast_to([128, GI, RM]), ALU.mult)
                vector.drain()
                vector.tensor_add(cc1[:], cc1[:], convtmp)
                vector.wait_ge(s_g.h, s_g.n)
                vector.drain()
                vector.tensor_add(cc1[:], cc1[:], ucsT[:])
                vector.drain()
                vector.tensor_tensor(cc1[:], cc1[:],
                                     convb[:].unsqueeze(2).broadcast_to([128, GI, RM]),
                                     ALU.add)
                s_v.inc(vector.drain(), 1)

            @blk.scalar
            def _(scalar, c=c):
                scalar.wait_ge(s_v.h, s_v.n)
                scalar.activation(ucsT[:], cc1[:], AF.Silu)
                s_s.inc(scalar.drain(), 1)

        with nc.Block() as blk:
            @blk.vector
            def _(vector, c=c):
                vector.wait_ge(s_s.h, s_s.n)
                vector.tensor_copy(ucbf[:], ucsT[:])
                s_v.inc(vector.drain(), 1)

            @blk.tensor
            def _(tensor, c=c):
                tensor.wait_ge(s_v.h, s_v.n)
                for kg in range(GI):
                    ins = tensor.matmul(pb[2][0:128, 0:RM], WxT[:, kg, :], ucbf[:, kg, :],
                                        start=(kg == 0), stop=(kg == GI - 1))
                s_pe.inc(ins, 1)

            @blk.scalar
            def _(scalar, c=c):
                scalar.wait_ge(s_pe.h, s_pe.n)
                scalar.copy(dtpre[0:64, :], pb[2][0:64, 0:RM])
                scalar.copy(xdblo[64:128, :], pb[2][64:128, 0:RM])
                s_s.inc(scalar.drain(), 1)

            @blk.gpsimd
            def _(gpsimd, c=c):
                gpsimd.wait_ge(s_s.h, s_s.n)
                s_gd.inc(gpsimd.dma_start(out=bc_scr.ap()[:, :], in_=xdblo[64:128, :]), 16)

        with nc.Block() as blk:
            @blk.tensor
            def _(tensor, c=c):
                s_base = s_s.n
                for g in range(GI):
                    if g >= 2:
                        tensor.wait_ge(s_s.h, s_base + g - 1)
                    ins = tensor.matmul(pb[2 + g % 2][0:128, 0:RM],
                                        wdtT[0:64, g * 128:(g + 1) * 128],
                                        dtpre[0:64, :], start=True, stop=True)
                    s_pe.inc(ins, 1)

            @blk.scalar
            def _(scalar, c=c):
                pe_base = s_pe.n - GI
                v_base3 = s_v.n
                for g in range(GI):
                    scalar.wait_ge(s_pe.h, pe_base + g + 1)
                    if g >= 1:
                        scalar.wait_ge(s_v.h, v_base3 + g)
                    ps = pb[2 + g % 2][0:128, 0:RM]
                    scalar.activation(sp_t1[:], ps, AF.Abs, bias=bdt[:, g:g + 1])
                    scalar.drain()
                    scalar.activation(sp_t1[:], sp_t1[:], AF.Exp, scale=-1.0)
                    scalar.drain()
                    scalar.activation(sp_t1[:], sp_t1[:], AF.Ln, bias=1.0)
                    scalar.drain()
                    scalar.activation(sp_t2[:], ps, AF.Relu, bias=bdt[:, g:g + 1])
                    s_s.inc(scalar.drain(), 1)

            @blk.vector
            def _(vector, c=c):
                s_base = s_s.n - GI
                for g in range(GI):
                    vector.wait_ge(s_s.h, s_base + g + 1)
                    vector.tensor_add(dtT[:, g, :], sp_t1[:], sp_t2[:])
                    if g < GI - 1:
                        s_v.inc(vector.drain(), 1)
                vector.wait_ge(s_gd.h, s_gd.n)
                vector.drain()
                vector.tensor_mul(upT[:, :, 0:RM], dtT[:], ucsT[:])
                s_v.inc(vector.drain(), 1)

        wb_wT = upT[:, :, 0:RM]

        g_base = s_g.n
        s_base = s_s.n
        v_base = s_v.n
        ld_base = s_ld.n
        gd_base = s_gd.n
        with nc.Block() as blk:
            @blk.sync
            def _(sync, c=c):
                sync.wait_ge(s_gd.h, gd_base)
                for q in range(NQ):
                    if q >= 2:
                        sync.wait_ge(s_v.h, v_base + (q - 1) * GI)
                    s_rep[q % 2].inc(sync.dma_start(
                        out=Brep[q % 2][:],
                        in_=bc_scr.ap()[q * QN:(q + 1) * QN, :]
                        .unsqueeze(0).broadcast_to([128, QN, RM])), 16)
                    s_rep[q % 2].inc(sync.dma_start(
                        out=Crep[q % 2][:],
                        in_=bc_scr.ap()[NS + q * QN:NS + (q + 1) * QN, :]
                        .unsqueeze(0).broadcast_to([128, QN, RM])), 16)
                    rep_done[q] = s_rep[q % 2].n

            @blk.gpsimd
            def _(gpsimd, c=c):
                for q in range(NQ):
                    for g in range(GI):
                        u = q * GI + g
                        if u >= 2:
                            gpsimd.wait_ge(s_v.h, v_base + u - 1)
                        gpsimd.tensor_tensor(
                            dA[u % 2][:],
                            A_sb[:, g, q * QN:(q + 1) * QN].unsqueeze(2).broadcast_to([128, QN, RM]),
                            dtT[:, g, :].unsqueeze(1).broadcast_to([128, QN, RM]),
                            ALU.mult)
                        s_g.inc(gpsimd.drain(), 1)

            @blk.scalar
            def _(scalar, c=c):
                for q in range(NQ):
                    for g in range(GI):
                        u = q * GI + g
                        scalar.wait_ge(s_g.h, g_base + u + 1)
                        scalar.activation(dA[u % 2][:], dA[u % 2][:], AF.Exp)
                        s_s.inc(scalar.drain(), 1)

            @blk.vector
            def _(vector, c=c):
                for q in range(NQ):
                    vector.wait_ge(s_rep[q % 2].h, rep_done[q])
                    for g in range(GI):
                        u = q * GI + g
                        vector.wait_ge(s_s.h, s_base + u + 1)
                        vector.tensor_tensor(
                            wB[:], wb_wT[:, g, :].unsqueeze(1).broadcast_to([128, QN, RM]),
                            Brep[q % 2][:], ALU.mult)
                        vector.drain()
                        for j in range(QN):
                            n_ = q * QN + j
                            if fwd:
                                vector.tensor_tensor_scan(
                                    out=hh[:, j, :], data0=dA[u % 2][:, j, :],
                                    data1=wB[:, j, :], initial=hcar[:, g, n_:n_ + 1],
                                    op0=ALU.mult, op1=ALU.add)
                            else:
                                vector.tensor_tensor_scan(
                                    out=hh[:, j, ::-1], data0=dA[u % 2][:, j, ::-1],
                                    data1=wB[:, j, ::-1], initial=hcar[:, g, n_:n_ + 1],
                                    op0=ALU.mult, op1=ALU.add)
                        vector.drain()
                        edge = RM - 1 if fwd else 0
                        vector.tensor_copy(hcar[:, g, q * QN:(q + 1) * QN],
                                           hh[:, :, edge:edge + 1].rearrange("p a b -> p (a b)"))
                        vector.tensor_tensor(wB[:], hh[:], Crep[q % 2][:], ALU.mult)
                        vector.drain()
                        if q == 0:
                            vector.reduce_sum(ysT[:, g, :], wB[:].transpose([0, 2, 1]), axis=AX.X)
                        else:
                            vector.reduce_sum(ypart[:], wB[:].transpose([0, 2, 1]), axis=AX.X)
                            vector.drain()
                            vector.tensor_add(ysT[:, g, :], ysT[:, g, :], ypart[:])
                        s_v.inc(vector.drain(), 1)

        with nc.Block() as blk:
            @blk.scalar
            def _(scalar, c=c):
                scalar.wait_ge(s_v.h, s_v.n)
                scalar.activation(zT[:], zT[:], AF.Silu)
                s_s.inc(scalar.drain(), 1)

            @blk.vector
            def _(vector, c=c):
                vector.wait_ge(s_s.h, s_s.n)
                vector.tensor_tensor(cc1[:], ucsT[:],
                                     Dp[:].unsqueeze(2).broadcast_to([128, GI, RM]), ALU.mult)
                vector.drain()
                vector.tensor_add(cc1[:], cc1[:], ysT[:])
                vector.drain()
                vector.tensor_tensor(yfbf[:], cc1[:], zT[:], ALU.mult)
                s_v.inc(vector.drain(), 1)

        with nc.Block() as blk:
            @blk.sync
            def _(sync, c=c):
                sync.wait_ge(s_v.h, s_v.n)
                s_x0.inc(sync.dma_start(out=x0c, in_=x0_in[t0:t0 + RM, :]
                                        .rearrange("(tt p) d -> p tt d", p=128)), 16)

            @blk.tensor
            def _(tensor, c=c):
                pass

        with nc.Block() as blk:
            @blk.sync
            def _(sync, c=c):
                for ech in range(2):
                    if ech == 1:
                        sync.wait_ge(s_pe.h, s_pe.n + 2)
                    s_wp.inc(sync.dma_start(
                        out=woutp, in_=WoutT.ap()[:, :, ech * 512:(ech + 1) * 512]
                        .rearrange("a p t -> p a t")), 16)
                    wp_done[ech] = s_wp.n

            @blk.tensor
            def _(tensor, c=c):
                for ech in range(2):
                    tensor.wait_ge(s_wp.h, wp_done[ech])
                    for tsub in range(2):
                        for kg in range(GI):
                            ins = tensor.matmul(pb[ech * 2 + tsub][0:128, 0:512],
                                                yfbf[:, kg, tsub * 128:(tsub + 1) * 128],
                                                woutp[:, kg, :],
                                                start=(kg == 0), stop=(kg == GI - 1))
                        s_pe.inc(ins, 1)

            @blk.scalar
            def _(scalar, c=c):
                pe_base = s_pe.n - 4
                i = 0
                for ech in range(2):
                    for tsub in range(2):
                        i += 1
                        scalar.wait_ge(s_pe.h, pe_base + i)
                        scalar.copy(mo[:, tsub, ech * 512:(ech + 1) * 512],
                                    pb[ech * 2 + tsub][0:128, 0:512])
                        s_s.inc(scalar.drain(), 1)

        ln_chunk(k, None, mo, molnbf, 2, D, s_v, s_s, lntmp, gain=gR[:], bias=bR[:])
        with nc.Block() as blk:
            @blk.tensor
            def _(tensor, c=c):
                tensor.wait_ge(s_v.h, s_v.n)
                for tt in range(2):
                    for dj in range(GD):
                        ins = tensor.transpose(
                            psT[:, (tt % 2) * 1024 + dj * 128:(tt % 2) * 1024 + (dj + 1) * 128],
                            molnbf[:, tt, dj * 128:(dj + 1) * 128], ident[:])
                    s_pe.inc(ins, 1)

            @blk.scalar
            def _(scalar, c=c):
                base = s_pe.n - 2
                for tt in range(2):
                    scalar.wait_ge(s_pe.h, base + tt + 1)
                    scalar.copy(folnT[:, :, tt * 128:(tt + 1) * 128],
                                psT[:, (tt % 2) * 1024:(tt % 2) * 1024 + GD * 128]
                                .rearrange("p (a b) -> p a b", a=GD))
                    s_s.inc(scalar.drain(), 1)

        with nc.Block() as blk:
            @blk.sync
            def _(sync, c=c):
                for dch in range(2):
                    if dch == 1:
                        sync.wait_ge(s_pe.h, s_pe.n + 2)
                    s_wp.inc(sync.dma_start(
                        out=wfusep, in_=WfuseT.ap()[(0 if fwd else GD):(GD if fwd else 2 * GD),
                                                    :, dch * 512:(dch + 1) * 512]
                        .rearrange("a p t -> p a t")), 16)
                    wp_done[dch] = s_wp.n

            @blk.tensor
            def _(tensor, c=c):
                for dch in range(2):
                    tensor.wait_ge(s_wp.h, wp_done[dch])
                    for tsub in range(2):
                        for ke in range(GD):
                            ins = tensor.matmul(pb[dch * 2 + tsub][0:128, 0:512],
                                                folnT[:, ke, tsub * 128:(tsub + 1) * 128],
                                                wfusep[:, ke, :],
                                                start=(ke == 0), stop=(ke == GD - 1))
                        s_pe.inc(ins, 1)

            @blk.vector
            def _(vector, c=c):
                pe_base = s_pe.n - 4
                vector.wait_ge(s_x0.h, s_x0.n)
                i = 0
                for dch in range(2):
                    for tsub in range(2):
                        i += 1
                        vector.wait_ge(s_pe.h, pe_base + i)
                        vector.tensor_add(x0c[:, tsub, dch * 512:(dch + 1) * 512],
                                          x0c[:, tsub, dch * 512:(dch + 1) * 512],
                                          pb[dch * 2 + tsub][0:128, 0:512])
                if not fwd:
                    vector.drain()
                    for tsub in range(2):
                        vector.tensor_add(x0c[:, tsub, :], x0c[:, tsub, :], bfR[:])
                s_v.inc(vector.drain(), 1)

            @blk.gpsimd
            def _(gpsimd, c=c):
                gpsimd.wait_ge(s_v.h, s_v.n)
                s_gd.inc(gpsimd.dma_start(
                    out=x1_dram.ap()[t0:t0 + RM, :].rearrange("(tt p) d -> p tt d", p=128),
                    in_=x0c), 16)
                if dbg is not None:
                    s_gd.inc(gpsimd.dma_start(
                        out=dbg[f"dbg_uc_{d}"].ap()[:, :, t0:t0 + RM].rearrange("a p t -> p a t"),
                        in_=ucsT[:]), 16)
                    s_gd.inc(gpsimd.dma_start(
                        out=dbg[f"dbg_dt_{d}"].ap()[:, :, t0:t0 + RM].rearrange("a p t -> p a t"),
                        in_=dtT[:]), 16)
                    s_gd.inc(gpsimd.dma_start(
                        out=dbg[f"dbg_ys_{d}"].ap()[:, :, t0:t0 + RM].rearrange("a p t -> p a t"),
                        in_=ysT[:]), 16)

    with nc.Block() as blk:
        @blk.sync
        def _(sync):
            sync.wait_ge(s_gd.h, s_gd.n)


def build_weight_prep(k, casts, trans, ident):
    nc = k.nc
    s_ld = k.t["s_ld"]; s_v = k.t["s_v"]; s_s = k.t["s_s"]; s_pe = k.t["s_pe"]; s_st = k.t["s_st"]
    psT = k.t["psT"]
    tf = k.t["wp_f32"]; tb = k.t["wp_bf"]; to = k.t["wp_out"]
    ident = ident

    for (srch, dsth, rows, cols) in casts:
        for i in range(rows // 128):
            with nc.Block() as blk:
                @blk.sync
                def _(sync, i=i, srch=srch, cols=cols):
                    sync.wait_ge(s_v.h, s_v.n)
                    s_ld.inc(sync.dma_start(out=tf[:, 0:cols],
                                            in_=srch.ap()[i * 128:(i + 1) * 128, :]), 16)

                @blk.vector
                def _(vector, cols=cols):
                    vector.wait_ge(s_ld.h, s_ld.n)
                    vector.wait_ge(s_st.h, s_st.n)
                    vector.tensor_copy(tb[:, 0:cols], tf[:, 0:cols])
                    s_v.inc(vector.drain(), 1)

                @blk.gpsimd
                def _(gpsimd, i=i, dsth=dsth, cols=cols):
                    gpsimd.wait_ge(s_v.h, s_v.n)
                    s_st.inc(gpsimd.dma_start(out=dsth.ap()[i, :, 0:cols],
                                              in_=tb[:, 0:cols]), 16)

    for (srch, dsth, rows, cols) in trans:
        ncol = cols // 128
        for i in range(rows // 128):
            with nc.Block() as blk:
                @blk.sync
                def _(sync, i=i, srch=srch, cols=cols):
                    sync.wait_ge(s_v.h, s_v.n)
                    s_ld.inc(sync.dma_start(out=tf[:, 0:cols],
                                            in_=srch.ap()[i * 128:(i + 1) * 128, :]), 16)

                @blk.vector
                def _(vector, cols=cols):
                    vector.wait_ge(s_ld.h, s_ld.n)
                    vector.wait_ge(s_pe.h, s_pe.n)
                    vector.wait_ge(s_st.h, s_st.n)
                    vector.tensor_copy(tb[:, 0:cols], tf[:, 0:cols])
                    s_v.inc(vector.drain(), 1)

                @blk.tensor
                def _(tensor, ncol=ncol):
                    tensor.wait_ge(s_v.h, s_v.n)
                    s_base_t = s_s.n
                    for j in range(ncol):
                        if j >= 2:
                            tensor.wait_ge(s_s.h, s_base_t + j - 1)
                        ins = tensor.transpose(
                            psT[:, (j % 2) * 1024:(j % 2) * 1024 + 128],
                            tb[:, j * 128:(j + 1) * 128], ident[:])
                        s_pe.inc(ins, 1)

                @blk.scalar
                def _(scalar, ncol=ncol):
                    pe_base = s_pe.n - ncol
                    scalar.wait_ge(s_st.h, s_st.n)
                    for j in range(ncol):
                        scalar.wait_ge(s_pe.h, pe_base + j + 1)
                        scalar.copy(to[:, j * 128:(j + 1) * 128],
                                    psT[:, (j % 2) * 1024:(j % 2) * 1024 + 128])
                        s_s.inc(scalar.drain(), 1)

                @blk.gpsimd
                def _(gpsimd, i=i, dsth=dsth, ncol=ncol):
                    gpsimd.wait_ge(s_s.h, s_s.n)
                    s_st.inc(gpsimd.dma_start(
                        out=dsth.ap()[:, :, i * 128:(i + 1) * 128]
                        .rearrange("a p t -> p a t"),
                        in_=to[:, 0:ncol * 128].rearrange("p (a b) -> p a b", a=ncol)), 16)


def build_prep_wdt(k, srch, dsth, ident):
    nc = k.nc
    s_ld = k.t["s_ld"]; s_v = k.t["s_v"]; s_s = k.t["s_s"]; s_pe = k.t["s_pe"]; s_st = k.t["s_st"]
    psT = k.t["psT"]
    tf = k.t["wp_f32"]; tb = k.t["wp_bf"]; to = k.t["wp_out"]
    for i in range(16):
        with nc.Block() as blk:
            @blk.sync
            def _(sync, i=i):
                sync.wait_ge(s_v.h, s_v.n)
                s_ld.inc(sync.dma_start(out=tf[:, 0:64],
                                        in_=srch.ap()[i * 128:(i + 1) * 128, :]), 16)

            @blk.vector
            def _(vector):
                vector.wait_ge(s_ld.h, s_ld.n)
                vector.wait_ge(s_pe.h, s_pe.n)
                vector.tensor_copy(tb[:, 0:64], tf[:, 0:64])
                s_v.inc(vector.drain(), 1)

            @blk.tensor
            def _(tensor):
                tensor.wait_ge(s_v.h, s_v.n)
                tensor.wait_ge(s_s.h, s_s.n)
                s_pe.inc(tensor.transpose(psT[0:64, 0:128], tb[:, 0:64], k.t["ident_sb"][:]), 1)

            @blk.scalar
            def _(scalar):
                scalar.wait_ge(s_pe.h, s_pe.n)
                scalar.wait_ge(s_st.h, s_st.n)
                scalar.copy(to[0:64, 0:128], psT[0:64, 0:128])
                s_s.inc(scalar.drain(), 1)

            @blk.gpsimd
            def _(gpsimd, i=i):
                gpsimd.wait_ge(s_s.h, s_s.n)
                s_st.inc(gpsimd.dma_start(out=dsth.ap()[0:64, i * 128:(i + 1) * 128],
                                          in_=to[0:64, 0:128]), 16)


def load_pg_vec(k, eng, srch, dst_sb, sem):
    for g in range(16):
        sem.inc(eng.dma_start(out=dst_sb[:, g:g + 1],
                              in_=srch.ap()[g * 128:(g + 1) * 128].unsqueeze(1)), 16)


def build_mamba_dir_setup(k, d, ins):
    nc = k.nc
    s_ld = k.t["s_ld"]; s_v = k.t["s_v"]; s_s = k.t["s_s"]
    with nc.Block() as blk:
        @blk.sync
        def _(sync):
            sync.wait_ge(s_v.h, s_v.n)
            sync.wait_ge(s_s.h, s_s.n)
            s_ld.inc(sync.dma_start(
                out=k.t["mb_A"][:],
                in_=ins[f"L0_attn_a_Alog_{d}"].ap().rearrange("(g p) n -> p g n", p=128)), 16)
            s_ld.inc(sync.dma_start(
                out=k.t["mb_convw"][:],
                in_=ins[f"L0_attn_a_convw_{d}"].ap().rearrange("(g p) c -> p g c", p=128)), 16)
            load_pg_vec(k, sync, ins[f"L0_attn_a_convb_{d}"], k.t["mb_convb"], s_ld)
            load_pg_vec(k, sync, ins[f"L0_attn_a_Dp_{d}"], k.t["mb_Dp"], s_ld)
            load_pg_vec(k, sync, ins[f"L0_attn_a_bdt_{d}"], k.t["mb_bdt"], s_ld)
            s_ld.inc(sync.dma_start(
                out=k.t["mb_gR"][:],
                in_=ins[f"L0_attn_a_ln_g_{d}"].ap().unsqueeze(0).broadcast_to([128, D])), 16)
            s_ld.inc(sync.dma_start(
                out=k.t["mb_bR"][:],
                in_=ins[f"L0_attn_a_ln_b_{d}"].ap().unsqueeze(0).broadcast_to([128, D])), 16)
            if d == 'b':
                s_ld.inc(sync.dma_start(
                    out=k.t["mb_bfR"][:],
                    in_=ins["L0_attn_a_bfuse"].ap().unsqueeze(0).broadcast_to([128, D])), 16)
            s_ld.inc(sync.dma_start(
                out=k.t["mb_wdtT"][0:64, :],
                in_=k.t[f"WdtT_{d}"].ap()[:, :]), 16)
            s_ld.inc(sync.dma_start(
                out=k.t["mb_wxT"][:],
                in_=k.t[f"WxT_{d}"].ap().rearrange("a p b -> p a b")), 16)

        @blk.vector
        def _(vector):
            vector.wait_ge(s_ld.h, s_ld.n)
            vector.memset(k.t["mb_tail3"][:], 0.0)
            vector.memset(k.t["mb_hcar"][:], 0.0)
            s_v.inc(vector.drain(), 1)

        @blk.scalar
        def _(scalar):
            scalar.wait_ge(s_ld.h, s_ld.n)
            scalar.activation(k.t["mb_A"][:], k.t["mb_A"][:], AF.Exp)
            scalar.drain()
            scalar.mul(k.t["mb_A"][:], k.t["mb_A"][:], -1.0)
            s_s.inc(scalar.drain(), 1)


def alloc_mamba(k):
    k.sb("mb_xlnTc", [128, GD, RM], BF16)
    k.sb("mb_winp0", [128, GD, 128], BF16)
    k.sb("mb_winp1", [128, GD, 128], BF16)
    k.sb("mb_upT", [128, GI, RM + 3], F32)
    k.sb("mb_tail3", [128, GI, 3], F32)
    k.sb("mb_zT", [128, GI, RM], F32)
    k.sb("mb_ucsT", [128, GI, RM], F32)
    k.sb("mb_dtT", [128, GI, RM], F32)
    k.sb("mb_xdblo", [128, RM], F32)
    k.sb("mb_dtpre", [128, RM], BF16)
    k.sb("mb_wdtT", [128, DI], BF16)
    k.sb("mb_wxT", [128, GI, 128], BF16)
    k.sb("mb_spt1", [128, RM], F32)
    k.sb("mb_spt2", [128, RM], F32)
    k.sb("mb_hcar", [128, GI, NS], F32)
    k.sb("mb_ysT", [128, GI, RM], F32)
    k.sb("mb_ypart", [128, RM], F32)
    k.sb("mb_scr", [128, 16384], F32)
    k.sb("mb_convw", [128, GI, 4], F32)
    k.sb("mb_convb", [128, GI], F32)
    k.sb("mb_Dp", [128, GI], F32)
    k.sb("mb_bdt", [128, GI], F32)
    k.sb("mb_A", [128, GI, NS], F32)
    k.sb("mb_gR", [128, D], F32)
    k.sb("mb_bR", [128, D], F32)
    k.sb("mb_bfR", [128, D], F32)
    k.sb("mb_cc1", [128, GI, RM], F32)
    k.sb("mb_yfbf", [128, GI, RM], BF16)
    k.sb("mb_lm", [128, 2]); k.sb("mb_ls1", [128, 2]); k.sb("mb_ls2", [128, 2])
    k.sb("mb_lvar", [128, 2]); k.sb("mb_lrstd", [128, 2])



MAMBA_W_NAMES = [
    "Win_f", "convw_f", "convb_f", "Wx_f", "Wdt_f", "bdt_f", "Alog_f", "Dp_f",
    "Wout_f", "ln_g_f", "ln_b_f",
    "Win_b", "convw_b", "convb_b", "Wx_b", "Wdt_b", "bdt_b", "Alog_b", "Dp_b",
    "Wout_b", "ln_g_b", "ln_b_b", "Wfuse", "bfuse",
]


def build_program_mamba(with_b=True, dbg=True, nch=NCHM):
    nc = bass.Bass("TRN2", target_bir_lowering=False)
    es = ExitStack()
    k = K(nc, es)
    ins = {}
    ins["x"] = k.dram("x", [T, D], F32, kind="ExternalInput")
    for nm in MAMBA_W_NAMES:
        full = "L0_attn_a_" + nm
        shp = {"Win": [2 * DI, D], "convw": [DI, 4], "convb": [DI], "Wx": [128, DI],
               "Wdt": [DI, 64], "bdt": [DI], "Alog": [DI, NS], "Dp": [DI],
               "Wout": [D, DI], "ln_g": [D], "ln_b": [D], "Wfuse": [D, 2 * DI // 2],
               "bfuse": [D]}[nm.rsplit("_", 1)[0] if nm not in ("Wfuse", "bfuse") else nm]
        ins[full] = k.dram(full, shp, F32, kind="ExternalInput")
    ident_in = k.dram("ident", [128, 128], BF16, kind="ExternalInput")

    k.dram("xlnT", [GD, 128, T], BF16)
    k.dram("x1", [T, D], F32, kind="ExternalOutput")
    k.dram("bc_scr", [64, RM], F32)
    for dd in "fb":
        k.dram(f"WinT_{dd}", [GD, 128, 2 * DI], BF16)
        k.dram(f"WoutT_{dd}", [GI, 128, D], BF16)
        k.dram(f"WxT_{dd}", [GI, 128, 128], BF16)
        k.dram(f"WdtT_{dd}", [64, DI], BF16)
    k.dram("WfuseT", [GI, 128, D], BF16)
    dbg_t = None
    if dbg:
        dbg_t = {}
        for dd in (("f", "b") if with_b else ("f",)):
            dbg_t[f"dbg_uc_{dd}"] = k.dram(f"dbg_uc_{dd}", [GI, 128, T], F32, kind="ExternalOutput")
            dbg_t[f"dbg_dt_{dd}"] = k.dram(f"dbg_dt_{dd}", [GI, 128, T], F32, kind="ExternalOutput")
            dbg_t[f"dbg_ys_{dd}"] = k.dram(f"dbg_ys_{dd}", [GI, 128, T], F32, kind="ExternalOutput")
            dbg_t[f"dbg_u_{dd}"] = k.dram(f"dbg_u_{dd}", [GI, 128, T], F32, kind="ExternalOutput")
            dbg_t[f"dbg_z_{dd}"] = k.dram(f"dbg_z_{dd}", [GI, 128, T], F32, kind="ExternalOutput")
        dbg_t["dbg_panel"] = k.dram("dbg_panel", [128, GD, 128], BF16, kind="ExternalOutput")
        dbg_t["dbg_xlnTc"] = k.dram("dbg_xlnTc", [128, GD, RM], BF16, kind="ExternalOutput")

    with es:
        for s in ["s_ld", "s_v", "s_s", "s_pe", "s_st", "s_g", "s_gd", "s_bc",
                  "s_xln", "s_pn0", "s_pn1", "s_rep0", "s_rep1", "s_wp", "s_x0",
                  "s_qk0", "s_qk1", "s_qp0", "s_qp1", "s_pf0", "s_pf1",
                  "s_sc0", "s_sc1"]:
            k.sem(s)
        psT = k.ps("psT", [128, 2048], BF16)
        for i in range(6):
            k.ps(f"pb{i}", [128, 512], F32)
        ident_sb = k.sb("ident_sb", [128, 128], BF16)
        k.sb("identf", [128, 128], F32)
        k.sb("epsT", [128, 1], F32)
        k.push_scope()
        k.sb("wp_f32", [128, 4096], F32)
        k.sb("wp_bf", [128, 4096], BF16)
        k.sb("wp_out", [128, 4096], BF16)

        with nc.Block() as blk:
            @blk.sync
            def _(sync):
                k.t["s_ld"].inc(sync.dma_start(out=ident_sb[:], in_=ident_in.ap()[:, :]), 16)

            @blk.vector
            def _(vector):
                vector.wait_ge(k.t["s_ld"].h, k.t["s_ld"].n)
                vector.memset(k.t["epsT"][:], EPS)
                vector.tensor_copy(k.t["identf"][:], ident_sb[:])
                k.t["s_v"].inc(vector.drain(), 1)

        dirs = ("f", "b") if with_b else ("f",)
        trans = []
        for dd in dirs:
            trans += [(ins[f"L0_attn_a_Win_{dd}"], k.t[f"WinT_{dd}"], 2 * DI, D),
                      (ins[f"L0_attn_a_Wout_{dd}"], k.t[f"WoutT_{dd}"], D, DI),
                      (ins[f"L0_attn_a_Wx_{dd}"], k.t[f"WxT_{dd}"], 128, DI)]
        trans += [(ins["L0_attn_a_Wfuse"], k.t["WfuseT"], D, DI)]
        build_weight_prep(k, [], trans, ident_sb)
        for dd in dirs:
            build_prep_wdt(k, ins[f"L0_attn_a_Wdt_{dd}"], k.t[f"WdtT_{dd}"], ident_sb)

        build_ln_prepass(k, ins["x"].ap(), k.t["xlnT"], ident_sb, nch=nch)
        k.pop_scope()

        alloc_mamba(k)
        weights = {f"WinT_{dd}": k.t[f"WinT_{dd}"] for dd in dirs}
        weights.update({f"WoutT_{dd}": k.t[f"WoutT_{dd}"] for dd in dirs})
        weights["WfuseT"] = k.t["WfuseT"]
        for dd in dirs:
            build_mamba_dir_setup(k, dd, ins)
            build_mamba_dir(k, dd, weights, k.t["xlnT"],
                            ins["x"].ap() if dd == 'f' else k.t["x1"].ap(),
                            k.t["x1"], dbg=dbg_t, nch=nch)

        with nc.Block() as blk:
            @blk.sync
            def _(sync):
                sync.wait_ge(k.t["s_gd"].h, k.t["s_gd"].n)
                sync.wait_ge(k.t["s_st"].h, k.t["s_st"].n)

    return nc


def alloc_ffn(k, tag):
    t = {}
    def sb(name, shape, dt=F32):
        t[name] = k.sb(f"{tag}_{name}", shape, dt)
        return t[name]
    sb("hT", [128, 2, GD, 512], BF16)
    sb("x1c", [128, 4, D], F32)
    sb("qT", [128, 4, 512], BF16)
    sb("keysT", [128, 8, 64], BF16)
    sb("fscr", [128, 8192], F32)
    sb("sxy", [128, 4, 2, 4, 8], F32)
    sb("ixyu", [128, 4, 2, 4, 8], U32)
    sb("ixyf", [128, 4, 2, 4, 8], F32)
    sb("sc8", [128, 4, 4, 8], F32)
    sb("pku", [128, 4, 4, 8], U32)
    sb("pkf", [128, 4, 4, 8], F32)
    sb("idx8f", [128, 4, 4, 8], F32)
    sb("idxi", [128, 2, 4, 4, 8], I32)
    sb("sm", [128, 4, 4, 8], F32)
    sb("smbf", [128, 2, 4, 4, 8], BF16)
    sb("mx8", [128, 16], F32)
    sb("nm8", [128, 16], F32)
    sb("sum8", [128, 16], F32)
    sb("rcp8", [128, 16], F32)
    sb("siluWT", [128, 32, 512], BF16)
    t["gate"] = t["siluWT"]
    sb("Ssum", [128, 32, 512], BF16)
    sb("Stmp", [128, 16, 512], BF16)
    sb("pnA0", [128, GD, 256], BF16)
    sb("pnA1", [128, GD, 256], BF16)
    sb("pnB0", [128, GD, 256], BF16)
    sb("pnB1", [128, GD, 256], BF16)
    sb("pnC0", [128, 1024], BF16)
    sb("pnC1", [128, 1024], BF16)
    sb("etmp", [128, 512], F32)
    sb("bgc", [128, 32], F32)
    sb("buc", [128, 32], F32)
    sb("bdR", [128, D], F32)
    sb("iota64", [128, 64], F32)
    sb("tlocf", [128, 4], F32)
    sb("zbf", [128, 8], BF16)
    sb("lm", [128, 4]); sb("ls1", [128, 4]); sb("ls2", [128, 4])
    sb("lvar", [128, 4]); sb("lrstd", [128, 4])
    sb("pnF0", [128, GD, 128], BF16)
    sb("pnF1", [128, GD, 128], BF16)
    return t


def build_ffn_setup(k, t, ins, iotain):
    nc = k.nc
    s_ld = k.t["s_ld"]; s_v = k.t["s_v"]; s_s = k.t["s_s"]
    with nc.Block() as blk:
        @blk.sync
        def _(sync):
            sync.wait_ge(s_v.h, s_v.n)
            sync.wait_ge(s_s.h, s_s.n)
            sync.wait_ge(k.t["s_st"].h, k.t["s_st"].n)
            s_ld.inc(sync.dma_start(out=t["keysT"][0:64, :, :],
                                    in_=k.t["keysT_dram"].ap().rearrange("p h n q -> n (p h) q")), 16)
            s_ld.inc(sync.dma_start(out=t["keysT"][64:128, :, :],
                                    in_=k.t["keysT_dram"].ap().rearrange("p h n q -> n (p h) q")), 16)
            for g in range(32):
                s_ld.inc(sync.dma_start(out=t["bgc"][:, g:g + 1],
                                        in_=ins["L0_ffn_bg"].ap()[g * 128:(g + 1) * 128].unsqueeze(1)), 16)
                s_ld.inc(sync.dma_start(out=t["buc"][:, g:g + 1],
                                        in_=ins["L0_ffn_bu"].ap()[g * 128:(g + 1) * 128].unsqueeze(1)), 16)
            s_ld.inc(sync.dma_start(out=t["bdR"][:],
                                    in_=ins["L0_ffn_bd"].ap().unsqueeze(0).broadcast_to([128, D])), 16)
            s_ld.inc(sync.dma_start(out=t["iota64"][:],
                                    in_=iotain.ap()[:, 0:64]), 16)
            s_ld.inc(sync.dma_start(out=t["tlocf"][:],
                                    in_=iotain.ap()[:, 64:68]), 16)

        @blk.vector
        def _(vector):
            vector.wait_ge(s_ld.h, s_ld.n)
            vector.memset(t["zbf"][:], 0.0)
            s_v.inc(vector.drain(), 1)


def build_ffn(k, t, ins, x_in, x_out, Sh_dram, first, dbg=None, hTd=None, tag="fp"):
    nc = k.nc
    s_ld = k.t["s_ld"]; s_v = k.t["s_v"]; s_s = k.t["s_s"]; s_pe = k.t["s_pe"]
    s_st = k.t["s_st"]; s_g = k.t["s_g"]; s_gd = k.t["s_gd"]
    s_xln = k.t["s_xln"]; s_pn = [k.t["s_pn0"], k.t["s_pn1"]]
    s_pf = [k.t["s_pf0"], k.t["s_pf1"]]
    s_sc = [k.t["s_sc0"], k.t["s_sc1"]]
    s_rep = [k.t["s_rep0"], k.t["s_rep1"]]; s_wp = k.t["s_wp"]; s_x0 = k.t["s_x0"]
    psT = k.t["psT"]
    pb = [k.t[f"pb{i}"] for i in range(6)]
    ident = k.t["ident_sb"]
    RF = 512
    NCF = T // RF
    simtok = carve(t["fscr"], 0, 2048, (4, 8, 64))
    simT = carve(t["fscr"], 4096, 4096, (8, 512))
    allsc = carve(t["fscr"], 2048, 256, (4, 64))
    allix = carve(t["fscr"], 2304, 256, (4, 64))
    oh = carve(t["fscr"], 4096, 2048, (4, 8, 64))
    pnl_done = []
    wp_done = [0] * 8

    WqH = ins["L0_ffn_Wq"]; WgH = k.t["Wg_bf"]; WuH = k.t["Wu_bf"]
    WdH = k.t["Wd_bf"]; ueH = k.t["ue_bf"]; deTH = k.t["deT"]
    WqbH = k.t["Wq_bf"]

    f5_done = {}
    sc_done = {}
    be_v_done = {}
    be_pe_done = {}
    ht_tgt = {}
    x0_tgt = {}

    build_ln_prepass(k, x_in, hTd, ident, nch=NCHM, tag=tag, scr=t["fscr"])

    def front(c):
        par = c % 2
        t0 = c * RF
        hTv = t["hT"][:, par]
        with nc.Block() as blk:
            @blk.sync
            def _(sync, c=c, par=par):
                if c == 0:
                    sync.wait_ge(s_v.h, s_v.n)
                    sync.wait_ge(s_s.h, s_s.n)
                    sync.wait_ge(s_st.h, s_st.n)
                if c >= 2:
                    sync.wait_ge(s_pe.h, be_pe_done[c - 2])
                s_xln.inc(sync.dma_start(
                    out=hTv[:], in_=hTd.ap()[:, :, t0:t0 + RF]
                    .rearrange("a p t -> p a t")), 16)
                ht_tgt[c] = s_xln.n

        with nc.Block() as blk:
            @blk.sync
            def _(sync, c=c):
                sync.wait_ge(s_s.h, s_s.n)
                pe_base0 = s_pe.n
                pnl_done.clear()
                for m in range(4):
                    if m >= 2:
                        sync.wait_ge(s_pe.h, pe_base0 + m - 1)
                    s_pf[m % 2].inc(sync.dma_start(
                        out=(t["pnF0"] if m % 2 == 0 else t["pnF1"])[:],
                        in_=WqbH.ap()[:, :, m * 128:(m + 1) * 128]
                        .rearrange("a p t -> p a t")), 16)
                    pnl_done.append(s_pf[m % 2].n)

            @blk.tensor
            def _(tensor, c=c):
                s_base = s_s.n
                tensor.wait_ge(s_xln.h, ht_tgt[c])
                for m in range(4):
                    tensor.wait_ge(s_pf[m % 2].h, pnl_done[m])
                    if m >= 2:
                        tensor.wait_ge(s_s.h, s_base + m - 1)
                    for kd in range(GD):
                        ins_ = tensor.matmul(pb[4 + m % 2][0:128, 0:RF],
                                             (t["pnF0"] if m % 2 == 0 else t["pnF1"])[:, kd, :],
                                             hTv[:, kd, :],
                                             start=(kd == 0), stop=(kd == GD - 1))
                    s_pe.inc(ins_, 1)

            @blk.scalar
            def _(scalar, c=c):
                base = s_pe.n - 4
                for m in range(4):
                    scalar.wait_ge(s_pe.h, base + m + 1)
                    scalar.copy(t["qT"][:, m, :], pb[4 + m % 2][0:128, 0:RF])
                    s_s.inc(scalar.drain(), 1)

        with nc.Block() as blk:
            @blk.tensor
            def _(tensor, c=c):
                s_base = s_s.n
                tensor.wait_ge(s_s.h, s_base)
                for ph in range(8):
                    p_, hh = ph // 4, ph % 4
                    if ph >= 2:
                        tensor.wait_ge(s_s.h, s_base + (ph - 2) + 1)
                    mtile = 2 * p_ + hh // 2
                    rbase = (hh % 2) * 64
                    ins_ = tensor.matmul(pb[4 + ph % 2][0:64, 0:RF],
                                         t["keysT"][rbase:rbase + 64, ph, :],
                                         t["qT"][rbase:rbase + 64, mtile, :],
                                         start=True, stop=True)
                    s_pe.inc(ins_, 1)

            @blk.scalar
            def _(scalar, c=c):
                base = s_pe.n - 8
                for ph in range(8):
                    scalar.wait_ge(s_pe.h, base + ph + 1)
                    if ph == 0 and c >= 1:
                        scalar.wait_ge(s_v.h, f5_done[c - 1])
                    scalar.copy(simT[0:64, ph, :], pb[4 + ph % 2][0:64, 0:RF])
                    s_s.inc(scalar.drain(), 1)

        with nc.Block() as blk:
            @blk.tensor
            def _(tensor, c=c):
                tensor.wait_ge(s_s.h, s_s.n)
                s_base_t = s_s.n
                grp = 0
                for tsub in range(4):
                    for phg in range(2):
                        if grp >= 2:
                            tensor.wait_ge(s_s.h, s_base_t + grp - 1)
                        for j in range(4):
                            ph = phg * 4 + j
                            ins_ = tensor.transpose(
                                pb[4 + grp % 2][0:128, j * 64:(j + 1) * 64],
                                simT[0:64, ph, tsub * 128:(tsub + 1) * 128],
                                k.t["identf"][0:64, 0:64])
                        s_pe.inc(ins_, 1)
                        grp += 1

            @blk.scalar
            def _(scalar, c=c):
                base = s_pe.n - 8
                grp = 0
                for tsub in range(4):
                    for phg in range(2):
                        scalar.wait_ge(s_pe.h, base + grp + 1)
                        scalar.copy(simtok[:, tsub, phg * 4:(phg + 1) * 4, :],
                                    pb[4 + grp % 2][0:128, 0:256]
                                    .rearrange("p (a b) -> p a b", a=4))
                        s_s.inc(scalar.drain(), 1)
                        grp += 1

        with nc.Block() as blk:
            @blk.vector
            def _(vector, c=c, par=par):
                vector.wait_ge(s_s.h, s_s.n)
                vector.wait_ge(s_sc[par].h, s_sc[par].n)
                for tsub in range(4):
                    for ph in range(8):
                        p_, hh = ph // 4, ph % 4
                        vector.max(t["sxy"][:, tsub, p_, hh, :],
                                   simtok[:, tsub, ph, :])
                vector.drain()
                for tsub in range(4):
                    for ph in range(8):
                        p_, hh = ph // 4, ph % 4
                        vector.max_index(t["ixyu"][:, tsub, p_, hh, :],
                                         t["sxy"][:, tsub, p_, hh, :],
                                         simtok[:, tsub, ph, :])
                vector.drain()
                vector.tensor_copy(t["ixyf"][:], t["ixyu"][:].bitcast(I32))
                vector.drain()
                s_v.inc(vector.drain(), 1)
                for tsub in range(4):
                    for hh in range(4):
                        vector.tensor_tensor(
                            allsc[:, hh, :].rearrange("p (i j) -> p i j", i=8),
                            t["sxy"][:, tsub, 0, hh, :].unsqueeze(2).broadcast_to([128, 8, 8]),
                            t["sxy"][:, tsub, 1, hh, :].unsqueeze(1).broadcast_to([128, 8, 8]),
                            ALU.add)
                    for hh in range(4):
                        vector.scalar_tensor_tensor(
                            out=allix[:, hh, :].rearrange("p (i j) -> p i j", i=8),
                            in0=t["ixyf"][:, tsub, 0, hh, :].unsqueeze(2).broadcast_to([128, 8, 8]),
                            scalar=64.0,
                            in1=t["ixyf"][:, tsub, 1, hh, :].unsqueeze(1).broadcast_to([128, 8, 8]),
                            op0=ALU.mult, op1=ALU.add)
                    vector.drain()
                    for hh in range(4):
                        vector.max(t["sc8"][:, tsub, hh, :], allsc[:, hh, :])
                    vector.drain()
                    for hh in range(4):
                        vector.max_index(t["pku"][:, tsub, hh, :], t["sc8"][:, tsub, hh, :],
                                         allsc[:, hh, :])
                    vector.drain()
                    vector.tensor_copy(t["pkf"][:, tsub], t["pku"][:, tsub].bitcast(I32))
                    vector.drain()
                    for hh in range(4):
                        vector.tensor_tensor(
                            oh[:, hh, :, :],
                            t["pkf"][:, tsub, hh, :].unsqueeze(2).broadcast_to([128, 8, 64]),
                            t["iota64"][:].unsqueeze(1).broadcast_to([128, 8, 64]),
                            ALU.is_equal)
                    vector.drain()
                    for hh in range(4):
                        vector.tensor_tensor(
                            oh[:, hh, :, :], oh[:, hh, :, :],
                            allix[:, hh, :].unsqueeze(1).broadcast_to([128, 8, 64]),
                            ALU.mult)
                    vector.drain()
                    vector.reduce_sum(t["idx8f"][:, tsub].rearrange("p a i -> p (a i)"),
                                      oh[:, 0:4, :, :].rearrange("p a i j -> p (a i) j"),
                                      axis=AX.X)
                    vector.drain()
                vector.reduce_max(t["mx8"][:].rearrange("p (a b) -> p a b", a=16),
                                  t["sc8"][:, :, :, :].rearrange("p a h i -> p (a h) i"),
                                  axis=AX.X)
                vector.drain()
                vector.tensor_scalar(out=t["nm8"][:], in0=t["mx8"][:], scalar1=-1.0,
                                     scalar2=None, op0=ALU.mult)
                s_v.inc(vector.drain(), 1)

            @blk.scalar
            def _(scalar, c=c):
                scalar.wait_ge(s_v.h, s_v.n)
                for tsub in range(4):
                    for hh in range(4):
                        scalar.activation(t["sm"][:, tsub, hh, :], t["sc8"][:, tsub, hh, :],
                                          AF.Exp, bias=t["nm8"][:, tsub * 4 + hh:tsub * 4 + hh + 1],
                                          scale=1.0)
                s_s.inc(scalar.drain(), 1)

        with nc.Block() as blk:
            @blk.vector
            def _(vector, c=c, par=par):
                vector.wait_ge(s_s.h, s_s.n)
                vector.reduce_sum(t["sum8"][:].rearrange("p (a b) -> p a b", a=16),
                                  t["sm"][:, :, :, :].rearrange("p a h i -> p (a h) i"),
                                  axis=AX.X)
                vector.drain()
                vector.reciprocal(t["rcp8"][:], t["sum8"][:])
                vector.drain()
                vector.tensor_tensor(
                    t["sm"][:].rearrange("p a h i -> p (a h) i"),
                    t["sm"][:].rearrange("p a h i -> p (a h) i"),
                    t["rcp8"][:].unsqueeze(2).broadcast_to([128, 16, 8]), ALU.mult)
                vector.drain()
                vector.tensor_copy(t["smbf"][:, par].rearrange("p a h i -> p (a h i)"),
                                   t["sm"][:].rearrange("p a h i -> p (a h i)"))
                for tsub in range(4):
                    vector.scalar_tensor_tensor(
                        out=t["idx8f"][:, tsub], in0=t["idx8f"][:, tsub], scalar=512.0,
                        in1=t["tlocf"][:, tsub:tsub + 1].unsqueeze(2).broadcast_to([128, 4, 8]),
                        op0=ALU.mult, op1=ALU.add)
                vector.drain()
                vector.tensor_copy(t["idxi"][:, par].rearrange("p a h i -> p (a h i)"),
                                   t["idx8f"][:].rearrange("p a h i -> p (a h i)"))
                s_v.inc(vector.drain(), 1)
        f5_done[c] = s_v.n

        with nc.Block() as blk:
            @blk.gpsimd
            def _(gpsimd, c=c, par=par):
                gpsimd.wait_ge(s_v.h, s_v.n)
                for hh in range(4):
                    for tsub in range(4):
                        for i in range(8):
                            s_sc[par].inc(gpsimd.indirect_dma_start(
                                out=Sh_dram[par * 4 + hh].ap().rearrange("a b -> (a b)").unsqueeze(1),
                                out_offset=bass.IndirectOffsetOnAxis(
                                    t["idxi"][:, par, tsub, hh, i:i + 1].bitcast(U32), 0),
                                in_=t["smbf"][:, par, tsub, hh, i:i + 1],
                                in_offset=None), 16)
        sc_done[c] = s_sc[par].n

    def back(c):
        par = c % 2
        t0 = c * RF
        x1cv = t["x1c"]
        hTv = t["hT"][:, par]
        with nc.Block() as blk:
            @blk.sync
            def _(sync, c=c):
                pe_base0 = s_pe.n
                pnl_done.clear()
                for ep in range(16):
                    if ep >= 2:
                        sync.wait_ge(s_pe.h, pe_base0 + 2 * ep - 2)
                    s_pn[ep % 2].inc(sync.dma_start(
                        out=(t["pnA0"] if ep % 2 == 0 else t["pnA1"])[:],
                        in_=deTH.ap()[:, :, ep * 256:(ep + 1) * 256]
                        .rearrange("a p t -> p a t")), 16)
                    pnl_done.append(s_pn[ep % 2].n)

            @blk.tensor
            def _(tensor, c=c):
                s_base = s_s.n
                for ep in range(16):
                    tensor.wait_ge(s_pn[ep % 2].h, pnl_done[ep])
                    for half in range(2):
                        et = 2 * ep + half
                        if et < 2 and c >= 1:
                            tensor.wait_ge(s_v.h, be_v_done[c - 1])
                        if et >= 2:
                            tensor.wait_ge(s_s.h, s_base + et - 1)
                        for kd in range(GD):
                            ins_ = tensor.matmul(
                                pb[et % 2][0:128, 0:RF],
                                (t["pnA0"] if ep % 2 == 0 else t["pnA1"])[:, kd, half * 128:(half + 1) * 128],
                                hTv[:, kd, :],
                                start=(kd == 0), stop=(kd == GD - 1))
                        s_pe.inc(ins_, 1)

            @blk.scalar
            def _(scalar, c=c):
                pe_base = s_pe.n - 32
                for et in range(32):
                    scalar.wait_ge(s_pe.h, pe_base + et + 1)
                    scalar.activation(t["siluWT"][:, et, :], pb[et % 2][0:128, 0:RF], AF.Silu)
                    s_s.inc(scalar.drain(), 1)

        with nc.Block() as blk:
            @blk.sync
            def _(sync, c=c, par=par):
                sync.wait_ge(s_sc[par].h, sc_done[c])
                if c >= 1:
                    sync.wait_ge(s_pe.h, be_pe_done[c - 1])
                s_rep[0].inc(sync.dma_start(
                    out=t["Ssum"][:], in_=Sh_dram[par * 4].ap()
                    .rearrange("(a p) t -> p a t", p=128)), 16)
                wp_done[0] = s_rep[0].n
                vv = s_v.n
                for hh in range(1, 4):
                    for half in range(2):
                        if hh + half > 1:
                            sync.wait_ge(s_v.h, vv + (hh - 1) * 2 + half - 1 + 1)
                        s_rep[1].inc(sync.dma_start(
                            out=t["Stmp"][:], in_=Sh_dram[par * 4 + hh].ap()
                            .rearrange("(a p) t -> p a t", p=128)[:, half * 16:(half + 1) * 16, :]), 16)
                        wp_done[1 + (hh - 1) * 2 + half] = s_rep[1].n

            @blk.vector
            def _(vector, c=c):
                vector.wait_ge(s_rep[0].h, wp_done[0])
                i = 0
                for hh in range(1, 4):
                    for half in range(2):
                        i += 1
                        vector.wait_ge(s_rep[1].h, wp_done[i])
                        vector.tensor_add(t["Ssum"][:, half * 16:(half + 1) * 16, :],
                                          t["Ssum"][:, half * 16:(half + 1) * 16, :],
                                          t["Stmp"][:, 0:16, :])
                        s_v.inc(vector.drain(), 1)
                vector.wait_ge(s_s.h, s_s.n)
                vector.tensor_mul(t["Ssum"][:], t["Ssum"][:], t["siluWT"][:])
                s_v.inc(vector.drain(), 1)
        v_after_f7b = s_v.n

        with nc.Block() as blk:
            @blk.gpsimd
            def _(gpsimd, c=c, par=par, v_tgt=v_after_f7b):
                gpsimd.wait_ge(s_v.h, v_tgt)
                for hh in range(4):
                    for tsub in range(4):
                        for i in range(8):
                            s_sc[par].inc(gpsimd.indirect_dma_start(
                                out=Sh_dram[par * 4 + hh].ap().rearrange("a b -> (a b)").unsqueeze(1),
                                out_offset=bass.IndirectOffsetOnAxis(
                                    t["idxi"][:, par, tsub, hh, i:i + 1].bitcast(U32), 0),
                                in_=t["zbf"][:, 0:1],
                                in_offset=None), 16)

        with nc.Block() as blk:
            @blk.sync
            def _(sync, c=c):
                pe_base0 = s_pe.n
                pnl_done.clear()
                for ep in range(16):
                    if ep >= 2:
                        sync.wait_ge(s_pe.h, pe_base0 + 4 * ep - 4)
                    s_pn[ep % 2].inc(sync.dma_start(
                        out=(t["pnA0"] if ep % 2 == 0 else t["pnA1"])[:],
                        in_=WgH.ap()[:, :, ep * 256:(ep + 1) * 256]
                        .rearrange("a p t -> p a t")), 16)
                    s_pn[ep % 2].inc(sync.dma_start(
                        out=(t["pnB0"] if ep % 2 == 0 else t["pnB1"])[:],
                        in_=WuH.ap()[:, :, ep * 256:(ep + 1) * 256]
                        .rearrange("a p t -> p a t")), 16)
                    pnl_done.append(s_pn[ep % 2].n)

            @blk.tensor
            def _(tensor, c=c):
                s_base = s_s.n
                v_base = s_v.n
                for ep in range(16):
                    tensor.wait_ge(s_pn[ep % 2].h, pnl_done[ep])
                    for half in range(2):
                        dt_ = 2 * ep + half
                        if dt_ < 2:
                            tensor.wait_ge(s_s.h, s_base)
                        if dt_ >= 2:
                            tensor.wait_ge(s_s.h, s_base + dt_ - 1)
                            tensor.wait_ge(s_v.h, v_base + dt_ - 1)
                        for kd in range(GD):
                            ins_ = tensor.matmul(
                                pb[2 * (dt_ % 2)][0:128, 0:RF],
                                (t["pnA0"] if ep % 2 == 0 else t["pnA1"])[:, kd, half * 128:(half + 1) * 128],
                                hTv[:, kd, :],
                                start=(kd == 0), stop=(kd == GD - 1))
                        for kd in range(GD):
                            ins_ = tensor.matmul(
                                pb[2 * (dt_ % 2) + 1][0:128, 0:RF],
                                (t["pnB0"] if ep % 2 == 0 else t["pnB1"])[:, kd, half * 128:(half + 1) * 128],
                                hTv[:, kd, :],
                                start=(kd == 0), stop=(kd == GD - 1))
                        s_pe.inc(ins_, 2)

            @blk.scalar
            def _(scalar, c=c):
                pe_base = s_pe.n - 64
                v_base = s_v.n
                for dt_ in range(32):
                    scalar.wait_ge(s_pe.h, pe_base + 2 * (dt_ + 1))
                    if dt_ >= 1:
                        scalar.wait_ge(s_v.h, v_base + dt_)
                    scalar.activation(t["etmp"][:], pb[2 * (dt_ % 2)][0:128, 0:RF],
                                      AF.Silu, bias=t["bgc"][:, dt_:dt_ + 1])
                    s_s.inc(scalar.drain(), 1)

            @blk.vector
            def _(vector, c=c):
                s_base2 = s_s.n - 32
                for dt_ in range(32):
                    vector.wait_ge(s_s.h, s_base2 + dt_ + 1)
                    vector.scalar_tensor_tensor(
                        out=t["gate"][:, dt_, :], in0=pb[2 * (dt_ % 2) + 1][0:128, 0:RF],
                        scalar=t["buc"][:, dt_:dt_ + 1], in1=t["etmp"][:],
                        op0=ALU.add, op1=ALU.mult)
                    s_v.inc(vector.drain(), 1)

        with nc.Block() as blk:
            @blk.sync
            def _(sync, c=c):
                sync.wait_ge(s_gd.h, s_gd.n)
                s_x0.inc(sync.dma_start(out=x1cv[:], in_=x_in[t0:t0 + RF, :]
                                        .rearrange("(tt p) d -> p tt d", p=128)), 16)
                x0_tgt[c] = s_x0.n
                pe_base0 = s_pe.n
                pnl_done.clear()
                idx = 0
                for dch in range(2):
                    for kp in range(32):
                        if idx >= 2:
                            sync.wait_ge(s_pe.h, pe_base0 + 2 * idx - 2)
                        srcH = WdH if kp < 16 else ueH
                        kk = (2 * kp) % 32
                        s_pn[idx % 2].inc(sync.dma_start(
                            out=(t["pnC0"] if idx % 2 == 0 else t["pnC1"])[:]
                            .rearrange("p (a n) -> p a n", a=2),
                            in_=srcH.ap()[kk:kk + 2, :, dch * 512:(dch + 1) * 512]
                            .rearrange("a p n -> p a n")), 16)
                        pnl_done.append(s_pn[idx % 2].n)
                        idx += 1

            @blk.tensor
            def _(tensor, c=c):
                v_base = s_v.n
                idx = 0
                for dch in range(2):
                    if dch == 0:
                        tensor.wait_ge(s_v.h, v_base)
                    if dch == 1:
                        tensor.wait_ge(s_v.h, v_base + 1)
                    for kp in range(32):
                        tensor.wait_ge(s_pn[idx % 2].h, pnl_done[idx])
                        for sub in range(2):
                            kd = 2 * kp + sub
                            lhs = t["gate"] if kd < 32 else t["Ssum"]
                            kk = kd % 32
                            for tsub in range(4):
                                ins_ = tensor.matmul(
                                    pb[tsub][0:128, 0:512],
                                    lhs[:, kk, tsub * 128:(tsub + 1) * 128],
                                    (t["pnC0"] if idx % 2 == 0 else t["pnC1"])[:, sub * 512:(sub + 1) * 512],
                                    start=(kd == 0), stop=(kd == 63))
                            s_pe.inc(ins_, 1)
                        idx += 1

            @blk.vector
            def _(vector, c=c):
                pe_base = s_pe.n - 128
                vector.wait_ge(s_x0.h, x0_tgt[c])
                for dch in range(2):
                    vector.wait_ge(s_pe.h, pe_base + 64 * (dch + 1))
                    for tsub in range(4):
                        vector.tensor_add(x1cv[:, tsub, dch * 512:(dch + 1) * 512],
                                          x1cv[:, tsub, dch * 512:(dch + 1) * 512],
                                          pb[tsub][0:128, 0:512])
                    vector.drain()
                    for tsub in range(4):
                        vector.tensor_add(x1cv[:, tsub, dch * 512:(dch + 1) * 512],
                                          x1cv[:, tsub, dch * 512:(dch + 1) * 512],
                                          t["bdR"][:, dch * 512:(dch + 1) * 512])
                    s_v.inc(vector.drain(), 1)

            @blk.gpsimd
            def _(gpsimd, c=c):
                gpsimd.wait_ge(s_v.h, s_v.n)
                s_gd.inc(gpsimd.dma_start(
                    out=x_out[t0:t0 + RF, :].rearrange("(tt p) d -> p tt d", p=128),
                    in_=x1cv[:]), 16)
        be_pe_done[c] = s_pe.n
        be_v_done[c] = s_v.n

    for it in range(NCF + 1):
        if it < NCF:
            front(it)
        if it >= 1:
            back(it - 1)

    with nc.Block() as blk:
        @blk.sync
        def _(sync):
            sync.wait_ge(s_gd.h, s_gd.n)
            sync.wait_ge(s_sc[0].h, s_sc[0].n)
            sync.wait_ge(s_sc[1].h, s_sc[1].n)


def build_prep_keys(k, srch, dsth, ident):
    nc = k.nc
    s_ld = k.t["s_ld"]; s_v = k.t["s_v"]; s_s = k.t["s_s"]; s_pe = k.t["s_pe"]; s_st = k.t["s_st"]
    psT = k.t["psT"]
    tf = k.t["wp_f32"]; tb = k.t["wp_bf"]; to = k.t["wp_out"]
    for p_ in range(2):
        for h in range(4):
            with nc.Block() as blk:
                @blk.sync
                def _(sync, p_=p_, h=h):
                    sync.wait_ge(s_v.h, s_v.n)
                    s_ld.inc(sync.dma_start(out=tf[0:64, 0:64],
                                            in_=srch.ap()[h, :, p_, :]), 16)

                @blk.vector
                def _(vector):
                    vector.wait_ge(s_ld.h, s_ld.n)
                    vector.wait_ge(s_pe.h, s_pe.n)
                    vector.tensor_copy(tb[0:64, 0:64], tf[0:64, 0:64])
                    s_v.inc(vector.drain(), 1)

                @blk.tensor
                def _(tensor):
                    tensor.wait_ge(s_v.h, s_v.n)
                    tensor.wait_ge(s_s.h, s_s.n)
                    s_pe.inc(tensor.transpose(psT[0:64, 0:64], tb[0:64, 0:64],
                                              k.t["ident_sb"][0:64, 0:64]), 1)

                @blk.scalar
                def _(scalar):
                    scalar.wait_ge(s_pe.h, s_pe.n)
                    scalar.wait_ge(s_st.h, s_st.n)
                    scalar.copy(to[0:64, 0:64], psT[0:64, 0:64])
                    s_s.inc(scalar.drain(), 1)

                @blk.gpsimd
                def _(gpsimd, p_=p_, h=h):
                    gpsimd.wait_ge(s_s.h, s_s.n)
                    s_st.inc(gpsimd.dma_start(out=dsth.ap()[p_, h, :, :],
                                              in_=to[0:64, 0:64]), 16)


FFN_W_NAMES = ["Wg", "bg", "Wu", "bu", "Wd", "bd", "Wq", "keys", "down_embed", "up_embed"]


def build_zero_sh(k, Sh):
    nc = k.nc
    s_v = k.t["s_v"]; s_st = k.t["s_st"]
    zz = k.t["wp_bf"]
    with nc.Block() as blk:
        @blk.vector
        def _(vector):
            vector.memset(zz[:, 0:512], 0.0)
            s_v.inc(vector.drain(), 1)

        @blk.gpsimd
        def _(gpsimd):
            gpsimd.wait_ge(s_v.h, s_v.n)
            for hh in range(len(Sh)):
                for a in range(32):
                    s_st.inc(gpsimd.dma_start(
                        out=Sh[hh].ap().rearrange("(a p) t -> p a t", p=128)[:, a, :],
                        in_=zz[:, 0:512].bitcast(BF16)[:, 0:512]), 16)


def build_program(phases, dbg=True, nch=NCHM, x1_ext=False):
    nc = bass.Bass("TRN2", target_bir_lowering=False)
    es = ExitStack()
    k = K(nc, es)
    ins = {}
    ins["x"] = k.dram("x", [T, D], F32, kind="ExternalInput")
    for nm in MAMBA_W_NAMES:
        full = "L0_attn_a_" + nm
        shp = {"Win": [2 * DI, D], "convw": [DI, 4], "convb": [DI], "Wx": [128, DI],
               "Wdt": [DI, 64], "bdt": [DI], "Alog": [DI, NS], "Dp": [DI],
               "Wout": [D, DI], "ln_g": [D], "ln_b": [D], "Wfuse": [D, DI],
               "bfuse": [D]}[nm.rsplit("_", 1)[0] if nm not in ("Wfuse", "bfuse") else nm]
        ins[full] = k.dram(full, shp, F32, kind="ExternalInput")
    for nm in FFN_W_NAMES:
        full = "L0_ffn_" + nm
        shp = {"Wg": [D, 4096], "bg": [4096], "Wu": [D, 4096], "bu": [4096],
               "Wd": [4096, D], "bd": [D], "Wq": [D, 512], "keys": [4, 64, 2, 64],
               "down_embed": [4096, D], "up_embed": [4096, D]}[nm]
        ins[full] = k.dram(full, shp, F32, kind="ExternalInput")
    for nm, shp in [("W_attn", [D, 3 * D]), ("b_attn", [3 * D]), ("W_proj", [D, D]),
                    ("b_proj", [D]), ("temperature", [1])]:
        ins["L0_attn_b_" + nm] = k.dram("L0_attn_b_" + nm, shp, F32, kind="ExternalInput")
    ident_in = k.dram("ident", [128, 128], BF16, kind="ExternalInput")
    iotain = k.dram("iotain", [128, 84], F32, kind="ExternalInput")
    k.t["iotain_in"] = iotain
    k.t["iotaT_in"] = k.dram("iotaT", [T], F32, kind="ExternalInput")

    k.dram("xlnT", [GD, 128, T], BF16)
    if x1_ext:
        x1h = k.dram("x1", [T, D], F32, kind="ExternalInput")
    else:
        x1h = k.dram("x1", [T, D], F32, kind="ExternalOutput" if "ffn1" not in phases else "Internal")
    if "ffn1" not in phases and "attn" in phases:
        x2h = k.dram("x2", [T, D], F32, kind="ExternalInput")
    else:
        x2h = k.dram("x2", [T, D], F32,
                     kind="Internal" if "attn" in phases else "ExternalOutput")
    k.dram("bc_scr", [64, RM], F32)
    for dd in "fb":
        k.dram(f"WinT_{dd}", [GD, 128, 2 * DI], BF16)
        k.dram(f"WoutT_{dd}", [GI, 128, D], BF16)
        k.dram(f"WxT_{dd}", [GI, 128, 128], BF16)
        k.dram(f"WdtT_{dd}", [64, DI], BF16)
    k.dram("WfuseT", [GI, 128, D], BF16)
    k.dram("Wg_bf", [GD, 128, 4096], BF16)
    k.dram("Wu_bf", [GD, 128, 4096], BF16)
    k.dram("Wd_bf", [32, 128, D], BF16)
    k.dram("Wq_bf", [GD, 128, 512], BF16)
    k.dram("ue_bf", [32, 128, D], BF16)
    k.dram("deT", [GD, 128, 4096], BF16)
    k.dram("keysT_dram", [2, 4, 64, 64], BF16)
    Sh = [k.dram(f"Sh{i}", [4096, 512], BF16) for i in range(8)]
    k.dram("x2lnT", [GD, 128, T], BF16)
    k.dram("qkT", [16, 128, T], BF16)
    k.dram("Vtok", [T, D], BF16)
    k.dram("attnT", [GD, 128, T], BF16)
    k.dram("Wattn_bf", [GD, 128, 3 * D], BF16)
    k.dram("Wproj_bf", [GD, 128, D], BF16)
    x3h = k.dram("x3", [T, D], F32,
                 kind="Internal" if "ffn2" in phases else "ExternalOutput")
    x4h = k.dram("x4", [T, D], F32, kind="ExternalOutput") if "ffn2" in phases else None

    dbg_t = None
    if dbg:
        dbg_t = {}
        if "mamba" in phases:
            for dd in ("f", "b"):
                for nmm in ("uc", "dt", "ys", "u", "z"):
                    dbg_t[f"dbg_{nmm}_{dd}"] = k.dram(f"dbg_{nmm}_{dd}", [GI, 128, T], F32,
                                                      kind="ExternalOutput")
        if "ffn1" in phases:
            dbg_t["dbg_gate"] = k.dram("dbg_gate", [32, 128, T], BF16, kind="ExternalOutput")
            dbg_t["dbg_sfin"] = k.dram("dbg_sfin", [32, 128, T], BF16, kind="ExternalOutput")
            dbg_t["dbg_idx"] = k.dram("dbg_idx", [T, 32], F32, kind="ExternalOutput")
            dbg_t["dbg_sm"] = k.dram("dbg_sm", [T, 32], F32, kind="ExternalOutput")

    with es:
        for s in ["s_ld", "s_v", "s_s", "s_pe", "s_st", "s_g", "s_gd", "s_bc",
                  "s_xln", "s_pn0", "s_pn1", "s_rep0", "s_rep1", "s_wp", "s_x0",
                  "s_qk0", "s_qk1", "s_qp0", "s_qp1", "s_pf0", "s_pf1",
                  "s_sc0", "s_sc1"]:
            k.sem(s)
        psT = k.ps("psT", [128, 2048], BF16)
        for i in range(6):
            k.ps(f"pb{i}", [128, 512], F32)
        ident_sb = k.sb("ident_sb", [128, 128], BF16)
        k.sb("identf", [128, 128], F32)
        k.sb("epsT", [128, 1], F32)
        k.push_scope()
        k.sb("wp_f32", [128, 4096], F32)
        k.sb("wp_bf", [128, 4096], BF16)
        k.sb("wp_out", [128, 4096], BF16)

        with nc.Block() as blk:
            @blk.sync
            def _(sync):
                k.t["s_ld"].inc(sync.dma_start(out=ident_sb[:], in_=ident_in.ap()[:, :]), 16)

            @blk.vector
            def _(vector):
                vector.wait_ge(k.t["s_ld"].h, k.t["s_ld"].n)
                vector.memset(k.t["epsT"][:], EPS)
                vector.tensor_copy(k.t["identf"][:], ident_sb[:])
                k.t["s_v"].inc(vector.drain(), 1)

        trans = []
        casts = []
        if "mamba" in phases:
            for dd in "fb":
                trans += [(ins[f"L0_attn_a_Win_{dd}"], k.t[f"WinT_{dd}"], 2 * DI, D),
                          (ins[f"L0_attn_a_Wout_{dd}"], k.t[f"WoutT_{dd}"], D, DI),
                          (ins[f"L0_attn_a_Wx_{dd}"], k.t[f"WxT_{dd}"], 128, DI)]
            trans += [(ins["L0_attn_a_Wfuse"], k.t["WfuseT"], D, DI)]
        if "attn" in phases:
            casts += [(ins["L0_attn_b_W_attn"], k.t["Wattn_bf"], D, 3 * D),
                      (ins["L0_attn_b_W_proj"], k.t["Wproj_bf"], D, D)]
        if "ffn1" in phases:
            trans += [(ins["L0_ffn_down_embed"], k.t["deT"], 4096, D)]
            casts += [(ins["L0_ffn_Wg"], k.t["Wg_bf"], D, 4096),
                      (ins["L0_ffn_Wu"], k.t["Wu_bf"], D, 4096),
                      (ins["L0_ffn_Wd"], k.t["Wd_bf"], 4096, D),
                      (ins["L0_ffn_Wq"], k.t["Wq_bf"], D, 512),
                      (ins["L0_ffn_up_embed"], k.t["ue_bf"], 4096, D)]
        build_weight_prep(k, casts, trans, ident_sb)
        if "mamba" in phases:
            for dd in "fb":
                build_prep_wdt(k, ins[f"L0_attn_a_Wdt_{dd}"], k.t[f"WdtT_{dd}"], ident_sb)
        if "ffn1" in phases:
            build_prep_keys(k, ins["L0_ffn_keys"], k.t["keysT_dram"], ident_sb)
            build_zero_sh(k, Sh)

        if "mamba" in phases:
            build_ln_prepass(k, ins["x"].ap(), k.t["xlnT"], ident_sb, nch=nch)
        k.pop_scope()

        if "mamba" in phases:
            k.push_scope()
            alloc_mamba(k)
            weights = {}
            for dd in "fb":
                weights[f"WinT_{dd}"] = k.t[f"WinT_{dd}"]
                weights[f"WoutT_{dd}"] = k.t[f"WoutT_{dd}"]
            weights["WfuseT"] = k.t["WfuseT"]
            for dd in ("f", "b"):
                build_mamba_dir_setup(k, dd, ins)
                build_mamba_dir(k, dd, weights, k.t["xlnT"],
                                ins["x"].ap() if dd == 'f' else x1h.ap(),
                                x1h, dbg=dbg_t, nch=nch)
            k.pop_scope()

        if "ffn1" in phases:
            k.push_scope()
            t = alloc_ffn(k, "f1")
            build_ffn_setup(k, t, ins, iotain)
            build_ffn(k, t, ins, x1h.ap(), x2h.ap(), Sh, True, dbg=dbg_t,
                      hTd=k.t["x2lnT"], tag="fp1")
            k.pop_scope()

        if "attn" in phases:
            k.push_scope()
            ta = alloc_attn(k)
            build_attn(k, ta, ins, x2h.ap(), x3h.ap(), dbg=dbg_t)
            k.pop_scope()

        if "ffn2" in phases:
            k.push_scope()
            t2 = alloc_ffn(k, "f2")
            build_ffn_setup(k, t2, ins, iotain)
            build_ffn(k, t2, ins, x3h.ap(), x4h.ap(), Sh, False, dbg=None,
                      hTd=k.t["x2lnT"], tag="fp2")
            k.pop_scope()

        with nc.Block() as blk:
            @blk.sync
            def _(sync):
                sync.wait_ge(k.t["s_gd"].h, k.t["s_gd"].n)
                sync.wait_ge(k.t["s_st"].h, k.t["s_st"].n)

    return nc


ALIBI_START = 2.0 ** (-2.0 ** (-(np.log2(16) - 3)))
ALIBI_SLOPES = [ALIBI_START * ALIBI_START ** i for i in range(16)]


def alloc_attn(k):
    t = {}
    def sb(name, shape, dt=F32):
        t[name] = k.sb("at_" + name, shape, dt)
        return t[name]
    sb("xlnTc", [128, GD, 512], BF16)
    sb("x2c", [128, 4, D], F32)
    sb("xnbf", [128, 4, D], BF16)
    sb("brow", [1, 3 * D], F32)
    sb("browb", [1, 3 * D], BF16)
    sb("bprow", [1, D], F32)
    sb("bprowb", [1, D], BF16)
    sb("onesr", [1, 512], BF16)
    sb("qscale", [128, 1], F32)
    sb("traw", [128, 1], F32)
    sb("kposR", [128, T], F32)
    sb("tslope", [128, 16], F32)
    sb("alibiF", [128, T], F32)
    sb("scores_sb", [128, T], F32)
    sb("Qp", [128, 2, 128], BF16)
    sb("Kfull", [128, T], BF16)
    sb("Vfull", [128, 32, 64], BF16)
    sb("probs", [128, 2, T], BF16)
    sb("probsT_sb", [128, 2, 128], BF16)
    sb("pnq0", [128, GD, 128], BF16)
    sb("pnq1", [128, GD, 128], BF16)
    sb("pnv", [128, GD, 512], BF16)
    sb("qk_ev", [128, 2, 512], BF16)
    sb("vev", [128, 2, 4, 512], BF16)
    sb("mparts", [128, 32], F32)
    sb("lparts", [128, 32], F32)
    sb("masks", [128, 4, 512], F32)
    sb("mrow", [128, 1], F32)
    sb("btile", [128, 1], F32)
    sb("lrow", [128, 1], F32)
    sb("lpart", [128, 1], F32)
    sb("lrec", [128, 1], F32)
    sb("atile", [128, 2, 128], BF16)
    sb("lsq", [128, 4096], F32)
    sb("lm", [128, 4]); sb("ls1", [128, 4]); sb("ls2", [128, 4])
    sb("lvar", [128, 4]); sb("lrstd", [128, 4])
    return t


def build_attn(k, t, ins, x_in, x_out, dbg=None):
    nc = k.nc
    s_ld = k.t["s_ld"]; s_v = k.t["s_v"]; s_s = k.t["s_s"]; s_pe = k.t["s_pe"]
    s_st = k.t["s_st"]; s_g = k.t["s_g"]; s_gd = k.t["s_gd"]
    s_xln = k.t["s_xln"]; s_pn = [k.t["s_pn0"], k.t["s_pn1"]]
    s_rep = [k.t["s_rep0"], k.t["s_rep1"]]; s_wp = k.t["s_wp"]; s_x0 = k.t["s_x0"]
    psT = k.t["psT"]
    pb = [k.t[f"pb{i}"] for i in range(4)]
    ident = k.t["ident_sb"]
    x2lnT = k.t["x2lnT"]; qkT = k.t["qkT"]; Vtok = k.t["Vtok"]; attnT = k.t["attnT"]
    WattnH = k.t["Wattn_bf"]; WprojH = k.t["Wproj_bf"]
    lntmp = {"m": t["lm"], "s1": t["ls1"], "s2": t["ls2"], "var": t["lvar"],
             "rstd": t["lrstd"], "sq": t["lsq"][:, 0:4096], "xn": None}
    pnl_done = []
    wp_done = [0] * 8

    with nc.Block() as blk:
        @blk.sync
        def _(sync):
            sync.wait_ge(s_v.h, s_v.n)
            sync.wait_ge(s_s.h, s_s.n)
            sync.wait_ge(s_gd.h, s_gd.n)
            s_ld.inc(sync.dma_start(out=t["brow"][0:1, :],
                                    in_=ins["L0_attn_b_b_attn"].ap().unsqueeze(0)), 16)
            s_ld.inc(sync.dma_start(out=t["bprow"][0:1, :],
                                    in_=ins["L0_attn_b_b_proj"].ap().unsqueeze(0)), 16)
            s_ld.inc(sync.dma_start(out=t["qscale"][:],
                                    in_=ins["L0_attn_b_temperature"].ap()
                                    .unsqueeze(0).broadcast_to([128, 1])), 16)
            s_ld.inc(sync.dma_start(out=t["traw"][:],
                                    in_=ins["L0_attn_b_temperature"].ap()
                                    .unsqueeze(0).broadcast_to([128, 1])), 16)
            s_ld.inc(sync.dma_start(out=t["kposR"][:],
                                    in_=k.t["iotaT_in"].ap().unsqueeze(0)
                                    .broadcast_to([128, T])), 16)
            s_ld.inc(sync.dma_start(out=t["tslope"][:],
                                    in_=k.t["iotain_in"].ap()[:, 68:84]), 16)

        @blk.vector
        def _(vector):
            vector.wait_ge(s_ld.h, s_ld.n)
            vector.tensor_scalar(out=t["qscale"][:], in0=t["qscale"][:], scalar1=8.0,
                                 scalar2=None, op0=ALU.mult)
            vector.drain()
            vector.reciprocal(t["qscale"][:], t["qscale"][:])
            vector.drain()
            vector.memset(t["onesr"][0:1, :], 1.0)
            vector.memset(t["masks"][:], 0.0)
            vector.tensor_scalar(out=t["tslope"][:], in0=t["tslope"][:],
                                 scalar1=t["traw"][:, 0:1], scalar2=8.0,
                                 op0=ALU.mult, op1=ALU.mult)
            vector.drain()
            vector.tensor_copy(t["browb"][0:1, :], t["brow"][0:1, :])
            vector.tensor_copy(t["bprowb"][0:1, :], t["bprow"][0:1, :])
            s_v.inc(vector.drain(), 1)

    with nc.Block() as blk:
        @blk.gpsimd
        def _(gpsimd):
            gpsimd.wait_ge(s_v.h, s_v.n)
            for m_ in range(4):
                gpsimd.affine_select(t["masks"][:, m_, :], t["masks"][:, m_, :],
                                     pattern=[[-1, 512]], compare_op=ALU.is_ge,
                                     fill=-1e30, base=m_ * 128, channel_multiplier=1)
            s_g.inc(gpsimd.drain(), 1)
    g_tgt = s_g.n

    RF = 512
    for c in range(T // RF):
        t0 = c * RF
        with nc.Block() as blk:
            @blk.sync
            def _(sync, c=c):
                sync.wait_ge(s_st.h, s_st.n)
                sync.wait_ge(s_v.h, s_v.n)
                s_x0.inc(sync.dma_start(out=t["x2c"][:], in_=x_in[t0:t0 + RF, :]
                                        .rearrange("(tt p) d -> p tt d", p=128)), 16)

            @blk.vector
            def _(vector, c=c):
                vector.wait_ge(s_x0.h, s_x0.n)
        ln_chunk(k, None, t["x2c"][:], t["xnbf"][:], 4, D, s_v, s_s, lntmp)
        with nc.Block() as blk:
            @blk.tensor
            def _(tensor, c=c):
                tensor.wait_ge(s_v.h, s_v.n)
                s_base_t = s_s.n
                for tt in range(4):
                    if tt >= 2:
                        tensor.wait_ge(s_s.h, s_base_t + tt - 1)
                    for dj in range(GD):
                        ins_ = tensor.transpose(
                            psT[:, (tt % 2) * 1024 + dj * 128:(tt % 2) * 1024 + (dj + 1) * 128],
                            t["xnbf"][:, tt, dj * 128:(dj + 1) * 128], ident[:])
                    s_pe.inc(ins_, 1)

            @blk.scalar
            def _(scalar, c=c):
                base = s_pe.n - 4
                for tt in range(4):
                    scalar.wait_ge(s_pe.h, base + tt + 1)
                    scalar.copy(t["xlnTc"][:, :, tt * 128:(tt + 1) * 128],
                                psT[:, (tt % 2) * 1024:(tt % 2) * 1024 + GD * 128]
                                .rearrange("p (a b) -> p a b", a=GD))
                    s_s.inc(scalar.drain(), 1)

        with nc.Block() as blk:
            @blk.sync
            def _(sync, c=c):
                pe_base0 = s_pe.n
                pnl_done.clear()
                for ft in range(16):
                    if ft >= 2:
                        sync.wait_ge(s_pe.h, pe_base0 + ft - 1)
                    s_pn[ft % 2].inc(sync.dma_start(
                        out=(t["pnq0"] if ft % 2 == 0 else t["pnq1"])[:],
                        in_=WattnH.ap()[:, :, ft * 128:(ft + 1) * 128]
                        .rearrange("a p m -> p a m")), 16)
                    pnl_done.append(s_pn[ft % 2].n)

            @blk.tensor
            def _(tensor, c=c):
                s_base = s_s.n
                for ft in range(16):
                    tensor.wait_ge(s_pn[ft % 2].h, pnl_done[ft])
                    if ft >= 2:
                        tensor.wait_ge(s_s.h, s_base + ft - 1)
                    for kd in range(GD):
                        ins_ = tensor.matmul(pb[ft % 2][0:128, 0:RF],
                                             (t["pnq0"] if ft % 2 == 0 else t["pnq1"])[:, kd, :],
                                             t["xlnTc"][:, kd, :],
                                             start=(kd == 0), stop=False)
                    ins_ = tensor.matmul(pb[ft % 2][0:128, 0:RF],
                                         t["browb"][0:1, ft * 128:(ft + 1) * 128],
                                         t["onesr"][0:1, 0:RF],
                                         start=False, stop=True)
                    s_pe.inc(ins_, 1)

            @blk.scalar
            def _(scalar, c=c):
                pe_base = s_pe.n - 16
                qk_b = [k.t["s_qk0"].n, k.t["s_qk1"].n]
                for ft in range(16):
                    scalar.wait_ge(s_pe.h, pe_base + ft + 1)
                    if ft >= 2:
                        scalar.wait_ge(k.t[f"s_qk{ft % 2}"].h,
                                       qk_b[ft % 2] + 16 * (ft // 2))
                    else:
                        scalar.wait_ge(k.t[f"s_qk{ft % 2}"].h, qk_b[ft % 2])
                    scalar.copy(t["qk_ev"][:, ft % 2, :], pb[ft % 2][0:128, 0:RF])
                    s_s.inc(scalar.drain(), 1)

            @blk.gpsimd
            def _(gpsimd, c=c):
                s_base = s_s.n - 16
                for ft in range(16):
                    gpsimd.wait_ge(s_s.h, s_base + ft + 1)
                    k.t[f"s_qk{ft % 2}"].inc(gpsimd.dma_start(
                        out=qkT.ap()[ft, :, t0:t0 + RF], in_=t["qk_ev"][:, ft % 2, :]), 16)

        with nc.Block() as blk:
            @blk.sync
            def _(sync, c=c):
                pe_base0 = s_pe.n
                wp_done[0] = 0
                for vch in range(2):
                    if vch == 1:
                        sync.wait_ge(s_pe.h, pe_base0 + 4)
                    s_wp.inc(sync.dma_start(
                        out=t["pnv"][:], in_=WattnH.ap()[:, :, 2 * D + vch * 512:2 * D + (vch + 1) * 512]
                        .rearrange("a p m -> p a m")), 16)
                    wp_done[vch] = s_wp.n

            @blk.tensor
            def _(tensor, c=c):
                v_base = s_v.n
                for vch in range(2):
                    tensor.wait_ge(s_wp.h, wp_done[vch])
                    if vch == 1:
                        tensor.wait_ge(s_v.h, v_base + 4)
                    for tsub in range(4):
                        for kd in range(GD):
                            ins_ = tensor.matmul(pb[tsub][0:128, 0:512],
                                                 t["xlnTc"][:, kd, tsub * 128:(tsub + 1) * 128],
                                                 t["pnv"][:, kd, :],
                                                 start=(kd == 0), stop=False)
                        ins_ = tensor.matmul(pb[tsub][0:128, 0:512],
                                             t["onesr"][0:1, tsub * 128:(tsub + 1) * 128],
                                             t["browb"][0:1, 2 * D + vch * 512:2 * D + (vch + 1) * 512],
                                             start=False, stop=True)
                        s_pe.inc(ins_, 1)

            @blk.vector
            def _(vector, c=c):
                pe_base = s_pe.n - 8
                vector.wait_ge(s_gd.h, s_gd.n)
                i = 0
                for vch in range(2):
                    for tsub in range(4):
                        i += 1
                        vector.wait_ge(s_pe.h, pe_base + i)
                        vector.tensor_copy(t["vev"][:, vch, tsub, :],
                                           pb[tsub][0:128, 0:512])
                        s_v.inc(vector.drain(), 1)

            @blk.gpsimd
            def _(gpsimd, c=c):
                v_base = s_v.n - 8
                i = 0
                for vch in range(2):
                    for tsub in range(4):
                        i += 1
                        gpsimd.wait_ge(s_v.h, v_base + i)
                        s_gd.inc(gpsimd.dma_start(
                            out=Vtok.ap()[t0 + tsub * 128:t0 + (tsub + 1) * 128,
                                          vch * 512:(vch + 1) * 512],
                            in_=t["vev"][:, vch, tsub, :]), 16)

    with nc.Block() as blk:
        @blk.sync
        def _(sync):
            sync.wait_ge(s_gd.h, s_gd.n)
            sync.wait_ge(k.t["s_qk0"].h, k.t["s_qk0"].n)
            sync.wait_ge(k.t["s_qk1"].h, k.t["s_qk1"].n)

    u_scores_done = []
    u_T_done = []
    u_copies_done = []
    u_atile_done = []
    s_qk = [k.t["s_qp0"], k.t["s_qp1"]]

    for h in range(16):
        ktile_f = 8 + h // 2
        krow = (h % 2) * 64
        kf_tgt = [0]
        with nc.Block() as blk:
            @blk.sync
            def _(sync, h=h):
                sync.wait_ge(s_v.h, s_v.n)
                sync.wait_ge(s_s.h, s_s.n)
                s_rep[0].inc(sync.dma_start(
                    out=t["Kfull"][0:64, :],
                    in_=qkT.ap()[ktile_f, krow:krow + 64, :]), 16)
                s_rep[0].inc(sync.dma_start(
                    out=t["Vfull"][:],
                    in_=Vtok.ap()[:, h * 64:(h + 1) * 64]
                    .rearrange("(a p) v -> p a v", p=128)), 16)

            @blk.vector
            def _(vector, h=h):
                vector.wait_ge(s_rep[0].h, s_rep[0].n)
                vector.wait_ge(s_g.h, g_tgt)
                vector.tensor_scalar(out=t["alibiF"][:], in0=t["kposR"][:],
                                     scalar1=t["tslope"][:, h:h + 1], scalar2=None,
                                     op0=ALU.mult)
                s_v.inc(vector.drain(), 1)
        kf_tgt[0] = s_v.n

        for q in range(32):
            u = h * 32 + q
            par = u % 2
            q0 = q * 128
            kmax = q0 + 128
            nkb = (kmax + 511) // 512
            with nc.Block() as blk:
                b_pe = s_pe.n
                b_s = s_s.n
                b_v = s_v.n
                ntot = sum((min(512, kmax - kb * 512) + 127) // 128 for kb in range(nkb))
                qp_tgt = [0]

                @blk.sync
                def _(sync, h=h, q=q, u=u, par=par):
                    if u >= 2:
                        sync.wait_ge(s_pe.h, u_scores_done[u - 2])
                    s_qk[par].inc(sync.dma_start(
                        out=t["Qp"][0:64, par, :],
                        in_=qkT.ap()[h // 2, (h % 2) * 64:(h % 2) * 64 + 64,
                                     q * 128:(q + 1) * 128]), 16)
                    qp_tgt[0] = s_qk[par].n

                @blk.tensor
                def _(tensor, h=h, q=q, u=u, par=par, nkb=nkb, kmax=kmax,
                      b_pe=b_pe, b_v=b_v, b_s=b_s, kf=kf_tgt[0]):
                    tensor.wait_ge(s_v.h, kf)
                    tensor.wait_ge(s_qk[par].h, qp_tgt[0])
                    if u >= 1:
                        tensor.wait_ge(s_s.h, u_copies_done[u - 1])
                    for kb in range(nkb):
                        n = min(512, kmax - kb * 512)
                        if kb >= 2:
                            tensor.wait_ge(s_s.h, b_s + kb - 1)
                        ins_ = tensor.matmul(pb[kb % 2][0:128, 0:n],
                                             t["Qp"][0:64, par, :],
                                             t["Kfull"][0:64, kb * 512:kb * 512 + n],
                                             start=True, stop=True)
                        s_pe.inc(ins_, 1)
                    u_scores_done.append(s_pe.n)
                    ntot = sum((min(512, kmax - kb * 512) + 127) // 128 for kb in range(nkb))
                    tiles = []
                    for kb in range(nkb):
                        n = min(512, kmax - kb * 512)
                        for j in range((n + 127) // 128):
                            tiles.append((kb, j, min(128, n - j * 128)))

                    def emit_T(jj):
                        kb, j, nn = tiles[jj]
                        if jj == 0:
                            tensor.wait_ge(s_v.h, b_v + nkb + 2)
                        if jj >= 2:
                            tensor.wait_ge(s_v.h, b_v + nkb + jj + 1)
                        ins_ = tensor.transpose(
                            psT[0:nn, (jj % 2) * 1024:(jj % 2) * 1024 + 128],
                            t["probs"][:, par, kb * 512 + j * 128:kb * 512 + j * 128 + nn],
                            ident[:])
                        s_pe.inc(ins_, 1)

                    def emit_PV(jj):
                        kb, j, nn = tiles[jj]
                        tensor.wait_ge(s_v.h, b_v + nkb + jj + 3)
                        if jj == 0 and u >= 2:
                            tensor.wait_ge(s_s.h, u_atile_done[u - 2])
                        ins_ = tensor.matmul(pb[2 + par][0:64, 0:128],
                                             t["Vfull"][0:nn, kb * 4 + j, :],
                                             t["probsT_sb"][0:nn, jj % 2, :],
                                             start=(jj == 0), stop=(jj == ntot - 1))
                        s_pe.inc(ins_, 1)

                    emit_T(0)
                    u_T_done.append(s_pe.n)
                    for jj in range(ntot - 1):
                        emit_T(jj + 1)
                        u_T_done[-1] = s_pe.n
                        emit_PV(jj)
                    emit_PV(ntot - 1)

                @blk.vector
                def _(vector, h=h, q=q, u=u, par=par, nkb=nkb, kmax=kmax,
                      b_pe=b_pe, b_s=b_s):
                    mvar = q % 4
                    for kb in range(nkb):
                        n = min(512, kmax - kb * 512)
                        vector.wait_ge(s_pe.h, b_pe + kb + 1)
                        vector.tensor_add(pb[kb % 2][0:128, 0:n],
                                          pb[kb % 2][0:128, 0:n],
                                          t["alibiF"][:, kb * 512:kb * 512 + n])
                        vector.drain()
                        if kb == nkb - 1:
                            vector.tensor_add(pb[kb % 2][0:128, 0:n],
                                              pb[kb % 2][0:128, 0:n],
                                              t["masks"][:, mvar, 0:n])
                            vector.drain()
                        vector.reduce_max(t["mparts"][:, kb:kb + 1],
                                          pb[kb % 2][0:128, 0:n], axis=AX.X)
                        s_v.inc(vector.drain(), 1)
                    vector.reduce_max(t["mrow"][:], t["mparts"][:, 0:nkb], axis=AX.X)
                    vector.drain()
                    vector.tensor_scalar(out=t["btile"][:], in0=t["mrow"][:],
                                         scalar1=t["qscale"][:, 0:1], scalar2=-1.0,
                                         op0=ALU.mult, op1=ALU.mult)
                    s_v.inc(vector.drain(), 1)
                    vector.wait_ge(s_s.h, b_s + 2 * nkb)
                    vector.reduce_sum(t["lrow"][:], t["lparts"][:, 0:nkb], axis=AX.X)
                    vector.drain()
                    vector.reciprocal(t["lrec"][:], t["lrow"][:])
                    vector.drain()
                    vector.tensor_scalar(out=t["probs"][:, par, 0:kmax],
                                         in0=t["probs"][:, par, 0:kmax],
                                         scalar1=t["lrec"][:, 0:1], scalar2=None,
                                         op0=ALU.mult)
                    s_v.inc(vector.drain(), 1)
                    jj = 0
                    for kb in range(nkb):
                        n = min(512, kmax - kb * 512)
                        for j in range((n + 127) // 128):
                            nn = min(128, n - j * 128)
                            vector.wait_ge(s_pe.h,
                                           b_pe + nkb + (1 if jj == 0 else 2 * jj))
                            vector.tensor_copy(t["probsT_sb"][0:nn, jj % 2, :],
                                               psT[0:nn, (jj % 2) * 1024:(jj % 2) * 1024 + 128])
                            s_v.inc(vector.drain(), 1)
                            jj += 1

                @blk.scalar
                def _(scalar, h=h, q=q, u=u, par=par, nkb=nkb, kmax=kmax,
                      b_pe=b_pe, b_v=b_v):
                    for kb in range(nkb):
                        n = min(512, kmax - kb * 512)
                        scalar.wait_ge(s_v.h, b_v + kb + 1)
                        scalar.copy(t["scores_sb"][:, kb * 512:kb * 512 + n],
                                    pb[kb % 2][0:128, 0:n])
                        s_s.inc(scalar.drain(), 1)
                    u_copies_done.append(s_s.n)
                    scalar.wait_ge(s_v.h, b_v + nkb + 1)
                    if u >= 2:
                        scalar.wait_ge(s_pe.h, u_T_done[u - 2])
                    for kb in range(nkb):
                        n = min(512, kmax - kb * 512)
                        scalar.activation(t["probs"][:, par, kb * 512:kb * 512 + n],
                                          t["scores_sb"][:, kb * 512:kb * 512 + n], AF.Exp,
                                          scale=t["qscale"][:, 0:1],
                                          bias=t["btile"][:, 0:1],
                                          accum_out=t["lparts"][:, kb:kb + 1])
                        s_s.inc(scalar.drain(), 1)
                    ntot = sum((min(512, kmax - kb * 512) + 127) // 128 for kb in range(nkb))
                    scalar.wait_ge(s_pe.h, b_pe + nkb + 2 * ntot)
                    scalar.wait_ge(s_gd.h, s_gd.n)
                    scalar.copy(t["atile"][0:64, par, :], pb[2 + par][0:64, 0:128])
                    s_s.inc(scalar.drain(), 1)
                    u_atile_done.append(s_s.n)

                @blk.gpsimd
                def _(gpsimd, h=h, q=q, u=u, par=par, nkb=nkb, b_s=b_s):
                    gpsimd.wait_ge(s_s.h, b_s + 2 * nkb + 1)
                    s_gd.inc(gpsimd.dma_start(
                        out=attnT.ap()[h // 2, (h % 2) * 64:(h % 2) * 64 + 64,
                                       q * 128:(q + 1) * 128],
                        in_=t["atile"][0:64, par, :]), 16)

    for c in range(T // 512):
        t0 = c * 512
        with nc.Block() as blk:
            @blk.sync
            def _(sync, c=c):
                sync.wait_ge(s_gd.h, s_gd.n)
                sync.wait_ge(s_v.h, s_v.n)
                s_xln.inc(sync.dma_start(
                    out=t["xlnTc"][:], in_=attnT.ap()[:, :, t0:t0 + 512]
                    .rearrange("a p t -> p a t")), 16)
                s_x0.inc(sync.dma_start(out=t["x2c"][:], in_=x_in[t0:t0 + 512, :]
                                        .rearrange("(tt p) d -> p tt d", p=128)), 16)
                wp_done[0] = 0
                pe_b = s_pe.n
                for dch in range(2):
                    if dch == 1:
                        sync.wait_ge(s_pe.h, pe_b + 4)
                    s_wp.inc(sync.dma_start(
                        out=t["pnv"][:], in_=WprojH.ap()[:, :, dch * 512:(dch + 1) * 512]
                        .rearrange("a p m -> p a m")), 16)
                    wp_done[dch] = s_wp.n

            @blk.tensor
            def _(tensor, c=c):
                tensor.wait_ge(s_xln.h, s_xln.n)
                v_b = s_v.n
                for dch in range(2):
                    tensor.wait_ge(s_wp.h, wp_done[dch])
                    if dch == 1:
                        tensor.wait_ge(s_v.h, v_b + 1)
                    for tsub in range(4):
                        for kd in range(GD):
                            ins_ = tensor.matmul(pb[tsub][0:128, 0:512],
                                                 t["xlnTc"][:, kd, tsub * 128:(tsub + 1) * 128],
                                                 t["pnv"][:, kd, :],
                                                 start=(kd == 0), stop=False)
                        ins_ = tensor.matmul(pb[tsub][0:128, 0:512],
                                             t["onesr"][0:1, tsub * 128:(tsub + 1) * 128],
                                             t["bprowb"][0:1, dch * 512:(dch + 1) * 512],
                                             start=False, stop=True)
                        s_pe.inc(ins_, 1)

            @blk.vector
            def _(vector, c=c):
                pe_base = s_pe.n - 8
                vector.wait_ge(s_x0.h, s_x0.n)
                for dch in range(2):
                    vector.wait_ge(s_pe.h, pe_base + 4 * (dch + 1))
                    for tsub in range(4):
                        vector.tensor_add(t["x2c"][:, tsub, dch * 512:(dch + 1) * 512],
                                          t["x2c"][:, tsub, dch * 512:(dch + 1) * 512],
                                          pb[tsub][0:128, 0:512])
                    s_v.inc(vector.drain(), 1)

            @blk.gpsimd
            def _(gpsimd, c=c):
                gpsimd.wait_ge(s_v.h, s_v.n)
                s_gd.inc(gpsimd.dma_start(
                    out=x_out[t0:t0 + 512, :].rearrange("(tt p) d -> p tt d", p=128),
                    in_=t["x2c"][:]), 16)


_NC_CACHE = {}


def _build_full():
    if "nc" not in _NC_CACHE:
        _NC_CACHE["nc"] = build_program(["mamba", "ffn1", "attn", "ffn2"],
                                        dbg=False, nch=NCHM)
    return _NC_CACHE["nc"]


def kernel(**inputs):
    nc = _build_full()
    x = np.ascontiguousarray(np.asarray(inputs["L0_attn_a_x"], np.float32))
    io = np.zeros((128, 84), np.float32)
    io[:, 0:64] = np.arange(64)[None, :]
    for ts in range(4):
        io[:, 64 + ts] = ts * 128 + np.arange(128)
    io[:, 68:84] = np.asarray(ALIBI_SLOPES, np.float32)[None, :]
    common = {"ident": np.eye(128, dtype=bfnp),
              "iotain": io,
              "iotaT": np.arange(T, dtype=np.float32)}
    for nm in MAMBA_W_NAMES:
        full = "L0_attn_a_" + nm
        common[full] = np.ascontiguousarray(np.asarray(inputs[full], np.float32))
    for nm in FFN_W_NAMES:
        full = "L0_ffn_" + nm
        common[full] = np.ascontiguousarray(np.asarray(inputs[full], np.float32))
    for nm in ["W_attn", "b_attn", "W_proj", "b_proj", "temperature"]:
        full = "L0_attn_b_" + nm
        common[full] = np.ascontiguousarray(np.asarray(inputs[full], np.float32))
    in_maps = []
    for i in range(8):
        m = dict(common)
        m["x"] = x[i]
        in_maps.append(m)
    res = run_bass_kernel_spmd(nc, in_maps, core_ids=list(range(8)))
    return np.stack([np.asarray(res.results[i]["x4"], np.float32)
                     for i in range(8)], 0)
